# version 11
# baseline (speedup 1.0000x reference)
import numpy as np
import ml_dtypes
from contextlib import ExitStack

import concourse.bass as bass
import concourse.mybir as mybir
from concourse.bass_core_rust import IndirectOffsetOnAxis
from concourse.bass_utils import run_bass_kernel_spmd

F32 = mybir.dt.float32
BF16 = mybir.dt.bfloat16
I32 = mybir.dt.int32
AF = mybir.ActivationFunctionType
ALU = mybir.AluOpType
AX = mybir.AxisListType

B, L, D = 4, 4096, 1024
DK = 256
CH = 128
NCHUNK = L // CH
PAD = 384
ROW0 = PAD
WIN = 2 * PAD + 2048
LPAD = L + 2 * PAD
TT = 512
ST4 = 256
NST4 = WIN // ST4
OUT0 = PAD
NOUT = 2048

DTB = {F32: 4, I32: 4, BF16: 2}


class KB:
    def __init__(self, stage_limit=99, dbg=()):
        self.stage_limit = stage_limit
        self.dbg = set(dbg)
        self.nc = bass.Bass("TRN2", target_bir_lowering=False)
        self.sb = {}
        self.dram = {}
        self.eng = None
        self.sbuf_off = 0
        self.regs = {}
        self.rsems = {}
        self.rc = {}
        self.ocnt = 0
        self.cur_eng = None
        self.pend = 0
        self.last_instr = None
        self._build_tensors()

    def sbuf(self, name, shape, dtype, at=None):
        if name in self.sb:
            return self.sb[name]
        base = (self.nc.sbuf_base + 511) // 512 * 512
        nbytes = int(np.prod(shape[1:])) * DTB[dtype]
        off = self.sbuf_off if at is None else at
        off = (off + 511) // 512 * 512
        if off < base:
            off = base
        h = self.nc.alloc_sbuf_tensor_at(name, list(shape), dtype, offset=off)
        if at is None:
            self.sbuf_off = off + nbytes
        assert off + nbytes <= 229300, f"SBUF overflow {name}: {off + nbytes}"
        self.sb[name] = h
        return h

    def din(self, name, shape, dtype):
        self.dram[name] = self.nc.dram_tensor(name, list(shape), dtype, kind="ExternalInput")

    def dout(self, name, shape, dtype):
        self.dram[name] = self.nc.dram_tensor(name, list(shape), dtype, kind="ExternalOutput")

    def dscratch(self, name, shape, dtype):
        kind = "ExternalOutput" if name in self.dbg else "Internal"
        self.dram[name] = self.nc.dram_tensor(name, list(shape), dtype, kind=kind)

    def d(self, name):
        return self.dram[name].ap()

    def on(self, e):
        return self.eng == e

    def _flush(self):
        if self.pend and self.last_instr is not None:
            self.last_instr.then_inc(self.osem, self.pend)
        if self.pend:
            self.ocnt += self.pend
        self.pend = 0
        self.last_instr = None

    def _pre(self, eng, is_dma=False):
        if eng != self.cur_eng:
            self._flush()
            self.cur_eng = eng
            if self.on(eng):
                getattr(self.nc, eng).wait_ge(self.osem, self.ocnt)
        elif is_dma and self.pend:
            pass

    def op(self, eng, fn):
        if eng != self.cur_eng:
            self.cur_eng = eng
            if self.on(eng):
                getattr(self.nc, eng).wait_ge(self.osem, self.ocnt)
        elif eng == "vector" and self.on(eng):
            getattr(self.nc, eng).wait_ge(self.osem, self.ocnt)
        if self.on(eng):
            fn().then_inc(self.osem, 1)
        self.ocnt += 1

    def dmaop(self, q, fn):
        self.cur_eng = q
        if self.on(q):
            getattr(self.nc, q).wait_ge(self.osem, self.ocnt)
            fn().then_inc(self.osem, 16)
        self.ocnt += 16

    def dma(self, q, out, in_, **kw):
        self.dmaop(q, lambda: getattr(self.nc, q).dma_start(out=out, in_=in_, **kw))

    def dmaT(self, q, out, in_):
        self.dmaop(q, lambda: getattr(self.nc, q).dma_start(out=out, in_=in_, transpose=True))

    def idma(self, out, in_, in_off, **kw):
        self.dmaop("gpsimd", lambda: self.nc.gpsimd.indirect_dma_start(
            out, None, in_, in_off, **kw))

    def mm(self, out, lhsT, rhs, start=True, stop=True, tp=None):
        self.op("tensor", lambda: self.nc.tensor.matmul(
            out, lhsT, rhs, start=start, stop=stop, tile_position=tp))

    def trans(self, out, in_, ident):
        self.op("tensor", lambda: self.nc.tensor.transpose(out, in_, ident))

    def act(self, out, in_, func, bias=0.0, scale=1.0):
        self.op("scalar", lambda: self.nc.scalar.activation(
            out=out, in_=in_, func=func, bias=bias, scale=scale))

    def v(self, meth, *a, eng="vector", **kw):
        self.op(eng, lambda: getattr(getattr(self.nc, eng), meth)(*a, **kw))

    def memset(self, ap, val, eng="vector"):
        self.op(eng, lambda: getattr(self.nc, eng).memset(ap, val))

    def dyndma(self, q, tensor_name, const_off, pattern, out, slot=0, load=False):
        def fn():
            e = getattr(self.nc, q)
            rt = self._dynoff(q, const_off, slot)
            ap = bass.AP(self.dram[tensor_name], rt, pattern)
            try:
                if load:
                    return e.dma_start(out=out, in_=ap)
                return e.dma_start(out=ap, in_=out)
            except ValueError as ex:
                import traceback as _tb
                _tb.print_exc()
                raise RuntimeError(f"dyndma {tensor_name} slot={slot} const={const_off} pat={pattern}: {ex}")
        self.dmaop(q, fn)

    def rsem(self, name):
        if name not in self.rsems:
            self.rsems[name] = self.nc.alloc_semaphore(name)
            self.rc[name] = 0
        return self.rsems[name]

    def rop(self, eng, fn, incs=(), waits=()):
        if self.on(eng):
            e = getattr(self.nc, eng)
            for nm, v in waits:
                e.wait_ge(self.osem if nm == "O" else self.rsems[nm], v)
            i = fn()
            for nm, n in incs:
                i.then_inc(self.rsems[nm], n)
        for nm, n in incs:
            self.rc[nm] += n

    def rwait(self, eng, waits):
        if self.on(eng):
            e = getattr(self.nc, eng)
            for nm, v in waits:
                e.wait_ge(self.osem if nm == "O" else self.rsems[nm], v)

    def rjoin(self, waits):
        for e in ("sync", "vector", "tensor", "scalar", "gpsimd"):
            self.rwait(e, waits)
        self.cur_eng = None

    def mmpipe(self, items, nbank=2, bank0=0, width=512):
        self.rsem("pM"); self.rsem("pE"); self.rsem("pS")
        m0, e0, s0 = self.rc["pM"], self.rc["pE"], self.rc["pS"]
        oc_e = self.ocnt
        evac_of_bank = {}
        has_store = any(it.get("store") for it in items)
        for i, it in enumerate(items):
            b = i % nbank
            bank = self.ps[bank0 + b][:, 0:width]
            w_ = [("O", oc_e)] if i < nbank else [("pE", evac_of_bank[b])]
            fns = it["mms"](bank)
            for k, f in enumerate(fns):
                self.rop("tensor", f,
                         incs=[("pM", 1)] if k == len(fns) - 1 else [],
                         waits=w_ if k == 0 else [])
            ew = [("pM", m0 + i + 1)]
            if i == 0:
                ew.append(("O", oc_e))
            if it.get("store") and i >= 2 * nbank:
                ew.append(("pS", self.rc["pS"] - 2 * 16))
            self.rop("scalar", it["evac"](bank), incs=[("pE", 1)], waits=ew)
            evac_of_bank[b] = self.rc["pE"]
            if it.get("store"):
                sw = [("pE", self.rc["pE"])]
                if i == 0:
                    sw.append(("O", oc_e))
                self.rop("gpsimd", it["store"], incs=[("pS", 16)], waits=sw)
        self.rjoin([("pM", self.rc["pM"]), ("pE", self.rc["pE"])] +
                   ([("pS", self.rc["pS"])] if has_store else []))

    def windma(self, tensor_name, n, out, idxcol):
        def fn():
            ap = bass.AP(self.dram[tensor_name], 0, [[1, 128], [1, n]])
            return self.nc.gpsimd.indirect_dma_start(
                out, None, ap, IndirectOffsetOnAxis(idxcol, 1))
        self.dmaop("gpsimd", fn)

    def setup_dynregs(self, q, slot=0):
        pass

    def _dynoff(self, q, const_off, slot):
        e = getattr(self.nc, q)
        if (q, "base") not in self.regs:
            self.regs[(q, "base")] = e.alloc_register(f"rb_{q}")
            self.regs[(q, "tmp")] = e.alloc_register(f"rt_{q}")
        rb, rt = self.regs[(q, "base")], self.regs[(q, "tmp")]
        e.wait_ge(self.osem, self.ocnt)
        e.reg_load(rb, self.sb["c_offs"][0:1, slot:slot + 1])
        e.reg_add(rt, rb, const_off)
        return rt

    def _build_tensors(self):
        self.din("x_tm", [L, D], F32)
        self.din("x_fm", [D, LPAD], F32)
        self.din("w_qkvb", [D, 3 * D + 4], F32)
        self.din("convw", [D, 12], F32)
        self.din("firw", [D, 42], F32)
        self.din("mlp_w1p", [1120, D], F32)
        self.din("mlp_b1c", [128, 8], F32)
        self.din("mlp_w2", [D, 24], F32)
        self.din("mlp_b2", [24, 1], F32)
        self.din("glt", [4, 1], F32)
        self.din("onormc", [128, 2], F32)
        self.din("o_wT", [D, D], F32)
        self.din("l0f_wT", [D, D], F32)
        self.din("l0f_gw", [D, 8], F32)
        self.din("l0f_U", [D, 128], F32)
        self.din("l0f_V", [128, D], F32)
        self.din("l1a_wavew", [D, 16], F32)
        self.din("l1a_wavebf", [8, 1], F32)
        self.din("l1a_wavebp", [8, 1], F32)
        self.din("l1a_kw", [D, 264], F32)
        self.din("l1a_kb", [128, 3], F32)
        self.din("l1a_outwT", [D, D], F32)
        self.din("l1f_w1", [8 * D, 4096], F32)
        self.din("l1f_w2", [8 * 4096, D], F32)
        self.din("l1f_wg", [D, 8], F32)
        self.din("l1f_bg", [8, 1], F32)
        self.din("ident", [128, 128], F32)
        self.din("maskAl", [128, 128], F32)
        self.din("maskAu", [128, 128], F32)
        self.din("maskIncT", [128, 128], F32)
        self.din("ind49", [128, 49], F32)
        self.din("sel4", [4, 512], F32)
        self.din("sel4f", [4, 512], F32)
        self.din("hsel", [24, 4], F32)
        self.din("hselT", [4, 24], F32)
        self.din("rsel24", [24, 24 * 128], F32)
        self.din("e16", [8, 128], F32)
        self.din("iotaG", [1, WIN], F32)
        self.din("sidx8", [1, 33], F32)
        self.din("w0col", [33, 1], F32)
        self.din("offs", [1, 16], I32)
        self.din("idxA", [128, 8], I32)
        self.din("idxB", [128, 1], I32)

        self.dscratch("xn_fm", [D, LPAD], BF16)
        self.dscratch("xn_tmp", [L, D], BF16)
        self.dscratch("q_fm", [D, L], BF16)
        self.dscratch("khat_fm", [D, L], BF16)
        self.dscratch("kb_fm", [D, L], BF16)
        self.dscratch("vb_fm", [D, L], BF16)
        self.dscratch("v_fm", [D, LPAD], F32)
        self.dscratch("delta_tm", [LPAD, D], BF16)
        self.dscratch("fir_dram", [4 * D, WIN], BF16)
        self.dscratch("x1_fm", [D, WIN], F32)
        self.dscratch("xn1_fm", [D, WIN], BF16)
        self.dscratch("xn1_tm", [WIN + 128, D], BF16)
        self.dscratch("s1_fm", [D, NOUT], F32)
        self.dscratch("xt_dram", [8 * D, NOUT], BF16)
        self.dscratch("ht_dram", [8 * 4096, NOUT], BF16)
        self.dscratch("invt_d", [4, 1], F32)
        self.dscratch("dbg3", [128, 4096], F32)
        self.dscratch("dbg4", [128, 4096], F32)
        self.dscratch("dbg5", [128, 16], I32)
        self.dout("x2_fm", [D, NOUT], F32)

        self.sbuf("c_nop", [1, 2], F32)
        self.sbuf("c_ident", [128, 128], BF16)
        self.sbuf("c_identf", [128, 128], F32)
        self.sbuf("c_maskAl", [128, 128], F32)
        self.sbuf("c_maskAu", [128, 128], F32)
        self.sbuf("c_maskIncT", [128, 128], F32)
        self.sbuf("c_ind49", [128, 49], BF16)
        self.sbuf("c_ind49f", [128, 49], F32)
        self.sbuf("c_sel4", [4, 512], BF16)
        self.sbuf("c_sel4f", [4, 512], F32)
        self.sbuf("c_hsel", [24, 4], F32)
        self.sbuf("c_hselT", [4, 24], F32)
        self.sbuf("c_e16", [8, 128], F32)
        self.sbuf("c_iotaG", [1, WIN], F32)
        self.sbuf("c_sidx8", [1, 33], F32)
        self.sbuf("c_w0col", [33, 1], F32)
        self.sbuf("c_offs", [1, 16], I32)
        self.sbuf("c_idxA", [128, 8], I32)
        self.sbuf("c_idxB", [128, 1], I32)
        self.sbuf("c_idxS", [128, 8], I32)
        self.sbuf("c_idxD", [128, 1], I32)
        self.sbuf("c_ones", [1, 512], BF16)
        self.sbuf("c_onesf", [1, 512], F32)
        self.sbuf("c_onescol", [128, 1], F32)
        self.sbuf("c_zero", [128, 512], F32)
        self.sbuf("rstdq_tm", [128, NCHUNK * 4], F32)
        self.sbuf("c_b1", [128, 8], F32)
        self.sbuf("c_b2", [24, 1], F32)
        self.sbuf("c_invt", [24, 1], F32)
        self.sbuf("c_b2i", [24, 1], F32)
        self.sbuf("c_onorm", [128, 2], F32)
        self.sbuf("c_wavebf", [8, 1], F32)
        self.sbuf("c_wavebp", [8, 1], F32)
        self.sbuf("c_kb1", [128, 3], F32)
        self.sbuf("c_bg", [8, 1], F32)
        self.sbuf("c_glt", [4, 1], F32)
        self.sbuf("c_sp", [4, 4], F32)
        self.static_top = ((self.sbuf_off + 511) // 512 * 512)

    def ar(self, name, shape, dtype, off):
        return self.sbuf(name, shape, dtype, at=self.static_top + off)

    def build(self):
        nc = self.nc
        with ExitStack() as ctx:
            E = ctx.enter_context
            self.ps = [E(nc.psum_tensor(f"ps{i}", [128, 512], F32)) for i in range(8)]
            self.osem = E(nc.semaphore("O"))
            blk = E(nc.Block())

            def walk(name):
                def f(handle):
                    self.eng = name
                    self.ocnt = 0
                    self.cur_eng = None
                    self.pend = 0
                    self.last_instr = None
                    for k in self.rc:
                        self.rc[k] = 0
                    self.emit_all()
                return f
            blk.sync(walk("sync"))
            blk.vector(walk("vector"))
            blk.tensor(walk("tensor"))
            blk.scalar(walk("scalar"))
            blk.gpsimd(walk("gpsimd"))
        return nc

    def emit_all(self):
        self.s0_consts()
        if self.stage_limit >= 1:
            self.s1_ln()
        if self.stage_limit >= 2:
            self.s2_proj()
        if self.stage_limit >= 3:
            self.s3_scan()
        if self.stage_limit >= 4:
            self.s4_fir()
        if self.stage_limit >= 5:
            self.s4_window()
        if self.stage_limit >= 6:
            self.s5_l1attn()
        if self.stage_limit >= 7:
            self.s6_moe()
        self._flush()
        for e in ("sync", "vector", "tensor", "scalar", "gpsimd"):
            if self.on(e):
                getattr(self.nc, e).wait_ge(self.osem, self.ocnt)

    def s0_consts(self):
        s = self.sb
        for dst, src, castb in [
            ("c_ident", "ident", True), ("c_identf", "ident", False),
            ("c_maskAl", "maskAl", False), ("c_maskAu", "maskAu", False),
            ("c_maskIncT", "maskIncT", False),
            ("c_ind49", "ind49", True), ("c_ind49f", "ind49", False),
            ("c_sel4", "sel4", True), ("c_sel4f", "sel4f", False),
            ("c_hsel", "hsel", False), ("c_hselT", "hselT", False),
            ("c_e16", "e16", False),
            ("c_iotaG", "iotaG", False), ("c_sidx8", "sidx8", False),
            ("c_w0col", "w0col", False), ("c_offs", "offs", False),
            ("c_idxA", "idxA", False), ("c_idxB", "idxB", False),
            ("c_b1", "mlp_b1c", False), ("c_b2", "mlp_b2", False),
            ("c_onorm", "onormc", False),
            ("c_wavebf", "l1a_wavebf", False), ("c_wavebp", "l1a_wavebp", False),
            ("c_kb1", "l1a_kb", False), ("c_bg", "l1f_bg", False),
            ("c_glt", "glt", False),
        ]:
            self.dma("gpsimd" if castb else "sync", s[dst][:, :], self.d(src)[:, :])
        self.memset(s["c_ones"][0:1, :], 1.0)
        self.memset(s["c_onesf"][0:1, :], 1.0)
        self.memset(s["c_onescol"][:, :], 1.0)
        self.memset(s["c_zero"][:, :], 0.0)
        zf = s["c_zero"][:, :]
        zb = s["c_zero"][:, :].bitcast(BF16)
        for c in range(8):
            r0, r1 = c * 128, (c + 1) * 128
            self.dma("gpsimd", self.d("v_fm")[r0:r1, 0:PAD], zf[:, 0:PAD])
            self.dma("gpsimd", self.d("v_fm")[r0:r1, LPAD - PAD:LPAD], zf[:, 0:PAD])
            self.dma("gpsimd", self.d("xn_fm")[r0:r1, 0:PAD], zb[:, 0:PAD])
            self.dma("gpsimd", self.d("xn_fm")[r0:r1, LPAD - PAD:LPAD], zb[:, 0:PAD])
        for r in range(0, PAD, 128):
            self.dma("gpsimd", self.d("delta_tm")[r:r + 128, :], zb[:, 0:D])
            self.dma("gpsimd", self.d("delta_tm")[LPAD - PAD + r:LPAD - PAD + r + 128, :], zb[:, 0:D])
        self.dma("gpsimd", self.d("xn1_tm")[WIN:WIN + 128, :], zb[:, 0:D])
        tmp, glt = s["c_sp"], s["c_glt"]
        self.v("tensor_scalar", tmp[:, 0:1], glt[:, :], -1.0, None, ALU.mult)
        self.v("tensor_max", tmp[:, 1:2], tmp[:, 0:1], glt[:, :])
        self.v("tensor_scalar_max", tmp[:, 2:3], glt[:, :], 0.0)
        self.act(tmp[:, 3:4], tmp[:, 1:2], AF.Exp, scale=-1.0)
        self.act(tmp[:, 0:1], tmp[:, 3:4], AF.Ln, bias=1.0)
        self.v("tensor_add", tmp[:, 1:2], tmp[:, 0:1], tmp[:, 2:3])
        self.v("tensor_scalar_add", tmp[:, 2:3], tmp[:, 1:2], 0.5)
        self.v("reciprocal", tmp[:, 3:4], tmp[:, 2:3])
        self.dma("gpsimd", self.d("invt_d")[:, :], tmp[:, 3:4])
        for s6 in range(6):
            self.dma("gpsimd", s["c_invt"][s6::6, :], self.d("invt_d")[:, :])
        self.v("tensor_mul", s["c_b2i"][:, :], s["c_invt"][:, :], s["c_b2"][:, :])

    def s1_ln(self):
        s = self.sb
        xt = self.ar("s1_x", [128, D], F32, 0)
        xo = self.ar("s1_o", [128, D], BF16, 4096)
        sq = self.ar("s1_sq", [128, D], F32, 6144)
        mv = self.ar("s1_mv", [128, 8], F32, 10752)
        xnT = self.ar("s1_xnT", [128, L], BF16, 11264)
        for t in range(L // 128):
            self.dma("sync", xt[:, :], self.d("x_tm")[t * 128:(t + 1) * 128, :])
            self.act(sq[:, :], xt[:, :], AF.Square)
            self.v("reduce_sum", mv[:, 0:1], xt[:, :], AX.X)
            self.v("reduce_sum", mv[:, 1:2], sq[:, :], AX.X)
            self.v("tensor_scalar_mul", mv[:, 2:3], mv[:, 0:1], 1.0 / D)
            self.v("tensor_mul", mv[:, 3:4], mv[:, 2:3], mv[:, 2:3])
            self.v("tensor_scalar", mv[:, 4:5], mv[:, 1:2], 1.0 / D, None, ALU.mult)
            self.v("tensor_sub", mv[:, 5:6], mv[:, 4:5], mv[:, 3:4])
            self.v("tensor_scalar_add", mv[:, 5:6], mv[:, 5:6], 1e-5)
            self.act(mv[:, 6:7], mv[:, 5:6], AF.Sqrt)
            self.v("reciprocal", mv[:, 7:8], mv[:, 6:7])
            self.v("scalar_tensor_tensor", xo[:, :], xt[:, :], mv[:, 2:3],
                   mv[:, 7:8].broadcast_to([128, D]), ALU.subtract, ALU.mult)
            self.dma("gpsimd", self.d("xn_tmp")[t * 128:(t + 1) * 128, :], xo[:, :])
        for c in range(8):
            self.dmaT("sync", xnT[:, :], self.d("xn_tmp")[:, c * 128:(c + 1) * 128])
            self.dma("gpsimd", self.d("xn_fm")[c * 128:(c + 1) * 128, ROW0:ROW0 + L],
                     xnT[:, :])

    def s2_proj(self):
        s = self.sb
        o = 0
        def nxt(name, shape, dtype):
            nonlocal o
            h = self.ar(name, shape, dtype, o)
            o += (int(np.prod(shape[1:])) * DTB[dtype] + 511) // 512 * 512
            return h
        xn_sb = nxt("s2_xn", [128, 8 * TT], BF16)
        proj = nxt("s2_proj", [128, 24 * (TT + 4)], BF16)
        qraw = nxt("s2_qraw", [128, 16 * TT], F32)
        vraw = nxt("s2_vraw", [128, 8 * TT], F32)
        xsq = nxt("s2_xsq", [128, TT], BF16)
        qt = [nxt(f"s2_qt{i}", [4, TT], F32) for i in range(6)]
        qb = nxt("s2_qb", [4, 3 * TT], BF16)
        obf = nxt("s2_obf", [128, 2 * TT], BF16)
        wq = nxt("s2_w", [128, 8 * 3076], BF16)
        dg = nxt("s2_diag", [128, 96 * 128], BF16)
        cw = nxt("s2_cw", [128, 96], F32)

        for kc in range(8):
            self.dma("gpsimd", wq[:, kc * 3076:(kc + 1) * 3076],
                     self.d("w_qkvb")[kc * 128:(kc + 1) * 128, :])
        self.dma("sync", cw[:, :].rearrange("p (c t) -> p c t", c=8),
                 self.d("convw").rearrange("(c p) t -> p c t", p=128))
        for m in range(24):
            p3, c = divmod(m, 8)
            for j in range(4):
                self.v("tensor_scalar", dg[:, (m * 4 + j) * 128:(m * 4 + j + 1) * 128],
                       s["c_ident"][:, :], cw[:, c * 12 + p3 * 4 + j: c * 12 + p3 * 4 + j + 1],
                       None, ALU.mult)
        for tt in range(8):
            t0 = tt * TT
            for kc in range(8):
                self.dma("sync", xn_sb[:, kc * TT:(kc + 1) * TT],
                         self.d("xn_fm")[kc * 128:(kc + 1) * 128, ROW0 + t0:ROW0 + t0 + TT])
            for m in range(25):
                bank = self.ps[m % 2]
                rows = 128 if m < 24 else 4
                for kc in range(8):
                    self.mm(bank[0:rows, 0:TT],
                            wq[:, kc * 3076 + m * 128: kc * 3076 + m * 128 + rows],
                            xn_sb[:, kc * TT:(kc + 1) * TT],
                            start=(kc == 0), stop=(kc == 7))
                if m < 24:
                    pr = proj[:, m * 516:(m + 1) * 516]
                    if tt == 0:
                        self.memset(pr[:, 0:4], 0.0)
                    self.act(pr[:, 4:4 + TT], bank[0:128, 0:TT], AF.Copy)
                else:
                    self.act(qt[3][:, 0:TT], bank[0:4, 0:TT], AF.Sigmoid)
            for m in range(24):
                pr = proj[:, m * 516:(m + 1) * 516]
                bank = self.ps[2 + m % 2]
                for j in range(4):
                    self.mm(bank[:, 0:TT], dg[:, (m * 4 + j) * 128:(m * 4 + j + 1) * 128],
                            pr[:, 1 + j: 1 + j + TT], start=(j == 0), stop=(j == 3))
                dst = qraw[:, m * TT:(m + 1) * TT] if m < 16 else vraw[:, (m - 16) * TT:(m - 15) * TT]
                self.act(dst, bank[:, 0:TT], AF.Silu)
                self.v("tensor_copy", pr[:, 0:4], pr[:, TT:TT + 4])
            for qk in range(2):
                bank = self.ps[4]
                for c in range(8):
                    m = qk * 8 + c
                    self.act(xsq[:, 0:TT], qraw[:, m * TT:(m + 1) * TT], AF.Square)
                    h4 = c // 2
                    self.mm(bank[0:4, 0:TT], s["c_ind49"][:, 24 - h4:28 - h4], xsq[:, 0:TT],
                            start=(c == 0), stop=(c == 7))
                self.v("tensor_scalar_add", qt[4][:, 0:TT], bank[0:4, 0:TT], 1e-6)
                self.act(qt[5][:, 0:TT], qt[4][:, 0:TT], AF.Sqrt)
                self.v("reciprocal", qt[qk][:, 0:TT], qt[5][:, 0:TT])
            self.v("tensor_mul", qt[2][:, 0:TT], qt[1][:, 0:TT], qt[3][:, 0:TT])
            self.v("tensor_copy", qb[:, 0:TT], qt[1][:, 0:TT])
            self.v("tensor_copy", qb[:, TT:2 * TT], qt[2][:, 0:TT])
            self.v("tensor_copy", qb[:, 2 * TT:3 * TT], qt[3][:, 0:TT])
            for cb in range(TT // 128):
                bank = self.ps[5]
                self.trans(bank[:, 0:4], qt[0][:, cb * 128:(cb + 1) * 128], s["c_identf"][0:4, 0:4])
                chn = (t0 + cb * 128) // 128
                self.v("tensor_copy", s["rstdq_tm"][:, chn * 4:(chn + 1) * 4], bank[:, 0:4])
            for h4 in range(4):
                for iq in range(3):
                    bank = self.ps[6 + iq % 2]
                    self.mm(bank[:, 0:TT], s["c_sel4"][:, h4 * 128:(h4 + 1) * 128],
                            qb[:, iq * TT:(iq + 1) * TT])
                    for cc in range(2):
                        m = h4 * 2 + cc
                        src = qraw[:, (8 + m) * TT:(9 + m) * TT] if iq < 2 else vraw[:, m * TT:(m + 1) * TT]
                        ob = obf[:, cc * TT:(cc + 1) * TT]
                        self.v("tensor_tensor", ob, src, bank[:, 0:TT], ALU.mult)
                        nm = ("khat_fm", "kb_fm", "vb_fm")[iq]
                        self.dma("gpsimd", self.d(nm)[m * 128:(m + 1) * 128, t0:t0 + TT], ob)
            for m in range(8):
                ob = obf[:, (m % 2) * TT:(m % 2 + 1) * TT]
                self.act(ob, qraw[:, m * TT:(m + 1) * TT], AF.Copy)
                self.dma("gpsimd", self.d("q_fm")[m * 128:(m + 1) * 128, t0:t0 + TT], ob)
                self.dma("sync", self.d("v_fm")[m * 128:(m + 1) * 128, ROW0 + t0:ROW0 + t0 + TT],
                         vraw[:, m * TT:(m + 1) * TT])

    def s3_scan(self):
        s = self.sb
        G = 512
        o = 0
        def nxt(name, shape, dtype):
            nonlocal o
            h = self.ar(name, shape, dtype, o)
            o += (int(np.prod(shape[1:])) * DTB[dtype] + 511) // 512 * 512
            return h
        g_q = nxt("s3_q", [128, 2 * G], BF16)
        g_kh = nxt("s3_kh", [128, 2 * G], BF16)
        g_kb = nxt("s3_kb", [128, 2 * G], BF16)
        g_kht = nxt("s3_kht", [128, 4 * DK], BF16)
        g_kbt = nxt("s3_kbt", [128, 4 * DK], BF16)
        g_vbt = nxt("s3_vbt", [128, 4 * DK], BF16)
        S_f = nxt("s3_Sf", [128, 2 * DK], F32)
        S_b = nxt("s3_Sb", [128, 2 * DK], BF16)
        A_sb = nxt("s3_A", [128, 128], BF16)
        AT_sb = nxt("s3_AT", [128, 128], BF16)
        Tt = nxt("s3_T", [128, 128], BF16)
        TtT = nxt("s3_TT", [128, 128], BF16)
        Xp = nxt("s3_X", [128, 128], BF16)
        XpT = nxt("s3_XT", [128, 128], BF16)
        uh = nxt("s3_uh", [128, DK], BF16)
        atT = nxt("s3_atT", [128, 128], BF16)
        wng = nxt("s3_wng", [128, 2 * 128], BF16)
        o_sb = nxt("s3_o", [128, DK], BF16)
        dbgf = nxt("s3_dbgf", [128, 128], F32)

        for h in range(4):
            self.memset(S_f[:, :], 0.0)
            self.memset(S_b[:, :], 0.0)
            r0 = h * DK
            for gi in range(L // G):
                t0 = gi * G
                for dc in range(2):
                    rr = r0 + dc * 128
                    self.dma("sync", g_q[:, dc * G:(dc + 1) * G], self.d("q_fm")[rr:rr + 128, t0:t0 + G])
                    self.dma("sync", g_kh[:, dc * G:(dc + 1) * G], self.d("khat_fm")[rr:rr + 128, t0:t0 + G])
                    self.dma("sync", g_kb[:, dc * G:(dc + 1) * G], self.d("kb_fm")[rr:rr + 128, t0:t0 + G])
                for ci in range(4):
                    tc = t0 + ci * CH
                    self.dmaT("scalar", g_kht[:, ci * DK:(ci + 1) * DK],
                              self.d("khat_fm")[r0:r0 + DK, tc:tc + CH])
                    self.dmaT("scalar", g_kbt[:, ci * DK:(ci + 1) * DK],
                              self.d("kb_fm")[r0:r0 + DK, tc:tc + CH])
                    self.dmaT("scalar", g_vbt[:, ci * DK:(ci + 1) * DK],
                              self.d("vb_fm")[r0:r0 + DK, tc:tc + CH])
                for ci in range(4):
                    cg = gi * 4 + ci
                    qs = [g_q[:, dc * G + ci * CH: dc * G + (ci + 1) * CH] for dc in range(2)]
                    khs = [g_kh[:, dc * G + ci * CH: dc * G + (ci + 1) * CH] for dc in range(2)]
                    kbs = [g_kb[:, dc * G + ci * CH: dc * G + (ci + 1) * CH] for dc in range(2)]
                    kht = g_kht[:, ci * DK:(ci + 1) * DK]
                    kbt = g_kbt[:, ci * DK:(ci + 1) * DK]
                    vbt = g_vbt[:, ci * DK:(ci + 1) * DK]
                    ps = self.ps
                    for dc in range(2):
                        self.mm(ps[0][:, 0:128], kbs[dc], khs[dc], start=(dc == 0), stop=(dc == 1))
                    for dc in range(2):
                        self.mm(ps[1][:, 0:128], khs[dc], kbs[dc], start=(dc == 0), stop=(dc == 1))
                    self.v("tensor_tensor", A_sb[:, :], ps[0][:, 0:128], s["c_maskAl"][:, :], ALU.mult)
                    self.v("tensor_tensor", AT_sb[:, :], ps[1][:, 0:128], s["c_maskAu"][:, :], ALU.mult)
                    if h == 0 and cg == 0 and "dbg3" in self.dbg:
                        self.dma("gpsimd", self.d("dbg3")[:, 3328:3456], s["c_maskAu"][:, :])
                    self.v("tensor_add", Tt[:, :], A_sb[:, :], s["c_ident"][:, :])
                    self.v("tensor_add", TtT[:, :], AT_sb[:, :], s["c_ident"][:, :])
                    self.v("tensor_copy", Xp[:, :], A_sb[:, :])
                    self.v("tensor_copy", XpT[:, :], AT_sb[:, :])
                    for j in range(6):
                        self.mm(ps[0][:, 0:128], XpT[:, :], Xp[:, :])
                        self.mm(ps[1][:, 0:128], Xp[:, :], XpT[:, :])
                        self.act(Xp[:, :], ps[0][:, 0:128], AF.Copy)
                        self.act(XpT[:, :], ps[1][:, 0:128], AF.Copy)
                        if h == 0 and cg == 0 and j == 0 and "dbg3" in self.dbg:
                            self.dma("gpsimd", self.d("dbg3")[:, 3456:3584], s["c_maskAu"][:, :])
                        self.mm(ps[2][:, 0:128], s["c_ident"][:, :], Tt[:, :], start=True, stop=False)
                        self.mm(ps[2][:, 0:128], XpT[:, :], Tt[:, :], start=False, stop=True)
                        self.mm(ps[3][:, 0:128], s["c_ident"][:, :], TtT[:, :], start=True, stop=False)
                        self.mm(ps[3][:, 0:128], Tt[:, :], XpT[:, :], start=False, stop=True)
                        self.act(Tt[:, :], ps[2][:, 0:128], AF.Copy)
                        self.act(TtT[:, :], ps[3][:, 0:128], AF.Copy)
                        if h == 0 and cg == 0 and j == 0 and "dbg3" in self.dbg:
                            self.dma("gpsimd", self.d("dbg3")[:, 3584:3712], s["c_maskAu"][:, :])
                        if h == 0 and cg == 0 and "dbg3" in self.dbg:
                            self.dma("gpsimd", self.d("dbg3")[:, 1536 + j * 384:1536 + j * 384 + 128], Tt[:, :])
                            self.dma("gpsimd", self.d("dbg3")[:, 1536 + j * 384 + 128:1536 + j * 384 + 256], Xp[:, :])
                            self.dma("gpsimd", self.d("dbg3")[:, 1536 + j * 384 + 256:1536 + j * 384 + 384], XpT[:, :])
                    for dc in range(2):
                        self.mm(ps[4][:, dc * 128:(dc + 1) * 128],
                                kbt[:, dc * 128:(dc + 1) * 128], TtT[:, :])
                    for dc in range(2):
                        self.v("tensor_scalar", wng[:, dc * 128:(dc + 1) * 128],
                               ps[4][:, dc * 128:(dc + 1) * 128], -1.0, None, ALU.mult)
                    self.mm(ps[5][:, 0:DK], TtT[:, :], vbt[:, :], start=True, stop=False)
                    for dc in range(2):
                        self.mm(ps[5][:, 0:DK], wng[:, dc * 128:(dc + 1) * 128],
                                S_b[:, dc * DK:(dc + 1) * DK], start=False, stop=(dc == 1))
                    self.act(uh[:, :], ps[5][:, 0:DK], AF.Copy)
                    for dc in range(2):
                        self.mm(ps[6][:, 0:128], khs[dc], qs[dc], start=(dc == 0), stop=(dc == 1))
                    self.v("tensor_tensor", atT[:, :], ps[6][:, 0:128], s["c_maskIncT"][:, :], ALU.mult)
                    for dc in range(2):
                        self.mm(ps[7][:, 0:DK], qs[dc], S_b[:, dc * DK:(dc + 1) * DK],
                                start=(dc == 0), stop=False)
                    self.mm(ps[7][:, 0:DK], atT[:, :], uh[:, :], start=False, stop=True)
                    self.v("tensor_scalar", o_sb[:, :], ps[7][:, 0:DK],
                           s["rstdq_tm"][:, cg * 4 + h: cg * 4 + h + 1], None, ALU.mult)
                    self.dma("gpsimd", self.d("delta_tm")[ROW0 + cg * CH:ROW0 + (cg + 1) * CH,
                                                          h * DK:(h + 1) * DK], o_sb[:, :])
                    if h == 0 and cg == 0 and "dbg3" in self.dbg:
                        dbg = self.d("dbg3")
                        self.dma("gpsimd", dbg[:, 0:128], A_sb[:, :])
                        self.dma("gpsimd", dbg[:, 128:256], Tt[:, :])
                        self.dma("gpsimd", dbg[:, 256:512], uh[:, :])
                        self.dma("gpsimd", dbg[:, 512:768], wng[:, :])
                        self.dma("gpsimd", dbg[:, 768:1024], g_kht[:, 0:256])
                        self.dma("gpsimd", dbg[:, 1024:1280], g_vbt[:, 0:256])
                        self.dma("gpsimd", dbg[:, 1280:1408], atT[:, :])
                    for dc in range(2):
                        self.mm(ps[dc][:, 0:DK], kht[:, dc * 128:(dc + 1) * 128], uh[:, :])
                    for dc in range(2):
                        self.v("tensor_tensor", S_f[:, dc * DK:(dc + 1) * DK],
                               S_f[:, dc * DK:(dc + 1) * DK], ps[dc][:, 0:DK], ALU.add)
                        self.act(S_b[:, dc * DK:(dc + 1) * DK], S_f[:, dc * DK:(dc + 1) * DK], AF.Copy)

    def s4_fir(self):
        s = self.sb
        o = 0
        def nxt(name, shape, dtype):
            nonlocal o
            h = self.ar(name, shape, dtype, o)
            o += (int(np.prod(shape[1:])) * DTB[dtype] + 511) // 512 * 512
            return h
        fw = nxt("s4a_fw", [128, 8 * 42], F32)
        dgf = nxt("s4a_dg", [128, 42 * 128], BF16)
        vwf = nxt("s4a_vwf", [128, 2 * 320], F32)
        vw = nxt("s4a_vw", [128, 2 * 320], BF16)
        ob = nxt("s4a_ob", [128, 4 * 256], BF16)
        idx11 = nxt("s4a_idx11", [128, NST4], I32)
        self.dma("sync", fw[:, :].rearrange("p (c t) -> p c t", c=8),
                 self.d("firw").rearrange("(c p) t -> p c t", p=128))
        TAPS = [(0, 0, 1), (1, 1, 3), (2, 4, 7), (3, 11, 31)]
        for nm in ("4L", "4P", "4E", "4S"):
            self.rsem(nm)
        for c in range(8):
            for t in range(42):
                self.v("tensor_scalar", dgf[:, t * 128:(t + 1) * 128], s["c_ident"][:, :],
                       fw[:, c * 42 + t: c * 42 + t + 1], None, ALU.mult)
            for st in range(NST4):
                self.v("tensor_scalar_add", idx11[:, st:st + 1], self.sb["c_idxA"][:, c:c + 1],
                       (0 if st == 0 else st * ST4 - 32))
            self.memset(vwf[:, 0:32], 0.0)
            oc0 = self.ocnt
            l0_, p0_, e0_, s0_ = (self.rc[nm] for nm in ("4L", "4P", "4E", "4S"))
            l_after, cp_after, ev_after = {}, {}, {}

            def gath4(st):
                b = st % 2
                w_ = [("O", oc0)] if st < 2 else [("4E", cp_after[st - 2])]
                if st == 0:
                    self.rop("gpsimd", lambda b=b, st=st: self.nc.gpsimd.indirect_dma_start(
                        vwf[:, b * 320 + 32:b * 320 + 320], None,
                        bass.AP(self.dram["v_fm"], 0, [[1, 128], [1, 288]]),
                        IndirectOffsetOnAxis(idx11[:, st:st + 1], 1)),
                        incs=[("4L", 16)], waits=w_)
                else:
                    self.rop("gpsimd", lambda b=b, st=st: self.nc.gpsimd.indirect_dma_start(
                        vwf[:, b * 320:b * 320 + 320], None,
                        bass.AP(self.dram["v_fm"], 0, [[1, 128], [1, 320]]),
                        IndirectOffsetOnAxis(idx11[:, st:st + 1], 1)),
                        incs=[("4L", 16)], waits=w_)
                l_after[st] = self.rc["4L"]

            gath4(0)
            for st in range(NST4):
                if st + 1 < NST4:
                    gath4(st + 1)
                b = st % 2
                cw = [("4L", l_after.get(st + 1, l_after[st]))]
                if st >= 2:
                    cw.append(("4P", p0_ + (st - 2) * 4 + 4))
                if st == 0:
                    cw.append(("O", oc0))
                self.rop("scalar", lambda b=b: self.nc.scalar.activation(
                    out=vw[:, b * 320:(b + 1) * 320], in_=vwf[:, b * 320:(b + 1) * 320],
                    func=AF.Copy), incs=[("4E", 1)], waits=cw)
                cp_after[st] = self.rc["4E"]
                for br, t0_, ntap in TAPS:
                    bank = self.ps[br]
                    w_ = [("4E", cp_after[st])]
                    if st >= 1:
                        w_.append(("4E", ev_after[(st - 1, br)]))
                    if c == 0 and st == 0 and br == 0:
                        w_.append(("O", oc0))
                    for i in range(ntap):
                        self.rop("tensor", lambda bank=bank, t0_=t0_, i=i, b=b, ntap=ntap:
                                 self.nc.tensor.matmul(
                                     bank[:, 0:ST4], dgf[:, (t0_ + i) * 128:(t0_ + i + 1) * 128],
                                     vw[:, b * 320 + 32 - (ntap - 1) + i:
                                        b * 320 + 32 - (ntap - 1) + i + ST4],
                                     start=(i == 0), stop=(i == ntap - 1)),
                                 incs=[("4P", 1)] if i == ntap - 1 else [],
                                 waits=w_ if i == 0 else [])
                    ew = [("4P", p0_ + st * 4 + br + 1)]
                    if st >= 1:
                        ew.append(("4S", s0_ + ((st - 1) * 4 + br + 1) * 16))
                    self.rop("scalar", lambda br=br, bank=bank: self.nc.scalar.activation(
                        out=ob[:, br * 256:br * 256 + ST4], in_=bank[:, 0:ST4], func=AF.Copy),
                        incs=[("4E", 1)], waits=ew)
                    ev_after[(st, br)] = self.rc["4E"]
                    self.rop("gpsimd", lambda br=br, c=c, st=st: self.nc.gpsimd.dma_start(
                        out=self.d("fir_dram")[(br * 8 + c) * 128:(br * 8 + c + 1) * 128,
                                               st * ST4:(st + 1) * ST4],
                        in_=ob[:, br * 256:br * 256 + ST4]),
                        incs=[("4S", 16)], waits=[("4E", ev_after[(st, br)])])
            self.rjoin([(nm, self.rc[nm]) for nm in ("4L", "4P", "4E", "4S")])

    def s4_window(self):
        s = self.sb
        o = 0
        def nxt(name, shape, dtype):
            nonlocal o
            h = self.ar(name, shape, dtype, o)
            o += (int(np.prod(shape[1:])) * DTB[dtype] + 511) // 512 * 512
            return h
        N = ST4
        w_ow = nxt("s4_ow", [128, 8 * D], BF16)
        w_m1 = nxt("s4_m1", [128, 8 * D], BF16)
        w_m1s = [nxt(f"s4_m1s{k}", [24, D], BF16) for k in range(4)]
        w_m2 = nxt("s4_m2", [128, 8 * 24], BF16)
        w_l0 = nxt("s4_l0", [128, 8 * D], BF16)
        w_gw = nxt("s4_gw", [128, 8 * 8], BF16)
        w_U = nxt("s4_U", [128, 8 * 128], BF16)
        w_V = nxt("s4_V", [128, D], BF16)
        c_rsel = nxt("s4_rsel", [24, 24 * 128], F32)
        b_v = nxt("s4_bv", [128, 8 * N], F32)
        b_vb = nxt("s4_bvb", [128, 8 * N], BF16)
        b_d = nxt("s4_bd", [128, 8 * N], BF16)
        b_f = [nxt(f"s4_bf{i}", [128, 8 * N], BF16) for i in range(4)]
        xnw = nxt("s4_xnw", [128, 8 * N], BF16)
        sq = nxt("s4_sq", [128, 8 * N], BF16)
        stm = nxt("s4_stm", [24, N], BF16)
        sts = nxt("s4_sts", [24, N], BF16)
        sta = nxt("s4_sta", [24, N], BF16)
        stl = nxt("s4_stl", [24, N], BF16)
        mf = nxt("s4_mf", [24, N], F32)
        t1 = nxt("s4_t1", [24, N], F32)
        hg = nxt("s4_hg", [128, 8 * N], BF16)
        expw = nxt("s4_expw", [24, N], F32)
        wf = nxt("s4_wf", [24, N], F32)
        fus = nxt("s4_fus", [128, 8 * N], F32)
        fzn = nxt("s4_fzn", [128, 8 * N], BF16)
        s_t = nxt("s4_s", [128, 8 * N], F32)
        g_t = nxt("s4_g", [128, 8 * N], F32)
        tf = nxt("s4_tf", [128, N], F32)
        tb = nxt("s4_tb", [128, N], BF16)
        x1t = nxt("s4_x1", [128, 8 * N], F32)
        sqf = nxt("s4_sqf", [128, N], F32)
        o_xn1t = o
        xn1t = nxt("s4_xn1", [128, 8 * N], BF16)
        ab = self.ar("s4_ab", [128, 8 * N], BF16, o_xn1t)
        xn1tt = nxt("s4_xn1t", [128, D], BF16)
        egl = nxt("s4_egl", [8, N], F32)
        erc = nxt("s4_erc", [1, N], F32)
        lnm = nxt("s4_lnm", [1, 2 * N], F32)

        for kc in range(8):
            self.dma("gpsimd", w_ow[:, kc * D:(kc + 1) * D], self.d("o_wT")[kc * 128:(kc + 1) * 128, :])
            self.dma("gpsimd", w_m1[:, kc * D:(kc + 1) * D], self.d("mlp_w1p")[kc * 128:(kc + 1) * 128, :])
            self.dma("gpsimd", w_m2[:, kc * 24:(kc + 1) * 24], self.d("mlp_w2")[kc * 128:(kc + 1) * 128, :])
            self.dma("gpsimd", w_l0[:, kc * D:(kc + 1) * D], self.d("l0f_wT")[kc * 128:(kc + 1) * 128, :])
            self.dma("gpsimd", w_gw[:, kc * 8:(kc + 1) * 8], self.d("l0f_gw")[kc * 128:(kc + 1) * 128, :])
            self.dma("gpsimd", w_U[:, kc * 128:(kc + 1) * 128], self.d("l0f_U")[kc * 128:(kc + 1) * 128, :])
        for k in range(4):
            self.dma("gpsimd", w_m1s[k][:, :], self.d("mlp_w1p")[1024 + k * 24:1024 + (k + 1) * 24, :])
        self.dma("gpsimd", w_V[:, :], self.d("l0f_V")[:, :])
        self.dma("sync", c_rsel[:, :], self.d("rsel24")[:, :])
        ps = self.ps

        for st in range(NST4):
            w0 = st * N
            self.v("tensor_scalar_add", self.sb["c_idxS"][:, :], self.sb["c_idxA"][:, :], w0)
            for c in range(8):
                ic = self.sb["c_idxS"][:, c:c + 1]
                self.windma("v_fm", N, b_v[:, c * N:(c + 1) * N], ic)
                self.windma("xn_fm", N, xnw[:, c * N:(c + 1) * N], ic)
                for br in range(4):
                    self.dma("gpsimd", b_f[br][:, c * N:(c + 1) * N],
                             self.d("fir_dram")[(br * 8 + c) * 128:(br * 8 + c + 1) * 128, w0:w0 + N])
                self.act(b_vb[:, c * N:(c + 1) * N], b_v[:, c * N:(c + 1) * N], AF.Copy)
            for tbk in range(N // 128):
                self.v("tensor_scalar_add", self.sb["c_idxD"][:, :], self.sb["c_idxB"][:, :], (w0 + tbk * 128) * D)
                self.windma("delta_tm", D, xn1tt[:, :], self.sb["c_idxD"][:, 0:1])
                for c in range(8):
                    self.trans(ps[3][:, 0:256].bitcast(BF16)[:, 0:128],
                               xn1tt[:, c * 128:(c + 1) * 128], s["c_ident"][:, :])
                    self.v("tensor_copy", b_d[:, c * N + tbk * 128: c * N + tbk * 128 + 128],
                           ps[3][:, 0:256].bitcast(BF16)[:, 0:128])
            branches = [b_f[0], b_f[1], b_f[2], b_f[3], b_d, b_vb]
            for bi, bt in enumerate(branches):
                for c in range(8):
                    self.act(sq[:, c * N:(c + 1) * N], bt[:, c * N:(c + 1) * N], AF.Square)
                for c in range(8):
                    sl = bt[:, c * N:(c + 1) * N]
                    self.v("scalar_tensor_tensor", ab[:, c * N:(c + 1) * N], sl, -1.0, sl,
                           ALU.mult, ALU.max)
                for c in range(8):
                    h4 = c // 2
                    col = 24 - (bi * 4 + h4)
                    self.mm(ps[0][0:24, 0:N], s["c_ind49"][:, col:col + 24], bt[:, c * N:(c + 1) * N],
                            start=(bi == 0 and c == 0), stop=(bi == 5 and c == 7))
                    self.mm(ps[1][0:24, 0:N], s["c_ind49"][:, col:col + 24], ab[:, c * N:(c + 1) * N],
                            start=(bi == 0 and c == 0), stop=(bi == 5 and c == 7))
                    self.mm(ps[2][0:24, 0:N], s["c_ind49"][:, col:col + 24], sq[:, c * N:(c + 1) * N],
                            start=(bi == 0 and c == 0), stop=(bi == 5 and c == 7))
            self.v("tensor_scalar", mf[:, :], ps[0][0:24, 0:N], 1.0 / 256, None, ALU.mult)
            self.v("tensor_copy", stm[:, :], mf[:, :])
            self.v("tensor_scalar", sta[:, :], ps[1][0:24, 0:N], 1.0 / 256, None, ALU.mult)
            self.act(stl[:, :], ps[2][0:24, 0:N], AF.Sqrt)
            self.v("tensor_tensor", t1[:, :], ps[0][0:24, 0:N], mf[:, :], ALU.mult)
            self.v("tensor_tensor", t1[:, :], ps[2][0:24, 0:N], t1[:, :], ALU.subtract)
            self.v("tensor_scalar", t1[:, :], t1[:, :], 1.0 / 255, None, ALU.mult)
            self.act(sts[:, :], t1[:, :], AF.Sqrt)
            def g_items():
                its = []
                for oc in range(8):
                    def mk(oc):
                        def mms(bank):
                            fns = []
                            for kc in range(8):
                                fns.append(lambda bank=bank, kc=kc, oc=oc: self.nc.tensor.matmul(
                                    bank, w_m1[:, kc * D + oc * 128: kc * D + oc * 128 + 128],
                                    xnw[:, kc * N:(kc + 1) * N], start=(kc == 0), stop=False))
                            for k in range(4):
                                stt = (stm, sts, sta, stl)[k]
                                fns.append(lambda bank=bank, k=k, oc=oc, stt=stt: self.nc.tensor.matmul(
                                    bank, w_m1s[k][:, oc * 128:(oc + 1) * 128], stt[:, :],
                                    start=False, stop=(k == 3)))
                            return fns
                        def ev(bank):
                            return lambda: self.nc.scalar.activation(
                                out=hg[:, oc * N:(oc + 1) * N], in_=bank, func=AF.Gelu,
                                bias=s["c_b1"][:, oc:oc + 1])
                        return {"mms": mms, "evac": ev}
                    its.append(mk(oc))
                return its
            self.mmpipe(g_items(), nbank=2, bank0=3, width=N)
            for kc in range(8):
                self.mm(ps[5][0:24, 0:N], w_m2[:, kc * 24:(kc + 1) * 24],
                        hg[:, kc * N:(kc + 1) * N], start=(kc == 0), stop=(kc == 7))
            self.act(expw[:, :], ps[5][0:24, 0:N], AF.Exp,
                     bias=s["c_b2i"][:, 0:1], scale=s["c_invt"][:, 0:1])
            self.mm(ps[6][0:4, 0:N], s["c_hsel"][:, :], expw[:, :])
            self.v("tensor_scalar_add", t1[0:4, :], ps[6][0:4, 0:N], 0.0)
            self.v("reciprocal", t1[0:4, :], t1[0:4, :])
            self.mm(ps[6][0:24, 0:N], s["c_hselT"][:, :], t1[0:4, :])
            self.v("tensor_tensor", wf[:, :], ps[6][0:24, 0:N], expw[:, :], ALU.mult)
            for h4 in range(4):
                for br in range(6):
                    bank = ps[br % 3]
                    self.mm(bank[:, 0:N],
                            c_rsel[:, (h4 * 6 + br) * 128:(h4 * 6 + br + 1) * 128], wf[:, :])
                    for cc in range(2):
                        c = h4 * 2 + cc
                        src = branches[br]
                        sl = (b_v[:, c * N:(c + 1) * N] if br == 5 else src[:, c * N:(c + 1) * N])
                        dst = fus[:, c * N:(c + 1) * N]
                        if br == 0:
                            self.v("tensor_tensor", dst, bank[:, 0:N], sl, ALU.mult)
                        else:
                            self.v("tensor_tensor", tf[:, :], bank[:, 0:N], sl, ALU.mult)
                            self.v("tensor_add", dst, dst, tf[:, :])
            for c in range(8):
                self.act(sq[:, c * N:(c + 1) * N], fus[:, c * N:(c + 1) * N], AF.Square)
            for c in range(8):
                h4 = c // 2
                self.mm(ps[3][0:4, 0:N], s["c_ind49"][:, 24 - h4:28 - h4], sq[:, c * N:(c + 1) * N],
                        start=(c == 0), stop=(c == 7))
            self.v("tensor_scalar", t1[0:4, :], ps[3][0:4, 0:N], 1.0 / 256, 1e-5, ALU.mult, ALU.add)
            self.act(t1[4:8, :] if False else mf[0:4, :], t1[0:4, :], AF.Sqrt)
            self.v("reciprocal", t1[0:4, :], mf[0:4, :])
            for h4 in range(4):
                bank = ps[4]
                self.mm(bank[:, 0:N], s["c_sel4f"][:, h4 * 128:(h4 + 1) * 128], t1[0:4, :])
                for cc in range(2):
                    c = h4 * 2 + cc
                    self.v("scalar_tensor_tensor", fzn[:, c * N:(c + 1) * N],
                           fus[:, c * N:(c + 1) * N], s["c_onorm"][:, c % 2:c % 2 + 1],
                           bank[:, 0:N], ALU.mult, ALU.mult)
            def ow_items():
                its = []
                for oc in range(8):
                    def mk(oc):
                        def mms(bank):
                            return [lambda bank=bank, kc=kc, oc=oc: self.nc.tensor.matmul(
                                bank, w_ow[:, kc * D + oc * 128: kc * D + oc * 128 + 128],
                                fzn[:, kc * N:(kc + 1) * N], start=(kc == 0), stop=(kc == 7))
                                for kc in range(8)]
                        def ev(bank):
                            return lambda: self.nc.scalar.activation(
                                out=s_t[:, oc * N:(oc + 1) * N], in_=bank, func=AF.Copy)
                        return {"mms": mms, "evac": ev}
                    its.append(mk(oc))
                return its
            self.mmpipe(ow_items(), nbank=2, bank0=5, width=N)
            for kc in range(8):
                self.mm(ps[0][0:8, 0:N], w_gw[:, kc * 8:(kc + 1) * 8],
                        xnw[:, kc * N:(kc + 1) * N], start=(kc == 0), stop=(kc == 7))
            self.act(egl[:, :], ps[0][0:8, 0:N], AF.Exp)
            self.mm(ps[1][0:1, 0:N], s["c_onescol"][0:8, 0:1], egl[:, :])
            self.v("tensor_scalar_add", erc[:, :], ps[1][0:1, 0:N], 0.0)
            self.v("reciprocal", erc[:, :], erc[:, :])
            self.mm(ps[1][:, 0:N], s["c_onesf"][0:1, 0:128], erc[:, :])
            self.mm(ps[2][:, 0:N], s["c_e16"][:, :], egl[:, :])
            for kc in range(8):
                self.mm(ps[0][:, 0:N], w_U[:, kc * 128:(kc + 1) * 128],
                        xnw[:, kc * N:(kc + 1) * N], start=(kc == 0), stop=(kc == 7))
            self.v("tensor_scalar_add", tf[:, :], ps[0][:, 0:N], 0.0)
            self.v("tensor_tensor", tf[:, :], ps[2][:, 0:N], tf[:, :], ALU.mult)
            self.v("scalar_tensor_tensor", tb[:, :], tf[:, :], 0.0625, ps[1][:, 0:N], ALU.mult, ALU.mult)
            def l0_items():
                its = []
                for oc in range(8):
                    def mk(oc):
                        def mms(bank):
                            fns = [lambda bank=bank, kc=kc, oc=oc: self.nc.tensor.matmul(
                                bank, w_l0[:, kc * D + oc * 128: kc * D + oc * 128 + 128],
                                xnw[:, kc * N:(kc + 1) * N], start=(kc == 0), stop=False)
                                for kc in range(8)]
                            fns.append(lambda bank=bank, oc=oc: self.nc.tensor.matmul(
                                bank, w_V[:, oc * 128:(oc + 1) * 128], tb[:, :],
                                start=False, stop=True))
                            return fns
                        def ev(bank):
                            return lambda: self.nc.scalar.activation(
                                out=g_t[:, oc * N:(oc + 1) * N], in_=bank, func=AF.Silu)
                        return {"mms": mms, "evac": ev}
                    its.append(mk(oc))
                return its
            self.mmpipe(l0_items(), nbank=2, bank0=3, width=N)
            if st == 3 and "dbg4" in self.dbg:
                dbg = self.d("dbg4")
                self.dma("gpsimd", dbg[:, 0:256], s_t[:, 0:N])
                self.dma("gpsimd", dbg[:, 256:512], g_t[:, 0:N])
                self.dma("gpsimd", dbg[:, 512:768], fzn[:, 0:N])
                self.dma("gpsimd", dbg[0:24, 768:1024], wf[:, :])
                self.dma("gpsimd", dbg[0:24, 1024:1280], stm[:, :])
                self.dma("gpsimd", dbg[0:24, 1280:1536], sts[:, :])
                self.dma("gpsimd", dbg[0:24, 1536:1792], sta[:, :])
                self.dma("gpsimd", dbg[0:24, 1792:2048], stl[:, :])
                self.dma("gpsimd", dbg[:, 2048:2304], hg[:, 0:N])
                self.dma("gpsimd", dbg[:, 2304:2560], fus[:, 0:N])
                self.dma("gpsimd", dbg[:, 2560:2816], b_d[:, 0:N])
            for c in range(8):
                self.windma("x_fm", N, sqf[:, :], self.sb["c_idxS"][:, c:c + 1])
                self.v("tensor_tensor", tf[:, :], s_t[:, c * N:(c + 1) * N],
                       g_t[:, c * N:(c + 1) * N], ALU.mult)
                self.v("tensor_add", x1t[:, c * N:(c + 1) * N], sqf[:, :], tf[:, :])
                self.dma("gpsimd", self.d("x1_fm")[c * 128:(c + 1) * 128, w0:w0 + N],
                         x1t[:, c * N:(c + 1) * N])
            for c in range(8):
                self.mm(ps[6][0:1, 0:N], s["c_onescol"][:, 0:1], x1t[:, c * N:(c + 1) * N],
                        start=(c == 0), stop=(c == 7))
            for c in range(8):
                self.act(sqf[:, :], x1t[:, c * N:(c + 1) * N], AF.Square)
                self.mm(ps[7][0:1, 0:N], s["c_onescol"][:, 0:1], sqf[:, :],
                        start=(c == 0), stop=(c == 7))
            self.v("tensor_scalar", lnm[:, 0:N], ps[6][0:1, 0:N], 1.0 / D, None, ALU.mult)
            self.v("tensor_scalar", lnm[:, N:2 * N], ps[7][0:1, 0:N], 1.0 / D, None, ALU.mult)
            self.v("tensor_tensor", erc[:, :], lnm[:, 0:N], lnm[:, 0:N], ALU.mult)
            self.v("tensor_tensor", erc[:, :], lnm[:, N:2 * N], erc[:, :], ALU.subtract)
            self.v("tensor_scalar_add", erc[:, :], erc[:, :], 1e-5)
            self.act(erc[:, :], erc[:, :], AF.Sqrt)
            self.v("reciprocal", erc[:, :], erc[:, :])
            self.mm(ps[6][:, 0:N], s["c_onesf"][0:1, 0:128], lnm[:, 0:N])
            self.mm(ps[7][:, 0:N], s["c_onesf"][0:1, 0:128], erc[:, :])
            for c in range(8):
                self.v("tensor_tensor", tf[:, :], x1t[:, c * N:(c + 1) * N], ps[6][:, 0:N], ALU.subtract)
                self.v("tensor_tensor", xn1t[:, c * N:(c + 1) * N], tf[:, :], ps[7][:, 0:N], ALU.mult)
                self.dma("gpsimd", self.d("xn1_fm")[c * 128:(c + 1) * 128, w0:w0 + N],
                         xn1t[:, c * N:(c + 1) * N])
            for tbk in range(N // 128):
                for c in range(8):
                    self.trans(ps[c % 2][:, 0:256].bitcast(BF16)[:, 0:128],
                               xn1t[:, c * N + tbk * 128: c * N + tbk * 128 + 128], s["c_ident"][:, :])
                    self.v("tensor_copy", xn1tt[:, c * 128:(c + 1) * 128],
                           ps[c % 2][:, 0:256].bitcast(BF16)[:, 0:128])
                self.dma("gpsimd", self.d("xn1_tm")[w0 + tbk * 128:w0 + tbk * 128 + 128, :], xn1tt[:, :])

    def s5_l1attn(self):
        s = self.sb
        o = 0
        def nxt(name, shape, dtype):
            nonlocal o
            h = self.ar(name, shape, dtype, o)
            o += (int(np.prod(shape[1:])) * DTB[dtype] + 511) // 512 * 512
            return h
        w_wv = nxt("s5_wv", [128, 8 * 16], BF16)
        w_kw = nxt("s5_kw", [128, 8 * 264], BF16)
        w_out = nxt("s5_out", [128, 8 * D], BF16)
        xnb = nxt("s5_xnb", [128, 8 * 128], BF16)
        f8 = nxt("s5_f8", [8, 128], F32)
        p8 = nxt("s5_p8", [8, 128], F32)
        fsum = nxt("s5_fsum", [1, 128], F32)
        tp = nxt("s5_tp", [1, 128], F32)
        posl = nxt("s5_posl", [33, 128], F32)
        pos_tm = nxt("s5_postm", [128, 33], F32)
        lo_i = nxt("s5_loi", [128, 33], I32)
        lo_f = nxt("s5_lof", [128, 33], F32)
        frac = nxt("s5_frac", [128, 33], F32)
        idxg = nxt("s5_idxg", [128, 33], I32)
        wk = [nxt(f"s5_wk{i}", [128, 128], F32) for i in range(3)]
        w_tm = nxt("s5_wtm", [128, 264], F32)
        wh = nxt("s5_wh", [128, 264], F32)
        wl = nxt("s5_wl", [128, 264], F32)
        NB = 16
        wl_all = nxt("s5_wlall", [128, NB * 264], BF16)
        wh_all = nxt("s5_whall", [128, NB * 264], BF16)
        idx_all = nxt("s5_idxall", [128, NB * 33], I32)
        NGT = 4
        gt = nxt("s5_gt", [128, NGT * 2048], BF16)
        dg = nxt("s5_dg", [128, 4 * 2048], BF16)
        ohfm_all = nxt("s5_ohfmall", [128, NB * 1024], BF16)
        s1t = nxt("s5_s1t", [128, 4 * 128], F32)
        ps = self.ps

        for kc in range(8):
            self.dma("gpsimd", w_wv[:, kc * 16:(kc + 1) * 16], self.d("l1a_wavew")[kc * 128:(kc + 1) * 128, :])
            self.dma("gpsimd", w_kw[:, kc * 264:(kc + 1) * 264], self.d("l1a_kw")[kc * 128:(kc + 1) * 128, :])
            self.dma("gpsimd", w_out[:, kc * D:(kc + 1) * D], self.d("l1a_outwT")[kc * 128:(kc + 1) * 128, :])
        for bi in range(16):
            w0 = OUT0 + bi * 128
            for kc in range(8):
                self.dma("sync", xnb[:, kc * 128:(kc + 1) * 128],
                         self.d("xn1_fm")[kc * 128:(kc + 1) * 128, w0:w0 + 128])
            for kc in range(8):
                self.mm(ps[0][0:8, 0:128], w_wv[:, kc * 16:kc * 16 + 8],
                        xnb[:, kc * 128:(kc + 1) * 128], start=(kc == 0), stop=(kc == 7))
            for kc in range(8):
                self.mm(ps[1][0:8, 0:128], w_wv[:, kc * 16 + 8:kc * 16 + 16],
                        xnb[:, kc * 128:(kc + 1) * 128], start=(kc == 0), stop=(kc == 7))
            self.act(f8[:, :], ps[0][0:8, 0:128], AF.Silu, bias=s["c_wavebf"][:, 0:1])
            self.act(p8[:, :], ps[1][0:8, 0:128], AF.Silu, bias=s["c_wavebp"][:, 0:1])
            self.act(f8[:, :], f8[:, :], AF.Sigmoid)
            self.act(p8[:, :], p8[:, :], AF.Tanh)
            self.v("tensor_scalar", f8[:, :], f8[:, :], 15.0, 1.0, ALU.mult, ALU.add)
            self.v("tensor_scalar", p8[:, :], p8[:, :], 16.0, None, ALU.mult)
            self.mm(ps[2][0:1, 0:128], s["c_onescol"][0:8, 0:1], f8[:, :])
            self.mm(ps[3][0:1, 0:128], s["c_onescol"][0:8, 0:1], p8[:, :])
            self.v("tensor_scalar_add", fsum[:, :], ps[2][0:1, 0:128], 0.0)
            self.v("scalar_tensor_tensor", tp[:, :], ps[3][0:1, 0:128], 0.125,
                   s["c_iotaG"][0:1, w0:w0 + 128], ALU.mult, ALU.add)
            self.mm(ps[4][0:33, 0:128], s["c_sidx8"][:, :], fsum[:, :], start=True, stop=False)
            self.mm(ps[4][0:33, 0:128], s["c_onesf"][0:1, 0:33], tp[:, :], start=False, stop=True)
            self.v("tensor_scalar", posl[:, :], ps[4][0:33, 0:128], 0.0, 4095.0, ALU.max, ALU.min)
            self.v("tensor_scalar", posl[:, :], posl[:, :], s["c_w0col"][:, 0:1], None, ALU.subtract)
            self.trans(ps[5][:, 0:33], posl[:, :], s["c_identf"][0:33, 0:33])
            self.v("tensor_scalar_add", pos_tm[:, :], ps[5][:, 0:33], 0.0)
            self.v("tensor_scalar", lo_f[:, :], pos_tm[:, :], 0.5, None, ALU.subtract)
            self.v("tensor_copy", lo_i[:, :], lo_f[:, :])
            self.v("tensor_copy", lo_f[:, :], lo_i[:, :])
            self.v("tensor_sub", frac[:, :], pos_tm[:, :], lo_f[:, :])
            self.v("tensor_scalar", idx_all[:, bi * 33:(bi + 1) * 33], lo_i[:, :], D, None, ALU.mult)
            for ocr, rows in ((0, 128), (1, 128), (2, 8)):
                bank = ps[6]
                for kc in range(8):
                    self.mm(bank[0:rows, 0:128], w_kw[:, kc * 264 + ocr * 128: kc * 264 + ocr * 128 + rows],
                            xnb[:, kc * 128:(kc + 1) * 128], start=(kc == 0), stop=(kc == 7))
                self.act(wk[ocr][0:rows, :], bank[0:rows, 0:128], AF.Silu, bias=s["c_kb1"][:, ocr:ocr + 1][0:rows, :])
                self.trans(ps[7][:, 0:rows], wk[ocr][0:rows, :], s["c_identf"][0:rows, 0:rows])
                self.v("tensor_scalar_add", w_tm[:, ocr * 128:ocr * 128 + rows], ps[7][:, 0:rows], 0.0)
            self.v("tensor_tensor", wh[:, :].rearrange("p (h s) -> p h s", h=8),
                   w_tm[:, :].rearrange("p (h s) -> p h s", h=8),
                   frac[:, :].unsqueeze(1).broadcast_to([128, 8, 33]), ALU.mult)
            self.v("tensor_sub", wl[:, :], w_tm[:, :], wh[:, :])
            self.act(wl_all[:, bi * 264:(bi + 1) * 264], wl[:, :], AF.Copy)
            self.act(wh_all[:, bi * 264:(bi + 1) * 264], wh[:, :], AF.Copy)

        for nm in ("5g", "5d", "5p", "5e", "5q"):
            self.rsem(nm)
        p0, d0, e0, gg0, q0 = (self.rc[nm] for nm in ("5p", "5d", "5e", "5g", "5q"))
        oc0 = self.ocnt
        NGI = NB * 33
        PARA5 = getattr(KB, "PARA5", False)
        _rop0 = self.rop
        def rop_p5(eng, fn, incs=(), waits=()):
            if PARA5:
                waits = list(waits) + [(nm, self.rc[nm]) for nm in ("5g", "5d", "5p", "5e")]
            _rop0(eng, fn, incs=incs, waits=waits)
        self.rop = rop_p5

        def gath(gidx, w_):
            bi2, si2 = divmod(gidx, 33)
            buf = gidx % NGT
            self.rop("gpsimd",
                     lambda buf=buf, bi2=bi2, si2=si2: self.nc.gpsimd.indirect_dma_start(
                         gt[:, buf * 2048:(buf + 1) * 2048], None,
                         bass.AP(self.dram["xn1_tm"], 0, [[1, 128], [1, 2048]]),
                         IndirectOffsetOnAxis(idx_all[:, bi2 * 33 + si2:bi2 * 33 + si2 + 1], 1)),
                     incs=[("5g", 16)], waits=w_)

        for gidx in range(NGT):
            gath(gidx, [("O", oc0)])
        for bi in range(NB):
            for si in range(33):
                gidx = bi * 33 + si
                if gidx + NGT < NGI:
                    gath(gidx + NGT, [("5p", p0 + gidx + 1)])
                dwait = ([("5p", p0 + gidx - 3)] if gidx >= 4 else [("O", oc0)])
                for lohi in range(2):
                    wall = wl_all if lohi == 0 else wh_all
                    dv = dg[:, (gidx % 4) * 2048 + lohi * 1024:
                            (gidx % 4) * 2048 + (lohi + 1) * 1024]
                    self.rop("vector",
                             lambda dv=dv, wall=wall, bi=bi, si=si:
                             self.nc.vector.tensor_tensor(
                                 dv.rearrange("p (h d) -> p h d", h=8),
                                 s["c_ident"][:, :].unsqueeze(1).broadcast_to([128, 8, 128]),
                                 wall[:, bi * 264:(bi + 1) * 264]
                                 .rearrange("p (h s) -> p h s", h=8)[:, :, si:si + 1]
                                 .broadcast_to([128, 8, 128]), ALU.mult),
                             incs=[("5d", 1)], waits=(dwait if lohi == 0 else []))
                buf = gidx % NGT
                for lohi in range(2):
                    for h8 in range(8):
                        w_ = []
                        if lohi == 0 and h8 == 0:
                            w_ = [("5g", gg0 + 16 * min(gidx + 2, NGI)),
                                  ("5d", d0 + min(2 * gidx + 4, 2 * NGI))]
                            if si == 0:
                                w_.append(("5e", e0 + 8 * bi))
                                if bi == 0:
                                    w_.append(("O", oc0))
                        incs_ = []
                        if lohi == 1 and h8 == 7:
                            incs_.append(("5p", 1))
                        elif lohi == 1 and si == 32:
                            incs_.append(("5q", 1))
                        self.rop("tensor",
                                 lambda buf=buf, lohi=lohi, h8=h8, gidx=gidx:
                                 self.nc.tensor.matmul(
                                     ps[h8][:, 0:128],
                                     gt[:, buf * 2048 + lohi * D + h8 * 128:
                                        buf * 2048 + lohi * D + (h8 + 1) * 128],
                                     dg[:, (gidx % 4) * 2048 + lohi * 1024 + h8 * 128:
                                        (gidx % 4) * 2048 + lohi * 1024 + (h8 + 1) * 128],
                                     start=(gidx % 33 == 0 and lohi == 0),
                                     stop=(lohi == 1)),
                                 incs=incs_, waits=w_)
            for h8 in range(8):
                self.rop("scalar",
                         lambda h8=h8, bi=bi: self.nc.scalar.activation(
                             out=ohfm_all[:, (bi * 8 + h8) * 128:(bi * 8 + h8 + 1) * 128],
                             in_=ps[h8][:, 0:128], func=AF.Copy),
                         incs=[("5e", 1)],
                         waits=([("5q", q0 + 7 * bi + h8 + 1)] if h8 < 7
                                else [("5p", p0 + 33 * (bi + 1))]))
        self.rop = _rop0
        self.rjoin([("5g", self.rc["5g"]), ("5d", self.rc["5d"]),
                    ("5p", self.rc["5p"]), ("5e", self.rc["5e"]),
                    ("5q", self.rc["5q"])])
        if "dbg4" in self.dbg:
            dbg = self.d("dbg4")
            self.dma("gpsimd", dbg[:, 0:1024], ohfm_all[:, 0:1024])
            self.v("tensor_copy", wl[:, :], wl_all[:, 0:264])
            self.dma("gpsimd", dbg[:, 1024:1288], wl[:, :])
            self.v("tensor_copy", wh[:, 0:33], idx_all[:, 0:33])
            self.dma("gpsimd", dbg[:, 1288:1321], wh[:, 0:33])
            self.dma("gpsimd", dbg[:, 1321:2345], dg[:, 0:1024])
            self.dma("gpsimd", dbg[:, 2345:3369], gt[:, 0:1024])
            self.v("tensor_copy", wh[:, :], wh_all[:, 0:264])
            self.dma("gpsimd", dbg[:, 3369:3633], wh[:, :])

        items = []
        for bi in range(NB):
            for oc in range(8):
                def mk(bi=bi, oc=oc):
                    def mms(bank):
                        return [(lambda kc=kc, bank=bank: self.nc.tensor.matmul(
                            bank[:, 0:128],
                            w_out[:, kc * D + oc * 128: kc * D + oc * 128 + 128],
                            ohfm_all[:, (bi * 8 + kc) * 128:(bi * 8 + kc + 1) * 128],
                            start=(kc == 0), stop=(kc == 7))) for kc in range(8)]
                    it = len(items)
                    sl = s1t[:, (it % 4) * 128:(it % 4 + 1) * 128]
                    return {
                        "mms": mms,
                        "evac": lambda bank, sl=sl: (lambda: self.nc.scalar.activation(
                            out=sl, in_=bank[:, 0:128], func=AF.Silu)),
                        "store": lambda sl=sl, bi=bi, oc=oc: self.nc.gpsimd.dma_start(
                            out=self.d("s1_fm")[oc * 128:(oc + 1) * 128, bi * 128:(bi + 1) * 128],
                            in_=sl),
                    }
                items.append(mk())
        self.mmpipe(items, nbank=2, bank0=0, width=128)

    def s6_moe(self):
        s = self.sb
        ps = self.ps
        o = 0
        def nxt(name, shape, dtype):
            nonlocal o
            h = self.ar(name, shape, dtype, o)
            o += (int(np.prod(shape[1:])) * DTB[dtype] + 511) // 512 * 512
            return h
        w_wg = nxt("s6_wg", [128, 8 * 8], BF16)
        xnb = nxt("s6_xnb", [128, 8 * 256], BF16)
        lgf = nxt("s6_lgf", [8, 256], F32)
        lg_tm = nxt("s6_lgtm", [128, 2 * 8], F32)
        mx = nxt("s6_mx", [128, 2], F32)
        mi = nxt("s6_mi", [128, 2 * 8], I32)
        m_tm = nxt("s6_mtm", [128, 2 * 8], F32)
        me = nxt("s6_me", [1, 8 * 256], F32)
        xe = nxt("s6_xe", [128, 8 * 256], BF16)
        for kc in range(8):
            self.dma("gpsimd", w_wg[:, kc * 8:(kc + 1) * 8], self.d("l1f_wg")[kc * 128:(kc + 1) * 128, :])
        NT = 256
        for tt in range(NOUT // NT):
            w0 = OUT0 + tt * NT
            for kc in range(8):
                self.dma("sync", xnb[:, kc * NT:(kc + 1) * NT],
                         self.d("xn1_fm")[kc * 128:(kc + 1) * 128, w0:w0 + NT])
            for kc in range(8):
                self.mm(ps[0][0:8, 0:NT], w_wg[:, kc * 8:(kc + 1) * 8],
                        xnb[:, kc * NT:(kc + 1) * NT], start=(kc == 0), stop=(kc == 7))
            self.v("tensor_scalar", lgf[:, :], ps[0][0:8, 0:NT], s["c_bg"][:, 0:1], None, ALU.add)
            for tb in range(2):
                self.trans(ps[1][:, 0:8], lgf[:, tb * 128:(tb + 1) * 128], s["c_identf"][0:8, 0:8])
                self.v("tensor_scalar_add", lg_tm[:, tb * 8:(tb + 1) * 8], ps[1][:, 0:8], 0.0)
                self.v("reduce_max", mx[:, tb:tb + 1], lg_tm[:, tb * 8:(tb + 1) * 8], AX.X)
                self.v("tensor_tensor", mi[:, tb * 8:(tb + 1) * 8], lg_tm[:, tb * 8:(tb + 1) * 8],
                       mx[:, tb:tb + 1].broadcast_to([128, 8]), ALU.is_ge)
                self.v("tensor_copy", m_tm[:, tb * 8:(tb + 1) * 8], mi[:, tb * 8:(tb + 1) * 8])
                for e in range(8):
                    self.trans(ps[2][0:1, 0:128], m_tm[:, tb * 8 + e: tb * 8 + e + 1], s["c_identf"][:, :])
                    self.v("tensor_scalar_add", me[0:1, e * NT + tb * 128: e * NT + (tb + 1) * 128],
                           ps[2][0:1, 0:128], 0.0)
            for e in range(8):
                self.mm(ps[3][:, 0:NT], s["c_onesf"][0:1, 0:128], me[0:1, e * NT:(e + 1) * NT])
                for kc in range(8):
                    self.v("tensor_tensor", xe[:, kc * NT:(kc + 1) * NT],
                           xnb[:, kc * NT:(kc + 1) * NT], ps[3][:, 0:NT], ALU.mult)
                for kc in range(8):
                    self.dma("gpsimd", self.d("xt_dram")[(e * 8 + kc) * 128:(e * 8 + kc + 1) * 128,
                                                         tt * NT:(tt + 1) * NT],
                             xe[:, kc * NT:(kc + 1) * NT])
        o = 0
        w1b = nxt("s6_w1b", [128, 2 * 8 * 128], BF16)
        xeb = nxt("s6_xeb", [128, 2 * 8 * NOUT], BF16)
        htb = nxt("s6_htb", [128, 2 * 512], BF16)
        for nm in ("bWL", "bXE", "bMM", "bGE", "bST"):
            self.rsem(nm)
        oc_e = self.ocnt
        mm_hist = {}
        ge_hist = {}
        wl_cnt = 0
        xe_cnt = 0
        gidx = 0

        def wload(e, hc):
            nonlocal wl_cnt
            slot = (e * 32 + hc) % 2
            w_ = [("O", oc_e)] if (e * 32 + hc) < 2 else [("bMM", (e * 32 + hc - 1) * 4)]
            self.rop("gpsimd", lambda slot=slot, e=e, hc=hc: self.nc.gpsimd.dma_start(
                out=w1b[:, slot * 1024:(slot + 1) * 1024].rearrange("p (kc h) -> p kc h", kc=8),
                in_=self.d("l1f_w1")[e * D:(e + 1) * D, hc * 128:(hc + 1) * 128]
                .rearrange("(kc p) h -> p kc h", p=128)),
                incs=[("bWL", 16)], waits=w_)
            wl_cnt += 16

        def xload(e):
            nonlocal xe_cnt
            slot = e % 2
            w_ = [("O", oc_e)] if e < 2 else [("bMM", (e - 1) * 32 * 4)]
            for kc in range(8):
                self.rop("sync", lambda slot=slot, e=e, kc=kc: self.nc.sync.dma_start(
                    out=xeb[:, slot * 8 * NOUT + kc * NOUT: slot * 8 * NOUT + (kc + 1) * NOUT],
                    in_=self.d("xt_dram")[(e * 8 + kc) * 128:(e * 8 + kc + 1) * 128, :]),
                    incs=[("bXE", 16)], waits=w_ if kc == 0 else [])
                xe_cnt += 16

        xload(0); wload(0, 0)
        xload(1); wload(0, 1)
        for e in range(8):
            if e < 7:
                pass
            for hc in range(32):
                gi = e * 32 + hc
                slot = gi % 2
                es = e % 2
                for t4 in range(4):
                    bank = ps[t4 % 2]
                    bkey = t4 % 2
                    w_ = []
                    if t4 == 0:
                        w_ = [("bWL", (gi + 1) * 16), ("bXE", (e + 1) * 8 * 16 if e < 8 else 0)]
                        if e >= 2:
                            w_[1] = ("bXE", min((e + 1), 8) * 8 * 16)
                    if bkey in ge_hist:
                        w_.append(("bGE", ge_hist[bkey]))
                    for kc in range(8):
                        self.rop("tensor", lambda bank=bank, slot=slot, es=es, kc=kc, t4=t4, e=e:
                                 self.nc.tensor.matmul(
                                     bank[:, 0:512], w1b[:, slot * 1024 + kc * 128: slot * 1024 + (kc + 1) * 128],
                                     xeb[:, es * 8 * NOUT + kc * NOUT + t4 * 512: es * 8 * NOUT + kc * NOUT + t4 * 512 + 512],
                                     start=(kc == 0), stop=(kc == 7)),
                                 incs=[("bMM", 1)] if kc == 7 else [],
                                 waits=w_ if kc == 0 else [])
                    hslot = (gi * 4 + t4) % 2
                    hw = [("bMM", gi * 4 + t4 + 1)]
                    if gi * 4 + t4 >= 2:
                        hw.append(("bST", (gi * 4 + t4 - 1) * 16))
                    self.rop("scalar", lambda hslot=hslot, bank=bank: self.nc.scalar.activation(
                        out=htb[:, hslot * 512:(hslot + 1) * 512], in_=bank[:, 0:512],
                        func=AF.Gelu_apprx_tanh), incs=[("bGE", 1)], waits=hw)
                    ge_hist[bkey] = self.rc["bGE"]
                    self.rop("gpsimd", lambda e=e, hc=hc, t4=t4, hslot=hslot: self.nc.gpsimd.dma_start(
                        out=self.d("ht_dram")[(e * 32 + hc) * 128:(e * 32 + hc + 1) * 128,
                                              t4 * 512:(t4 + 1) * 512],
                        in_=htb[:, hslot * 512:(hslot + 1) * 512]),
                        incs=[("bST", 16)], waits=[("bGE", self.rc["bGE"])])
                if gi + 2 < 256:
                    wload((gi + 2) // 32, (gi + 2) % 32)
            if e + 2 < 8:
                xload(e + 2)
        self.rjoin([("bST", self.rc["bST"]), ("bMM", self.rc["bMM"])])
        o = 0
        g1acc = nxt("s6_acc", [128, 8 * NOUT], F32)
        w2b = nxt("s6_w2b", [128, 2 * 16 * D], BF16)
        O_X2 = o
        htt = nxt("s6_htt", [128, 2 * 16 * 512], BF16)
        for nm in ("cWL", "cHT", "cMC", "cAD"):
            self.rsem(nm)
        oc_e = self.ocnt

        def w2load(e, half):
            idx = e * 2 + half
            w_ = [("O", oc_e)] if idx < 2 else [("cMC", 32 * e)]
            for i in range(16):
                hc = half * 16 + i
                self.rop("gpsimd", lambda e=e, half=half, hc=hc, i=i: self.nc.gpsimd.dma_start(
                    out=w2b[:, (half * 16 + i) * D:(half * 16 + i + 1) * D],
                    in_=self.d("l1f_w2")[(e * 32 + hc) * 128:(e * 32 + hc + 1) * 128, :]),
                    incs=[("cWL", 16)], waits=w_ if i == 0 else [])

        def htload(e, t4, half):
            idx = (e * 4 + t4) * 2 + half
            w_ = [("O", oc_e)] if idx < 2 else [("cMC", 8 * ((idx - 2) // 2 + 1))]
            self.rop("sync", lambda e=e, t4=t4, half=half: self.nc.sync.dma_start(
                out=htt[:, half * 16 * 512:(half + 1) * 16 * 512].rearrange("p (hc t) -> p hc t", hc=16),
                in_=self.d("ht_dram").rearrange("(g p) t -> g p t", p=128)
                [e * 32 + half * 16:e * 32 + (half + 1) * 16, :, t4 * 512:(t4 + 1) * 512]
                .transpose([1, 0, 2])),
                incs=[("cHT", 16)], waits=w_)

        w2load(0, 0); htload(0, 0, 0)
        w2load(0, 1); htload(0, 0, 1)
        ad_hist = {}
        for e in range(8):
            for t4 in range(4):
                for half in range(2):
                    hidx = (e * 4 + t4) * 2 + half
                    for oc in range(8):
                        bank = ps[oc]
                        w_ = []
                        if oc == 0:
                            w_ = [("cWL", (e * 2 + half + 1) * 256), ("cHT", (hidx + 1) * 16)]
                        if half == 0 and oc in ad_hist:
                            w_.append(("cAD", ad_hist[oc]))
                        for i in range(16):
                            self.rop("tensor", lambda bank=bank, half=half, i=i, oc=oc:
                                     self.nc.tensor.matmul(
                                         bank[:, 0:512],
                                         w2b[:, (half * 16 + i) * D + oc * 128: (half * 16 + i) * D + oc * 128 + 128],
                                         htt[:, half * 16 * 512 + i * 512: half * 16 * 512 + (i + 1) * 512],
                                         start=(half == 0 and i == 0 and True), stop=(half == 1 and i == 15)),
                                     incs=[("cMC", 1)] if (half == 1 and i == 15) else [],
                                     waits=w_ if i == 0 else [])
                    if hidx + 2 < 64:
                        ne, nt4, nh = (hidx + 2) // 8, ((hidx + 2) % 8) // 2, (hidx + 2) % 2
                        htload(ne, nt4, nh)
                    widx = e * 2 + half
                    if t4 == 3 and widx + 2 < 16:
                        w2load((widx + 2) // 2, (widx + 2) % 2)
                for oc in range(8):
                    bank = ps[oc]
                    dst = g1acc[:, oc * NOUT + t4 * 512: oc * NOUT + (t4 + 1) * 512]
                    w_ = [("cMC", (e * 4 + t4) * 8 + oc + 1)]
                    if e == 0:
                        self.rop("vector", lambda dst=dst, bank=bank:
                                 self.nc.vector.tensor_scalar_add(dst, bank[:, 0:512], 0.0),
                                 incs=[("cAD", 1)], waits=w_)
                    else:
                        self.rop("vector", lambda dst=dst, bank=bank:
                                 self.nc.vector.tensor_tensor(dst, dst, bank[:, 0:512], ALU.add),
                                 incs=[("cAD", 1)], waits=w_)
                    ad_hist[oc] = self.rc["cAD"]
        self.rjoin([("cAD", self.rc["cAD"])])
        if "dbg4" in self.dbg:
            for t4 in range(4):
                self.dma("gpsimd", self.d("dbg4")[:, t4 * 512:(t4 + 1) * 512],
                         g1acc[:, 0 * NOUT + t4 * 512: 0 * NOUT + (t4 + 1) * 512])
        o = O_X2
        s1l = nxt("s6_s1l", [128, 512], F32)
        x1l = nxt("s6_x1l", [128, 512], F32)
        g1s = nxt("s6_g1s", [128, 512], F32)
        x2t = nxt("s6_x2t", [128, 512], F32)
        for t4 in range(4):
            for c in range(8):
                self.dma("sync", s1l[:, :], self.d("s1_fm")[c * 128:(c + 1) * 128, t4 * 512:(t4 + 1) * 512])
                self.dma("sync", x1l[:, :], self.d("x1_fm")[c * 128:(c + 1) * 128,
                                                            OUT0 + t4 * 512:OUT0 + (t4 + 1) * 512])
                self.act(g1s[:, :], g1acc[:, c * NOUT + t4 * 512: c * NOUT + (t4 + 1) * 512], AF.Silu)
                self.v("tensor_mul", g1s[:, :], g1s[:, :], s1l[:, :])
                self.v("tensor_add", x2t[:, :], x1l[:, :], g1s[:, :])
                self.dma("gpsimd", self.d("x2_fm")[c * 128:(c + 1) * 128, t4 * 512:(t4 + 1) * 512], x2t[:, :])


def _consts():
    ident = np.eye(128, dtype=np.float32)
    r = np.arange(128)
    maskAl = np.where(r[None, :] < r[:, None], -1.0, 0.0).astype(np.float32)
    maskAu = np.where(r[None, :] > r[:, None], -1.0, 0.0).astype(np.float32)
    maskIncT = np.where(r[:, None] <= r[None, :], 1.0, 0.0).astype(np.float32)
    ind49 = np.zeros((128, 49), np.float32)
    ind49[:, 24] = 1.0
    sel4 = np.zeros((4, 512), np.float32)
    for h in range(4):
        sel4[h, h * 128:(h + 1) * 128] = 1.0
    hsel = np.zeros((24, 4), np.float32)
    for h in range(4):
        hsel[h * 6:(h + 1) * 6, h] = 1.0
    rsel24 = np.zeros((24, 24 * 128), np.float32)
    for rr in range(24):
        rsel24[rr, rr * 128:(rr + 1) * 128] = 1.0
    e16 = np.zeros((8, 128), np.float32)
    for e in range(8):
        e16[e, e * 16:(e + 1) * 16] = 1.0
    return ident, maskAl, maskAu, maskIncT, ind49, sel4, hsel, rsel24, e16


def make_inputs(full, core):
    b, j = core // 2, core % 2
    x = np.asarray(full["L0_attn_hidden_states"][b], np.float32)
    ident, maskAl, maskAu, maskIncT, ind49, sel4, hsel, rsel24, e16 = _consts()
    xf = np.zeros((D, LPAD), np.float32)
    xf[:, ROW0:ROW0 + L] = x.T
    g = lambda n: np.asarray(full[n], np.float32)
    w_qkvb = np.concatenate([g("L0_attn_q_w"), g("L0_attn_k_w"), g("L0_attn_v_w"),
                             g("L0_attn_b_w")], axis=1)
    convw = np.concatenate([g("L0_attn_qc_w")[:, 0, :], g("L0_attn_kc_w")[:, 0, :],
                            g("L0_attn_vc_w")[:, 0, :]], axis=1)
    firw = np.zeros((D, 42), np.float32)
    firw[:, 0:1] = g("L0_attn_fir_w1")[:, 0, :]
    firw[:, 1:4] = g("L0_attn_fir_w3")[:, 0, :]
    firw[:, 4:11] = g("L0_attn_fir_w7")[:, 0, :]
    firw[:, 11:42] = g("L0_attn_fir_w31")[:, 0, :]
    w1 = g("L0_attn_mlp_w1")
    perm = np.arange(1120)
    for kind in range(4):
        for br in range(6):
            for h in range(4):
                perm[1024 + kind * 24 + br * 4 + h] = 1024 + h * 24 + br * 4 + kind
    w1p = np.ascontiguousarray(w1[perm])
    kb264 = g("L1_attn_kernel_b").reshape(8, 64)[:, :33].reshape(264)
    kbp = np.zeros((128, 3), np.float32)
    kbp.T.flat[0:264] = kb264
    W0g = j * 2048 - PAD
    offs = np.zeros((1, 16), np.int32)
    p = np.arange(128)
    idxA = ((p[:, None] + np.arange(8)[None, :] * 128) * LPAD + j * 2048).astype(np.int32)
    idxB = ((j * 2048 + p) * D).astype(np.int32).reshape(128, 1)
    d = {
        "x_tm": x, "x_fm": xf, "w_qkvb": w_qkvb, "convw": convw, "firw": firw,
        "mlp_w1p": w1p, "mlp_b1c": g("L0_attn_mlp_b1").reshape(8, 128).T.copy(),
        "mlp_w2": g("L0_attn_mlp_w2"), "mlp_b2": g("L0_attn_mlp_b2").reshape(24, 1),
        "glt": g("L0_attn_gate_log_temp").reshape(4, 1),
        "onormc": g("L0_attn_onorm_w").reshape(2, 128).T.copy(),
        "o_wT": g("L0_attn_o_w"),
        "l0f_wT": g("L0_ffn_weight").T.copy(),
        "l0f_gw": g("L0_ffn_gate_w").T.copy(),
        "l0f_U": g("L0_ffn_lora_U").reshape(128, D).T.copy(),
        "l0f_V": g("L0_ffn_lora_V").transpose(0, 2, 1).reshape(128, D).copy(),
        "l1a_wavew": g("L1_attn_wave_w").T.copy(),
        "l1a_wavebf": g("L1_attn_wave_b")[0:8].reshape(8, 1),
        "l1a_wavebp": g("L1_attn_wave_b")[8:16].reshape(8, 1),
        "l1a_kw": g("L1_attn_kernel_w").reshape(8, 64, D)[:, :33].reshape(264, D).T.copy(),
        "l1a_kb": kbp,
        "l1a_outwT": g("L1_attn_out_w").T.copy(),
        "l1f_w1": g("L1_ffn_W1").reshape(8 * D, 4096),
        "l1f_w2": g("L1_ffn_W2").reshape(8 * 4096, D),
        "l1f_wg": g("L1_ffn_Wg"), "l1f_bg": g("L1_ffn_bg").reshape(8, 1),
        "ident": ident, "maskAl": maskAl, "maskAu": maskAu, "maskIncT": maskIncT,
        "ind49": ind49, "sel4": sel4, "sel4f": sel4, "hsel": hsel,
        "hselT": np.ascontiguousarray(hsel.T), "rsel24": rsel24, "e16": e16,
        "iotaG": (W0g + np.arange(WIN, dtype=np.float32))[None, :],
        "sidx8": ((np.arange(33, dtype=np.float32) - 16) / 8.0)[None, :],
        "w0col": np.full((33, 1), W0g, np.float32), "offs": offs,
        "idxA": idxA, "idxB": idxB,
    }
    return {k: np.ascontiguousarray(v) for k, v in d.items()}


def build_kb(stage_limit=99, dbg=()):
    kb = KB(stage_limit=stage_limit, dbg=dbg)
    nc = kb.build()
    return kb, nc


def kernel(**inputs):
    kb, nc = build_kb()
    in_maps = [make_inputs(inputs, c) for c in range(8)]
    res = run_bass_kernel_spmd(nc, in_maps, core_ids=list(range(8)))
    out = np.zeros((B, L, D), np.float32)
    for c in range(8):
        b, j = c // 2, c % 2
        out[b, j * 2048:(j + 1) * 2048, :] = res.results[c]["x2_fm"].T
    return out


# version 12
# speedup vs baseline: 1.2454x; 1.2454x over previous
import numpy as np
import ml_dtypes
from contextlib import ExitStack

import concourse.bass as bass
import concourse.mybir as mybir
from concourse.bass_core_rust import IndirectOffsetOnAxis
from concourse.bass_utils import run_bass_kernel_spmd

F32 = mybir.dt.float32
BF16 = mybir.dt.bfloat16
I32 = mybir.dt.int32
AF = mybir.ActivationFunctionType
ALU = mybir.AluOpType
AX = mybir.AxisListType

B, L, D = 4, 4096, 1024
DK = 256
CH = 128
NCHUNK = L // CH
PAD = 384
ROW0 = PAD
WIN = 2 * PAD + 2048
LPAD = L + 2 * PAD
TT = 512
ST4 = 256
NST4 = WIN // ST4
OUT0 = PAD
NOUT = 2048

DTB = {F32: 4, I32: 4, BF16: 2}


class KB:
    def __init__(self, stage_limit=99, dbg=()):
        self.stage_limit = stage_limit
        self.dbg = set(dbg)
        self.nc = bass.Bass("TRN2", target_bir_lowering=False)
        self.sb = {}
        self.dram = {}
        self.eng = None
        self.sbuf_off = 0
        self.regs = {}
        self.rsems = {}
        self.rc = {}
        self.ocnt = 0
        self.cur_eng = None
        self.pend = 0
        self.last_instr = None
        self._build_tensors()

    def sbuf(self, name, shape, dtype, at=None):
        if name in self.sb:
            return self.sb[name]
        base = (self.nc.sbuf_base + 511) // 512 * 512
        nbytes = int(np.prod(shape[1:])) * DTB[dtype]
        off = self.sbuf_off if at is None else at
        off = (off + 511) // 512 * 512
        if off < base:
            off = base
        h = self.nc.alloc_sbuf_tensor_at(name, list(shape), dtype, offset=off)
        if at is None:
            self.sbuf_off = off + nbytes
        assert off + nbytes <= 229300, f"SBUF overflow {name}: {off + nbytes}"
        self.sb[name] = h
        return h

    def din(self, name, shape, dtype):
        self.dram[name] = self.nc.dram_tensor(name, list(shape), dtype, kind="ExternalInput")

    def dout(self, name, shape, dtype):
        self.dram[name] = self.nc.dram_tensor(name, list(shape), dtype, kind="ExternalOutput")

    def dscratch(self, name, shape, dtype):
        kind = "ExternalOutput" if name in self.dbg else "Internal"
        self.dram[name] = self.nc.dram_tensor(name, list(shape), dtype, kind=kind)

    def d(self, name):
        return self.dram[name].ap()

    def on(self, e):
        return self.eng == e

    def _flush(self):
        if self.pend and self.last_instr is not None:
            self.last_instr.then_inc(self.osem, self.pend)
        if self.pend:
            self.ocnt += self.pend
        self.pend = 0
        self.last_instr = None

    def _pre(self, eng, is_dma=False):
        if eng != self.cur_eng:
            self._flush()
            self.cur_eng = eng
            if self.on(eng):
                getattr(self.nc, eng).wait_ge(self.osem, self.ocnt)
        elif is_dma and self.pend:
            pass

    def op(self, eng, fn):
        if eng != self.cur_eng:
            self.cur_eng = eng
            if self.on(eng):
                getattr(self.nc, eng).wait_ge(self.osem, self.ocnt)
        elif eng == "vector" and self.on(eng):
            getattr(self.nc, eng).wait_ge(self.osem, self.ocnt)
        if self.on(eng):
            fn().then_inc(self.osem, 1)
        self.ocnt += 1

    def dmaop(self, q, fn):
        self.cur_eng = q
        if self.on(q):
            getattr(self.nc, q).wait_ge(self.osem, self.ocnt)
            fn().then_inc(self.osem, 16)
        self.ocnt += 16

    def dma(self, q, out, in_, **kw):
        self.dmaop(q, lambda: getattr(self.nc, q).dma_start(out=out, in_=in_, **kw))

    def dmaT(self, q, out, in_):
        self.dmaop(q, lambda: getattr(self.nc, q).dma_start(out=out, in_=in_, transpose=True))

    def idma(self, out, in_, in_off, **kw):
        self.dmaop("gpsimd", lambda: self.nc.gpsimd.indirect_dma_start(
            out, None, in_, in_off, **kw))

    def mm(self, out, lhsT, rhs, start=True, stop=True, tp=None):
        self.op("tensor", lambda: self.nc.tensor.matmul(
            out, lhsT, rhs, start=start, stop=stop, tile_position=tp))

    def trans(self, out, in_, ident):
        self.op("tensor", lambda: self.nc.tensor.transpose(out, in_, ident))

    def act(self, out, in_, func, bias=0.0, scale=1.0):
        self.op("scalar", lambda: self.nc.scalar.activation(
            out=out, in_=in_, func=func, bias=bias, scale=scale))

    def v(self, meth, *a, eng="vector", **kw):
        self.op(eng, lambda: getattr(getattr(self.nc, eng), meth)(*a, **kw))

    def memset(self, ap, val, eng="vector"):
        self.op(eng, lambda: getattr(self.nc, eng).memset(ap, val))

    def dyndma(self, q, tensor_name, const_off, pattern, out, slot=0, load=False):
        def fn():
            e = getattr(self.nc, q)
            rt = self._dynoff(q, const_off, slot)
            ap = bass.AP(self.dram[tensor_name], rt, pattern)
            try:
                if load:
                    return e.dma_start(out=out, in_=ap)
                return e.dma_start(out=ap, in_=out)
            except ValueError as ex:
                import traceback as _tb
                _tb.print_exc()
                raise RuntimeError(f"dyndma {tensor_name} slot={slot} const={const_off} pat={pattern}: {ex}")
        self.dmaop(q, fn)

    def rsem(self, name):
        if name not in self.rsems:
            self.rsems[name] = self.nc.alloc_semaphore(name)
            self.rc[name] = 0
        return self.rsems[name]

    def rop(self, eng, fn, incs=(), waits=()):
        if self.on(eng):
            e = getattr(self.nc, eng)
            for nm, v in waits:
                e.wait_ge(self.osem if nm == "O" else self.rsems[nm], v)
            i = fn()
            for nm, n in incs:
                i.then_inc(self.rsems[nm], n)
        for nm, n in incs:
            self.rc[nm] += n

    def rwait(self, eng, waits):
        if self.on(eng):
            e = getattr(self.nc, eng)
            for nm, v in waits:
                e.wait_ge(self.osem if nm == "O" else self.rsems[nm], v)

    def rjoin(self, waits):
        for e in ("sync", "vector", "tensor", "scalar", "gpsimd"):
            self.rwait(e, waits)
        self.cur_eng = None

    def mmpipe(self, items, nbank=2, bank0=0, width=512):
        self.rsem("pM"); self.rsem("pE"); self.rsem("pS")
        m0, e0, s0 = self.rc["pM"], self.rc["pE"], self.rc["pS"]
        oc_e = self.ocnt
        evac_of_bank = {}
        has_store = any(it.get("store") for it in items)
        for i, it in enumerate(items):
            b = i % nbank
            bank = self.ps[bank0 + b][:, 0:width]
            w_ = [("O", oc_e)] if i < nbank else [("pE", evac_of_bank[b])]
            fns = it["mms"](bank)
            for k, f in enumerate(fns):
                self.rop("tensor", f,
                         incs=[("pM", 1)] if k == len(fns) - 1 else [],
                         waits=w_ if k == 0 else [])
            ew = [("pM", m0 + i + 1)]
            if i == 0:
                ew.append(("O", oc_e))
            if it.get("store") and i >= 2 * nbank:
                ew.append(("pS", self.rc["pS"] - 2 * 16))
            self.rop("scalar", it["evac"](bank), incs=[("pE", 1)], waits=ew)
            evac_of_bank[b] = self.rc["pE"]
            if it.get("store"):
                sw = [("pE", self.rc["pE"])]
                if i == 0:
                    sw.append(("O", oc_e))
                self.rop("gpsimd", it["store"], incs=[("pS", 16)], waits=sw)
        self.rjoin([("pM", self.rc["pM"]), ("pE", self.rc["pE"])] +
                   ([("pS", self.rc["pS"])] if has_store else []))

    def windma(self, tensor_name, n, out, idxcol):
        def fn():
            ap = bass.AP(self.dram[tensor_name], 0, [[1, 128], [1, n]])
            return self.nc.gpsimd.indirect_dma_start(
                out, None, ap, IndirectOffsetOnAxis(idxcol, 1))
        self.dmaop("gpsimd", fn)

    def setup_dynregs(self, q, slot=0):
        pass

    def _dynoff(self, q, const_off, slot):
        e = getattr(self.nc, q)
        if (q, "base") not in self.regs:
            self.regs[(q, "base")] = e.alloc_register(f"rb_{q}")
            self.regs[(q, "tmp")] = e.alloc_register(f"rt_{q}")
        rb, rt = self.regs[(q, "base")], self.regs[(q, "tmp")]
        e.wait_ge(self.osem, self.ocnt)
        e.reg_load(rb, self.sb["c_offs"][0:1, slot:slot + 1])
        e.reg_add(rt, rb, const_off)
        return rt

    def _build_tensors(self):
        self.din("x_tm", [L, D], F32)
        self.din("x_fm", [D, LPAD], F32)
        self.din("w_qkvb", [D, 3 * D + 4], F32)
        self.din("convw", [D, 12], F32)
        self.din("firw", [D, 42], F32)
        self.din("mlp_w1p", [1120, D], F32)
        self.din("mlp_b1c", [128, 8], F32)
        self.din("mlp_w2", [D, 24], F32)
        self.din("mlp_b2", [24, 1], F32)
        self.din("glt", [4, 1], F32)
        self.din("onormc", [128, 2], F32)
        self.din("o_wT", [D, D], F32)
        self.din("l0f_wT", [D, D], F32)
        self.din("l0f_gw", [D, 8], F32)
        self.din("l0f_U", [D, 128], F32)
        self.din("l0f_V", [128, D], F32)
        self.din("l1a_wavew", [D, 16], F32)
        self.din("l1a_wavebf", [8, 1], F32)
        self.din("l1a_wavebp", [8, 1], F32)
        self.din("l1a_kw", [D, 264], F32)
        self.din("l1a_kb", [128, 3], F32)
        self.din("l1a_outwT", [D, D], F32)
        self.din("l1f_w1", [8 * D, 4096], F32)
        self.din("l1f_w2", [8 * 4096, D], F32)
        self.din("l1f_wg", [D, 8], F32)
        self.din("l1f_bg", [8, 1], F32)
        self.din("ident", [128, 128], F32)
        self.din("maskAl", [128, 128], F32)
        self.din("maskAu", [128, 128], F32)
        self.din("maskIncT", [128, 128], F32)
        self.din("ind49", [128, 49], F32)
        self.din("sel4", [4, 512], F32)
        self.din("sel4f", [4, 512], F32)
        self.din("hsel", [24, 4], F32)
        self.din("hselT", [4, 24], F32)
        self.din("rsel24", [24, 24 * 128], F32)
        self.din("e16", [8, 128], F32)
        self.din("iotaG", [1, WIN], F32)
        self.din("sidx8", [1, 33], F32)
        self.din("w0col", [33, 1], F32)
        self.din("offs", [1, 16], I32)
        self.din("idxA", [128, 8], I32)
        self.din("idxB", [128, 1], I32)

        self.dscratch("xn_fm", [D, LPAD], BF16)
        self.dscratch("xn_tmp", [L, D], BF16)
        self.dscratch("q_fm", [D, L], BF16)
        self.dscratch("khat_fm", [D, L], BF16)
        self.dscratch("kb_fm", [D, L], BF16)
        self.dscratch("vb_fm", [D, L], BF16)
        self.dscratch("v_fm", [D, LPAD], F32)
        self.dscratch("delta_tm", [LPAD, D], BF16)
        self.dscratch("fir_dram", [4 * D, WIN], BF16)
        self.dscratch("x1_fm", [D, WIN], F32)
        self.dscratch("xn1_fm", [D, WIN], BF16)
        self.dscratch("xn1_tm", [WIN + 128, D], BF16)
        self.dscratch("s1_fm", [D, NOUT], F32)
        self.dscratch("xt_dram", [8 * D, NOUT], BF16)
        self.dscratch("ht_dram", [8 * 4096, NOUT], BF16)
        self.dscratch("invt_d", [4, 1], F32)
        self.dscratch("dbg3", [128, 4096], F32)
        self.dscratch("dbg4", [128, 4096], F32)
        self.dscratch("dbg5", [128, 16], I32)
        self.dout("x2_fm", [D, NOUT], F32)

        self.sbuf("c_nop", [1, 2], F32)
        self.sbuf("c_ident", [128, 128], BF16)
        self.sbuf("c_identf", [128, 128], F32)
        self.sbuf("c_maskAl", [128, 128], F32)
        self.sbuf("c_maskAu", [128, 128], F32)
        self.sbuf("c_maskIncT", [128, 128], F32)
        self.sbuf("c_ind49", [128, 49], BF16)
        self.sbuf("c_ind49f", [128, 49], F32)
        self.sbuf("c_sel4", [4, 512], BF16)
        self.sbuf("c_sel4f", [4, 512], F32)
        self.sbuf("c_hsel", [24, 4], F32)
        self.sbuf("c_hselT", [4, 24], F32)
        self.sbuf("c_e16", [8, 128], F32)
        self.sbuf("c_iotaG", [1, WIN], F32)
        self.sbuf("c_sidx8", [1, 33], F32)
        self.sbuf("c_w0col", [33, 1], F32)
        self.sbuf("c_offs", [1, 16], I32)
        self.sbuf("c_idxA", [128, 8], I32)
        self.sbuf("c_idxB", [128, 1], I32)
        self.sbuf("c_idxS", [128, 8], I32)
        self.sbuf("c_idxD", [128, 1], I32)
        self.sbuf("c_ones", [1, 512], BF16)
        self.sbuf("c_onesf", [1, 512], F32)
        self.sbuf("c_onescol", [128, 1], F32)
        self.sbuf("c_zero", [128, 512], F32)
        self.sbuf("rstdq_tm", [128, NCHUNK * 4], F32)
        self.sbuf("c_b1", [128, 8], F32)
        self.sbuf("c_b2", [24, 1], F32)
        self.sbuf("c_invt", [24, 1], F32)
        self.sbuf("c_b2i", [24, 1], F32)
        self.sbuf("c_onorm", [128, 2], F32)
        self.sbuf("c_wavebf", [8, 1], F32)
        self.sbuf("c_wavebp", [8, 1], F32)
        self.sbuf("c_kb1", [128, 3], F32)
        self.sbuf("c_bg", [8, 1], F32)
        self.sbuf("c_glt", [4, 1], F32)
        self.sbuf("c_sp", [4, 4], F32)
        self.static_top = ((self.sbuf_off + 511) // 512 * 512)

    def ar(self, name, shape, dtype, off):
        return self.sbuf(name, shape, dtype, at=self.static_top + off)

    def build(self):
        nc = self.nc
        with ExitStack() as ctx:
            E = ctx.enter_context
            self.ps = [E(nc.psum_tensor(f"ps{i}", [128, 512], F32)) for i in range(8)]
            self.osem = E(nc.semaphore("O"))
            blk = E(nc.Block())

            def walk(name):
                def f(handle):
                    self.eng = name
                    self.ocnt = 0
                    self.cur_eng = None
                    self.pend = 0
                    self.last_instr = None
                    for k in self.rc:
                        self.rc[k] = 0
                    self.emit_all()
                return f
            blk.sync(walk("sync"))
            blk.vector(walk("vector"))
            blk.tensor(walk("tensor"))
            blk.scalar(walk("scalar"))
            blk.gpsimd(walk("gpsimd"))
        return nc

    def emit_all(self):
        self.s0_consts()
        if self.stage_limit >= 1:
            self.s1_ln()
        if self.stage_limit >= 2:
            self.s2_proj()
        if self.stage_limit >= 3:
            self.s3_scan()
        if self.stage_limit >= 4:
            self.s4_fir()
        if self.stage_limit >= 5:
            self.s4_window()
        if self.stage_limit >= 6:
            self.s5_l1attn()
        if self.stage_limit >= 7:
            self.s6_moe()
        self._flush()
        for e in ("sync", "vector", "tensor", "scalar", "gpsimd"):
            if self.on(e):
                getattr(self.nc, e).wait_ge(self.osem, self.ocnt)

    def s0_consts(self):
        s = self.sb
        for dst, src, castb in [
            ("c_ident", "ident", True), ("c_identf", "ident", False),
            ("c_maskAl", "maskAl", False), ("c_maskAu", "maskAu", False),
            ("c_maskIncT", "maskIncT", False),
            ("c_ind49", "ind49", True), ("c_ind49f", "ind49", False),
            ("c_sel4", "sel4", True), ("c_sel4f", "sel4f", False),
            ("c_hsel", "hsel", False), ("c_hselT", "hselT", False),
            ("c_e16", "e16", False),
            ("c_iotaG", "iotaG", False), ("c_sidx8", "sidx8", False),
            ("c_w0col", "w0col", False), ("c_offs", "offs", False),
            ("c_idxA", "idxA", False), ("c_idxB", "idxB", False),
            ("c_b1", "mlp_b1c", False), ("c_b2", "mlp_b2", False),
            ("c_onorm", "onormc", False),
            ("c_wavebf", "l1a_wavebf", False), ("c_wavebp", "l1a_wavebp", False),
            ("c_kb1", "l1a_kb", False), ("c_bg", "l1f_bg", False),
            ("c_glt", "glt", False),
        ]:
            self.dma("gpsimd" if castb else "sync", s[dst][:, :], self.d(src)[:, :])
        self.memset(s["c_ones"][0:1, :], 1.0)
        self.memset(s["c_onesf"][0:1, :], 1.0)
        self.memset(s["c_onescol"][:, :], 1.0)
        self.memset(s["c_zero"][:, :], 0.0)
        zf = s["c_zero"][:, :]
        zb = s["c_zero"][:, :].bitcast(BF16)
        for c in range(8):
            r0, r1 = c * 128, (c + 1) * 128
            self.dma("gpsimd", self.d("v_fm")[r0:r1, 0:PAD], zf[:, 0:PAD])
            self.dma("gpsimd", self.d("v_fm")[r0:r1, LPAD - PAD:LPAD], zf[:, 0:PAD])
            self.dma("gpsimd", self.d("xn_fm")[r0:r1, 0:PAD], zb[:, 0:PAD])
            self.dma("gpsimd", self.d("xn_fm")[r0:r1, LPAD - PAD:LPAD], zb[:, 0:PAD])
        for r in range(0, PAD, 128):
            self.dma("gpsimd", self.d("delta_tm")[r:r + 128, :], zb[:, 0:D])
            self.dma("gpsimd", self.d("delta_tm")[LPAD - PAD + r:LPAD - PAD + r + 128, :], zb[:, 0:D])
        self.dma("gpsimd", self.d("xn1_tm")[WIN:WIN + 128, :], zb[:, 0:D])
        tmp, glt = s["c_sp"], s["c_glt"]
        self.v("tensor_scalar", tmp[:, 0:1], glt[:, :], -1.0, None, ALU.mult)
        self.v("tensor_max", tmp[:, 1:2], tmp[:, 0:1], glt[:, :])
        self.v("tensor_scalar_max", tmp[:, 2:3], glt[:, :], 0.0)
        self.act(tmp[:, 3:4], tmp[:, 1:2], AF.Exp, scale=-1.0)
        self.act(tmp[:, 0:1], tmp[:, 3:4], AF.Ln, bias=1.0)
        self.v("tensor_add", tmp[:, 1:2], tmp[:, 0:1], tmp[:, 2:3])
        self.v("tensor_scalar_add", tmp[:, 2:3], tmp[:, 1:2], 0.5)
        self.v("reciprocal", tmp[:, 3:4], tmp[:, 2:3])
        self.dma("gpsimd", self.d("invt_d")[:, :], tmp[:, 3:4])
        for s6 in range(6):
            self.dma("gpsimd", s["c_invt"][s6::6, :], self.d("invt_d")[:, :])
        self.v("tensor_mul", s["c_b2i"][:, :], s["c_invt"][:, :], s["c_b2"][:, :])

    def s1_ln(self):
        s = self.sb
        xt = self.ar("s1_x", [128, D], F32, 0)
        xo = self.ar("s1_o", [128, D], BF16, 4096)
        sq = self.ar("s1_sq", [128, D], F32, 6144)
        mv = self.ar("s1_mv", [128, 8], F32, 10752)
        xnT = self.ar("s1_xnT", [128, L], BF16, 11264)
        for t in range(L // 128):
            self.dma("sync", xt[:, :], self.d("x_tm")[t * 128:(t + 1) * 128, :])
            self.act(sq[:, :], xt[:, :], AF.Square)
            self.v("reduce_sum", mv[:, 0:1], xt[:, :], AX.X)
            self.v("reduce_sum", mv[:, 1:2], sq[:, :], AX.X)
            self.v("tensor_scalar_mul", mv[:, 2:3], mv[:, 0:1], 1.0 / D)
            self.v("tensor_mul", mv[:, 3:4], mv[:, 2:3], mv[:, 2:3])
            self.v("tensor_scalar", mv[:, 4:5], mv[:, 1:2], 1.0 / D, None, ALU.mult)
            self.v("tensor_sub", mv[:, 5:6], mv[:, 4:5], mv[:, 3:4])
            self.v("tensor_scalar_add", mv[:, 5:6], mv[:, 5:6], 1e-5)
            self.act(mv[:, 6:7], mv[:, 5:6], AF.Sqrt)
            self.v("reciprocal", mv[:, 7:8], mv[:, 6:7])
            self.v("scalar_tensor_tensor", xo[:, :], xt[:, :], mv[:, 2:3],
                   mv[:, 7:8].broadcast_to([128, D]), ALU.subtract, ALU.mult)
            self.dma("gpsimd", self.d("xn_tmp")[t * 128:(t + 1) * 128, :], xo[:, :])
        for c in range(8):
            self.dmaT("sync", xnT[:, :], self.d("xn_tmp")[:, c * 128:(c + 1) * 128])
            self.dma("gpsimd", self.d("xn_fm")[c * 128:(c + 1) * 128, ROW0:ROW0 + L],
                     xnT[:, :])

    def s2_proj(self):
        s = self.sb
        o = 0
        def nxt(name, shape, dtype):
            nonlocal o
            h = self.ar(name, shape, dtype, o)
            o += (int(np.prod(shape[1:])) * DTB[dtype] + 511) // 512 * 512
            return h
        xn_sb = nxt("s2_xn", [128, 8 * TT], BF16)
        proj = nxt("s2_proj", [128, 24 * (TT + 4)], BF16)
        qraw = nxt("s2_qraw", [128, 16 * TT], F32)
        vraw = nxt("s2_vraw", [128, 8 * TT], F32)
        xsq = nxt("s2_xsq", [128, TT], BF16)
        qt = [nxt(f"s2_qt{i}", [4, TT], F32) for i in range(6)]
        qb = nxt("s2_qb", [4, 3 * TT], BF16)
        obf = nxt("s2_obf", [128, 2 * TT], BF16)
        wq = nxt("s2_w", [128, 8 * 3076], BF16)
        dg = nxt("s2_diag", [128, 96 * 128], BF16)
        cw = nxt("s2_cw", [128, 96], F32)

        for kc in range(8):
            self.dma("gpsimd", wq[:, kc * 3076:(kc + 1) * 3076],
                     self.d("w_qkvb")[kc * 128:(kc + 1) * 128, :])
        self.dma("sync", cw[:, :].rearrange("p (c t) -> p c t", c=8),
                 self.d("convw").rearrange("(c p) t -> p c t", p=128))
        for m in range(24):
            p3, c = divmod(m, 8)
            for j in range(4):
                self.v("tensor_scalar", dg[:, (m * 4 + j) * 128:(m * 4 + j + 1) * 128],
                       s["c_ident"][:, :], cw[:, c * 12 + p3 * 4 + j: c * 12 + p3 * 4 + j + 1],
                       None, ALU.mult)
        for tt in range(8):
            t0 = tt * TT
            for kc in range(8):
                self.dma("sync", xn_sb[:, kc * TT:(kc + 1) * TT],
                         self.d("xn_fm")[kc * 128:(kc + 1) * 128, ROW0 + t0:ROW0 + t0 + TT])
            for m in range(25):
                bank = self.ps[m % 2]
                rows = 128 if m < 24 else 4
                for kc in range(8):
                    self.mm(bank[0:rows, 0:TT],
                            wq[:, kc * 3076 + m * 128: kc * 3076 + m * 128 + rows],
                            xn_sb[:, kc * TT:(kc + 1) * TT],
                            start=(kc == 0), stop=(kc == 7))
                if m < 24:
                    pr = proj[:, m * 516:(m + 1) * 516]
                    if tt == 0:
                        self.memset(pr[:, 0:4], 0.0)
                    self.act(pr[:, 4:4 + TT], bank[0:128, 0:TT], AF.Copy)
                else:
                    self.act(qt[3][:, 0:TT], bank[0:4, 0:TT], AF.Sigmoid)
            for m in range(24):
                pr = proj[:, m * 516:(m + 1) * 516]
                bank = self.ps[2 + m % 2]
                for j in range(4):
                    self.mm(bank[:, 0:TT], dg[:, (m * 4 + j) * 128:(m * 4 + j + 1) * 128],
                            pr[:, 1 + j: 1 + j + TT], start=(j == 0), stop=(j == 3))
                dst = qraw[:, m * TT:(m + 1) * TT] if m < 16 else vraw[:, (m - 16) * TT:(m - 15) * TT]
                self.act(dst, bank[:, 0:TT], AF.Silu)
                self.v("tensor_copy", pr[:, 0:4], pr[:, TT:TT + 4])
            for qk in range(2):
                bank = self.ps[4]
                for c in range(8):
                    m = qk * 8 + c
                    self.act(xsq[:, 0:TT], qraw[:, m * TT:(m + 1) * TT], AF.Square)
                    h4 = c // 2
                    self.mm(bank[0:4, 0:TT], s["c_ind49"][:, 24 - h4:28 - h4], xsq[:, 0:TT],
                            start=(c == 0), stop=(c == 7))
                self.v("tensor_scalar_add", qt[4][:, 0:TT], bank[0:4, 0:TT], 1e-6)
                self.act(qt[5][:, 0:TT], qt[4][:, 0:TT], AF.Sqrt)
                self.v("reciprocal", qt[qk][:, 0:TT], qt[5][:, 0:TT])
            self.v("tensor_mul", qt[2][:, 0:TT], qt[1][:, 0:TT], qt[3][:, 0:TT])
            self.v("tensor_copy", qb[:, 0:TT], qt[1][:, 0:TT])
            self.v("tensor_copy", qb[:, TT:2 * TT], qt[2][:, 0:TT])
            self.v("tensor_copy", qb[:, 2 * TT:3 * TT], qt[3][:, 0:TT])
            for cb in range(TT // 128):
                bank = self.ps[5]
                self.trans(bank[:, 0:4], qt[0][:, cb * 128:(cb + 1) * 128], s["c_identf"][0:4, 0:4])
                chn = (t0 + cb * 128) // 128
                self.v("tensor_copy", s["rstdq_tm"][:, chn * 4:(chn + 1) * 4], bank[:, 0:4])
            for h4 in range(4):
                for iq in range(3):
                    bank = self.ps[6 + iq % 2]
                    self.mm(bank[:, 0:TT], s["c_sel4"][:, h4 * 128:(h4 + 1) * 128],
                            qb[:, iq * TT:(iq + 1) * TT])
                    for cc in range(2):
                        m = h4 * 2 + cc
                        src = qraw[:, (8 + m) * TT:(9 + m) * TT] if iq < 2 else vraw[:, m * TT:(m + 1) * TT]
                        ob = obf[:, cc * TT:(cc + 1) * TT]
                        self.v("tensor_tensor", ob, src, bank[:, 0:TT], ALU.mult)
                        nm = ("khat_fm", "kb_fm", "vb_fm")[iq]
                        self.dma("gpsimd", self.d(nm)[m * 128:(m + 1) * 128, t0:t0 + TT], ob)
            for m in range(8):
                ob = obf[:, (m % 2) * TT:(m % 2 + 1) * TT]
                self.act(ob, qraw[:, m * TT:(m + 1) * TT], AF.Copy)
                self.dma("gpsimd", self.d("q_fm")[m * 128:(m + 1) * 128, t0:t0 + TT], ob)
                self.dma("sync", self.d("v_fm")[m * 128:(m + 1) * 128, ROW0 + t0:ROW0 + t0 + TT],
                         vraw[:, m * TT:(m + 1) * TT])

    def s3_scan(self):
        s = self.sb
        G = 512
        o = 0
        def nxt(name, shape, dtype):
            nonlocal o
            h = self.ar(name, shape, dtype, o)
            o += (int(np.prod(shape[1:])) * DTB[dtype] + 511) // 512 * 512
            return h
        g_q = nxt("s3_q", [128, 2 * G], BF16)
        g_kh = nxt("s3_kh", [128, 2 * G], BF16)
        g_kb = nxt("s3_kb", [128, 2 * G], BF16)
        g_kht = nxt("s3_kht", [128, 4 * DK], BF16)
        g_kbt = nxt("s3_kbt", [128, 4 * DK], BF16)
        g_vbt = nxt("s3_vbt", [128, 4 * DK], BF16)
        S_f = nxt("s3_Sf", [128, 2 * DK], F32)
        S_b = nxt("s3_Sb", [128, 2 * DK], BF16)
        A_sb = nxt("s3_A", [128, 128], BF16)
        AT_sb = nxt("s3_AT", [128, 128], BF16)
        Tt = nxt("s3_T", [128, 128], BF16)
        TtT = nxt("s3_TT", [128, 128], BF16)
        Xp = nxt("s3_X", [128, 128], BF16)
        XpT = nxt("s3_XT", [128, 128], BF16)
        uh = nxt("s3_uh", [128, DK], BF16)
        atT = nxt("s3_atT", [128, 128], BF16)
        wng = nxt("s3_wng", [128, 2 * 128], BF16)
        o_sb = nxt("s3_o", [128, DK], BF16)
        dbgf = nxt("s3_dbgf", [128, 128], F32)

        self.rsem("3L")

        def ldma(q, out, in_, transpose=False):
            kw = {"transpose": True} if transpose else {}
            oc_now = self.ocnt
            self.rop(q, lambda: getattr(self.nc, q).dma_start(out=out, in_=in_, **kw),
                     incs=[("3L", 16)], waits=[("O", oc_now)])

        for h in range(4):
            self.memset(S_f[:, :], 0.0)
            self.memset(S_b[:, :], 0.0)
            r0 = h * DK
            for gi in range(L // G):
                t0 = gi * G
                for dc in range(2):
                    rr = r0 + dc * 128
                    ldma("sync", g_q[:, dc * G:(dc + 1) * G], self.d("q_fm")[rr:rr + 128, t0:t0 + G])
                    ldma("sync", g_kh[:, dc * G:(dc + 1) * G], self.d("khat_fm")[rr:rr + 128, t0:t0 + G])
                    ldma("sync", g_kb[:, dc * G:(dc + 1) * G], self.d("kb_fm")[rr:rr + 128, t0:t0 + G])
                for ci in range(4):
                    tc = t0 + ci * CH
                    ldma("scalar", g_kht[:, ci * DK:(ci + 1) * DK],
                         self.d("khat_fm")[r0:r0 + DK, tc:tc + CH], transpose=True)
                    ldma("scalar", g_kbt[:, ci * DK:(ci + 1) * DK],
                         self.d("kb_fm")[r0:r0 + DK, tc:tc + CH], transpose=True)
                    ldma("scalar", g_vbt[:, ci * DK:(ci + 1) * DK],
                         self.d("vb_fm")[r0:r0 + DK, tc:tc + CH], transpose=True)
                self.rwait("tensor", [("3L", self.rc["3L"])])
                for ci in range(4):
                    cg = gi * 4 + ci
                    qs = [g_q[:, dc * G + ci * CH: dc * G + (ci + 1) * CH] for dc in range(2)]
                    khs = [g_kh[:, dc * G + ci * CH: dc * G + (ci + 1) * CH] for dc in range(2)]
                    kbs = [g_kb[:, dc * G + ci * CH: dc * G + (ci + 1) * CH] for dc in range(2)]
                    kht = g_kht[:, ci * DK:(ci + 1) * DK]
                    kbt = g_kbt[:, ci * DK:(ci + 1) * DK]
                    vbt = g_vbt[:, ci * DK:(ci + 1) * DK]
                    ps = self.ps
                    for dc in range(2):
                        self.mm(ps[0][:, 0:128], kbs[dc], khs[dc], start=(dc == 0), stop=(dc == 1))
                    for dc in range(2):
                        self.mm(ps[1][:, 0:128], khs[dc], kbs[dc], start=(dc == 0), stop=(dc == 1))
                    self.v("tensor_tensor", A_sb[:, :], ps[0][:, 0:128], s["c_maskAl"][:, :], ALU.mult)
                    self.v("tensor_tensor", AT_sb[:, :], ps[1][:, 0:128], s["c_maskAu"][:, :], ALU.mult)
                    if h == 0 and cg == 0 and "dbg3" in self.dbg:
                        self.dma("gpsimd", self.d("dbg3")[:, 3328:3456], s["c_maskAu"][:, :])
                    self.v("tensor_add", Tt[:, :], A_sb[:, :], s["c_ident"][:, :])
                    self.v("tensor_add", TtT[:, :], AT_sb[:, :], s["c_ident"][:, :])
                    self.v("tensor_copy", Xp[:, :], A_sb[:, :])
                    self.v("tensor_copy", XpT[:, :], AT_sb[:, :])
                    for j in range(6):
                        self.mm(ps[0][:, 0:128], XpT[:, :], Xp[:, :])
                        self.mm(ps[1][:, 0:128], Xp[:, :], XpT[:, :])
                        self.act(Xp[:, :], ps[0][:, 0:128], AF.Copy)
                        self.act(XpT[:, :], ps[1][:, 0:128], AF.Copy)
                        if h == 0 and cg == 0 and j == 0 and "dbg3" in self.dbg:
                            self.dma("gpsimd", self.d("dbg3")[:, 3456:3584], s["c_maskAu"][:, :])
                        self.mm(ps[2][:, 0:128], s["c_ident"][:, :], Tt[:, :], start=True, stop=False)
                        self.mm(ps[2][:, 0:128], XpT[:, :], Tt[:, :], start=False, stop=True)
                        self.mm(ps[3][:, 0:128], s["c_ident"][:, :], TtT[:, :], start=True, stop=False)
                        self.mm(ps[3][:, 0:128], Tt[:, :], XpT[:, :], start=False, stop=True)
                        self.act(Tt[:, :], ps[2][:, 0:128], AF.Copy)
                        self.act(TtT[:, :], ps[3][:, 0:128], AF.Copy)
                        if h == 0 and cg == 0 and j == 0 and "dbg3" in self.dbg:
                            self.dma("gpsimd", self.d("dbg3")[:, 3584:3712], s["c_maskAu"][:, :])
                        if h == 0 and cg == 0 and "dbg3" in self.dbg:
                            self.dma("gpsimd", self.d("dbg3")[:, 1536 + j * 384:1536 + j * 384 + 128], Tt[:, :])
                            self.dma("gpsimd", self.d("dbg3")[:, 1536 + j * 384 + 128:1536 + j * 384 + 256], Xp[:, :])
                            self.dma("gpsimd", self.d("dbg3")[:, 1536 + j * 384 + 256:1536 + j * 384 + 384], XpT[:, :])
                    for dc in range(2):
                        self.mm(ps[4][:, dc * 128:(dc + 1) * 128],
                                kbt[:, dc * 128:(dc + 1) * 128], TtT[:, :])
                    for dc in range(2):
                        self.v("tensor_scalar", wng[:, dc * 128:(dc + 1) * 128],
                               ps[4][:, dc * 128:(dc + 1) * 128], -1.0, None, ALU.mult)
                    self.mm(ps[5][:, 0:DK], TtT[:, :], vbt[:, :], start=True, stop=False)
                    for dc in range(2):
                        self.mm(ps[5][:, 0:DK], wng[:, dc * 128:(dc + 1) * 128],
                                S_b[:, dc * DK:(dc + 1) * DK], start=False, stop=(dc == 1))
                    self.act(uh[:, :], ps[5][:, 0:DK], AF.Copy)
                    for dc in range(2):
                        self.mm(ps[6][:, 0:128], khs[dc], qs[dc], start=(dc == 0), stop=(dc == 1))
                    self.v("tensor_tensor", atT[:, :], ps[6][:, 0:128], s["c_maskIncT"][:, :], ALU.mult)
                    for dc in range(2):
                        self.mm(ps[7][:, 0:DK], qs[dc], S_b[:, dc * DK:(dc + 1) * DK],
                                start=(dc == 0), stop=False)
                    self.mm(ps[7][:, 0:DK], atT[:, :], uh[:, :], start=False, stop=True)
                    self.v("tensor_scalar", o_sb[:, :], ps[7][:, 0:DK],
                           s["rstdq_tm"][:, cg * 4 + h: cg * 4 + h + 1], None, ALU.mult)
                    self.dma("gpsimd", self.d("delta_tm")[ROW0 + cg * CH:ROW0 + (cg + 1) * CH,
                                                          h * DK:(h + 1) * DK], o_sb[:, :])
                    if h == 0 and cg == 0 and "dbg3" in self.dbg:
                        dbg = self.d("dbg3")
                        self.dma("gpsimd", dbg[:, 0:128], A_sb[:, :])
                        self.dma("gpsimd", dbg[:, 128:256], Tt[:, :])
                        self.dma("gpsimd", dbg[:, 256:512], uh[:, :])
                        self.dma("gpsimd", dbg[:, 512:768], wng[:, :])
                        self.dma("gpsimd", dbg[:, 768:1024], g_kht[:, 0:256])
                        self.dma("gpsimd", dbg[:, 1024:1280], g_vbt[:, 0:256])
                        self.dma("gpsimd", dbg[:, 1280:1408], atT[:, :])
                    for dc in range(2):
                        self.mm(ps[dc][:, 0:DK], kht[:, dc * 128:(dc + 1) * 128], uh[:, :])
                    for dc in range(2):
                        self.v("tensor_tensor", S_f[:, dc * DK:(dc + 1) * DK],
                               S_f[:, dc * DK:(dc + 1) * DK], ps[dc][:, 0:DK], ALU.add)
                        self.act(S_b[:, dc * DK:(dc + 1) * DK], S_f[:, dc * DK:(dc + 1) * DK], AF.Copy)

    def s4_fir(self):
        s = self.sb
        o = 0
        def nxt(name, shape, dtype):
            nonlocal o
            h = self.ar(name, shape, dtype, o)
            o += (int(np.prod(shape[1:])) * DTB[dtype] + 511) // 512 * 512
            return h
        fw = nxt("s4a_fw", [128, 8 * 42], F32)
        dgf = nxt("s4a_dg", [128, 42 * 128], BF16)
        vwf = nxt("s4a_vwf", [128, 2 * 320], F32)
        vw = nxt("s4a_vw", [128, 2 * 320], BF16)
        ob = nxt("s4a_ob", [128, 4 * 256], BF16)
        idx11 = nxt("s4a_idx11", [128, NST4], I32)
        self.dma("sync", fw[:, :].rearrange("p (c t) -> p c t", c=8),
                 self.d("firw").rearrange("(c p) t -> p c t", p=128))
        TAPS = [(0, 0, 1), (1, 1, 3), (2, 4, 7), (3, 11, 31)]
        for nm in ("4L", "4P", "4E", "4S"):
            self.rsem(nm)
        for c in range(8):
            for t in range(42):
                self.v("tensor_scalar", dgf[:, t * 128:(t + 1) * 128], s["c_ident"][:, :],
                       fw[:, c * 42 + t: c * 42 + t + 1], None, ALU.mult)
            for st in range(NST4):
                self.v("tensor_scalar_add", idx11[:, st:st + 1], self.sb["c_idxA"][:, c:c + 1],
                       (0 if st == 0 else st * ST4 - 32))
            self.memset(vwf[:, 0:32], 0.0)
            oc0 = self.ocnt
            l0_, p0_, e0_, s0_ = (self.rc[nm] for nm in ("4L", "4P", "4E", "4S"))
            l_after, cp_after, ev_after = {}, {}, {}

            def gath4(st):
                b = st % 2
                w_ = [("O", oc0)] if st < 2 else [("4E", cp_after[st - 2])]
                if st == 0:
                    self.rop("gpsimd", lambda b=b, st=st: self.nc.gpsimd.indirect_dma_start(
                        vwf[:, b * 320 + 32:b * 320 + 320], None,
                        bass.AP(self.dram["v_fm"], 0, [[1, 128], [1, 288]]),
                        IndirectOffsetOnAxis(idx11[:, st:st + 1], 1)),
                        incs=[("4L", 16)], waits=w_)
                else:
                    self.rop("gpsimd", lambda b=b, st=st: self.nc.gpsimd.indirect_dma_start(
                        vwf[:, b * 320:b * 320 + 320], None,
                        bass.AP(self.dram["v_fm"], 0, [[1, 128], [1, 320]]),
                        IndirectOffsetOnAxis(idx11[:, st:st + 1], 1)),
                        incs=[("4L", 16)], waits=w_)
                l_after[st] = self.rc["4L"]

            gath4(0)
            for st in range(NST4):
                if st + 1 < NST4:
                    gath4(st + 1)
                b = st % 2
                cw = [("4L", l_after.get(st + 1, l_after[st]))]
                if st >= 2:
                    cw.append(("4P", p0_ + (st - 2) * 4 + 4))
                if st == 0:
                    cw.append(("O", oc0))
                self.rop("scalar", lambda b=b: self.nc.scalar.activation(
                    out=vw[:, b * 320:(b + 1) * 320], in_=vwf[:, b * 320:(b + 1) * 320],
                    func=AF.Copy), incs=[("4E", 1)], waits=cw)
                cp_after[st] = self.rc["4E"]
                for br, t0_, ntap in TAPS:
                    bank = self.ps[br]
                    w_ = [("4E", cp_after[st])]
                    if st >= 1:
                        w_.append(("4E", ev_after[(st - 1, br)]))
                    if c == 0 and st == 0 and br == 0:
                        w_.append(("O", oc0))
                    for i in range(ntap):
                        self.rop("tensor", lambda bank=bank, t0_=t0_, i=i, b=b, ntap=ntap:
                                 self.nc.tensor.matmul(
                                     bank[:, 0:ST4], dgf[:, (t0_ + i) * 128:(t0_ + i + 1) * 128],
                                     vw[:, b * 320 + 32 - (ntap - 1) + i:
                                        b * 320 + 32 - (ntap - 1) + i + ST4],
                                     start=(i == 0), stop=(i == ntap - 1)),
                                 incs=[("4P", 1)] if i == ntap - 1 else [],
                                 waits=w_ if i == 0 else [])
                    ew = [("4P", p0_ + st * 4 + br + 1)]
                    if st >= 1:
                        ew.append(("4S", s0_ + ((st - 1) * 4 + br + 1) * 16))
                    self.rop("scalar", lambda br=br, bank=bank: self.nc.scalar.activation(
                        out=ob[:, br * 256:br * 256 + ST4], in_=bank[:, 0:ST4], func=AF.Copy),
                        incs=[("4E", 1)], waits=ew)
                    ev_after[(st, br)] = self.rc["4E"]
                    self.rop("gpsimd", lambda br=br, c=c, st=st: self.nc.gpsimd.dma_start(
                        out=self.d("fir_dram")[(br * 8 + c) * 128:(br * 8 + c + 1) * 128,
                                               st * ST4:(st + 1) * ST4],
                        in_=ob[:, br * 256:br * 256 + ST4]),
                        incs=[("4S", 16)], waits=[("4E", ev_after[(st, br)])])
            self.rjoin([(nm, self.rc[nm]) for nm in ("4L", "4P", "4E", "4S")])

    def s4_window(self):
        s = self.sb
        o = 0
        def nxt(name, shape, dtype):
            nonlocal o
            h = self.ar(name, shape, dtype, o)
            o += (int(np.prod(shape[1:])) * DTB[dtype] + 511) // 512 * 512
            return h
        N = ST4
        w_ow = nxt("s4_ow", [128, 8 * D], BF16)
        w_m1 = nxt("s4_m1", [128, 8 * D], BF16)
        w_m1s = [nxt(f"s4_m1s{k}", [24, D], BF16) for k in range(4)]
        w_m2 = nxt("s4_m2", [128, 8 * 24], BF16)
        w_l0 = nxt("s4_l0", [128, 8 * D], BF16)
        w_gw = nxt("s4_gw", [128, 8 * 8], BF16)
        w_U = nxt("s4_U", [128, 8 * 128], BF16)
        w_V = nxt("s4_V", [128, D], BF16)
        c_rsel = nxt("s4_rsel", [24, 24 * 128], F32)
        b_v = nxt("s4_bv", [128, 8 * N], F32)
        b_vb = nxt("s4_bvb", [128, 8 * N], BF16)
        b_d = nxt("s4_bd", [128, 8 * N], BF16)
        b_f = [nxt(f"s4_bf{i}", [128, 8 * N], BF16) for i in range(4)]
        xnw = nxt("s4_xnw", [128, 8 * N], BF16)
        sq = nxt("s4_sq", [128, 8 * N], BF16)
        stm = nxt("s4_stm", [24, N], BF16)
        sts = nxt("s4_sts", [24, N], BF16)
        sta = nxt("s4_sta", [24, N], BF16)
        stl = nxt("s4_stl", [24, N], BF16)
        mf = nxt("s4_mf", [24, N], F32)
        t1 = nxt("s4_t1", [24, N], F32)
        hg = nxt("s4_hg", [128, 8 * N], BF16)
        expw = nxt("s4_expw", [24, N], F32)
        wf = nxt("s4_wf", [24, N], F32)
        fus = nxt("s4_fus", [128, 8 * N], F32)
        fzn = nxt("s4_fzn", [128, 8 * N], BF16)
        s_t = nxt("s4_s", [128, 8 * N], F32)
        g_t = nxt("s4_g", [128, 8 * N], F32)
        tf = nxt("s4_tf", [128, N], F32)
        tb = nxt("s4_tb", [128, N], BF16)
        x1t = nxt("s4_x1", [128, 8 * N], F32)
        sqf = nxt("s4_sqf", [128, N], F32)
        o_xn1t = o
        xn1t = nxt("s4_xn1", [128, 8 * N], BF16)
        ab = self.ar("s4_ab", [128, 8 * N], BF16, o_xn1t)
        xn1tt = nxt("s4_xn1t", [128, D], BF16)
        egl = nxt("s4_egl", [8, N], F32)
        erc = nxt("s4_erc", [1, N], F32)
        lnm = nxt("s4_lnm", [1, 2 * N], F32)

        for kc in range(8):
            self.dma("gpsimd", w_ow[:, kc * D:(kc + 1) * D], self.d("o_wT")[kc * 128:(kc + 1) * 128, :])
            self.dma("gpsimd", w_m1[:, kc * D:(kc + 1) * D], self.d("mlp_w1p")[kc * 128:(kc + 1) * 128, :])
            self.dma("gpsimd", w_m2[:, kc * 24:(kc + 1) * 24], self.d("mlp_w2")[kc * 128:(kc + 1) * 128, :])
            self.dma("gpsimd", w_l0[:, kc * D:(kc + 1) * D], self.d("l0f_wT")[kc * 128:(kc + 1) * 128, :])
            self.dma("gpsimd", w_gw[:, kc * 8:(kc + 1) * 8], self.d("l0f_gw")[kc * 128:(kc + 1) * 128, :])
            self.dma("gpsimd", w_U[:, kc * 128:(kc + 1) * 128], self.d("l0f_U")[kc * 128:(kc + 1) * 128, :])
        for k in range(4):
            self.dma("gpsimd", w_m1s[k][:, :], self.d("mlp_w1p")[1024 + k * 24:1024 + (k + 1) * 24, :])
        self.dma("gpsimd", w_V[:, :], self.d("l0f_V")[:, :])
        self.dma("sync", c_rsel[:, :], self.d("rsel24")[:, :])
        ps = self.ps

        self.rsem("4W")

        def wrop(fn):
            oc_now = self.ocnt
            self.rop("gpsimd", fn, incs=[("4W", 16)], waits=[("O", oc_now)])

        for st in range(NST4):
            w0 = st * N
            self.v("tensor_scalar_add", self.sb["c_idxS"][:, :], self.sb["c_idxA"][:, :], w0)
            for c in range(8):
                ic = self.sb["c_idxS"][:, c:c + 1]
                wrop(lambda c=c, ic=ic: self.nc.gpsimd.indirect_dma_start(
                    b_v[:, c * N:(c + 1) * N], None,
                    bass.AP(self.dram["v_fm"], 0, [[1, 128], [1, N]]),
                    IndirectOffsetOnAxis(ic, 1)))
                wrop(lambda c=c, ic=ic: self.nc.gpsimd.indirect_dma_start(
                    xnw[:, c * N:(c + 1) * N], None,
                    bass.AP(self.dram["xn_fm"], 0, [[1, 128], [1, N]]),
                    IndirectOffsetOnAxis(ic, 1)))
                for br in range(4):
                    wrop(lambda br=br, c=c, w0=w0: self.nc.gpsimd.dma_start(
                        out=b_f[br][:, c * N:(c + 1) * N],
                        in_=self.d("fir_dram")[(br * 8 + c) * 128:(br * 8 + c + 1) * 128, w0:w0 + N]))
            for e_ in ("scalar", "vector", "tensor"):
                self.rwait(e_, [("4W", self.rc["4W"])])
            for c in range(8):
                self.act(b_vb[:, c * N:(c + 1) * N], b_v[:, c * N:(c + 1) * N], AF.Copy)
            for tbk in range(N // 128):
                self.v("tensor_scalar_add", self.sb["c_idxD"][:, :], self.sb["c_idxB"][:, :], (w0 + tbk * 128) * D)
                self.windma("delta_tm", D, xn1tt[:, :], self.sb["c_idxD"][:, 0:1])
                for c in range(8):
                    self.trans(ps[3][:, 0:256].bitcast(BF16)[:, 0:128],
                               xn1tt[:, c * 128:(c + 1) * 128], s["c_ident"][:, :])
                    self.v("tensor_copy", b_d[:, c * N + tbk * 128: c * N + tbk * 128 + 128],
                           ps[3][:, 0:256].bitcast(BF16)[:, 0:128])
            branches = [b_f[0], b_f[1], b_f[2], b_f[3], b_d, b_vb]
            for bi, bt in enumerate(branches):
                for c in range(8):
                    self.act(sq[:, c * N:(c + 1) * N], bt[:, c * N:(c + 1) * N], AF.Square)
                for c in range(8):
                    sl = bt[:, c * N:(c + 1) * N]
                    self.v("scalar_tensor_tensor", ab[:, c * N:(c + 1) * N], sl, -1.0, sl,
                           ALU.mult, ALU.max)
                for c in range(8):
                    h4 = c // 2
                    col = 24 - (bi * 4 + h4)
                    self.mm(ps[0][0:24, 0:N], s["c_ind49"][:, col:col + 24], bt[:, c * N:(c + 1) * N],
                            start=(bi == 0 and c == 0), stop=(bi == 5 and c == 7))
                    self.mm(ps[1][0:24, 0:N], s["c_ind49"][:, col:col + 24], ab[:, c * N:(c + 1) * N],
                            start=(bi == 0 and c == 0), stop=(bi == 5 and c == 7))
                    self.mm(ps[2][0:24, 0:N], s["c_ind49"][:, col:col + 24], sq[:, c * N:(c + 1) * N],
                            start=(bi == 0 and c == 0), stop=(bi == 5 and c == 7))
            self.v("tensor_scalar", mf[:, :], ps[0][0:24, 0:N], 1.0 / 256, None, ALU.mult)
            self.v("tensor_copy", stm[:, :], mf[:, :])
            self.v("tensor_scalar", sta[:, :], ps[1][0:24, 0:N], 1.0 / 256, None, ALU.mult)
            self.act(stl[:, :], ps[2][0:24, 0:N], AF.Sqrt)
            self.v("tensor_tensor", t1[:, :], ps[0][0:24, 0:N], mf[:, :], ALU.mult)
            self.v("tensor_tensor", t1[:, :], ps[2][0:24, 0:N], t1[:, :], ALU.subtract)
            self.v("tensor_scalar", t1[:, :], t1[:, :], 1.0 / 255, None, ALU.mult)
            self.act(sts[:, :], t1[:, :], AF.Sqrt)
            def g_items():
                its = []
                for oc in range(8):
                    def mk(oc):
                        def mms(bank):
                            fns = []
                            for kc in range(8):
                                fns.append(lambda bank=bank, kc=kc, oc=oc: self.nc.tensor.matmul(
                                    bank, w_m1[:, kc * D + oc * 128: kc * D + oc * 128 + 128],
                                    xnw[:, kc * N:(kc + 1) * N], start=(kc == 0), stop=False))
                            for k in range(4):
                                stt = (stm, sts, sta, stl)[k]
                                fns.append(lambda bank=bank, k=k, oc=oc, stt=stt: self.nc.tensor.matmul(
                                    bank, w_m1s[k][:, oc * 128:(oc + 1) * 128], stt[:, :],
                                    start=False, stop=(k == 3)))
                            return fns
                        def ev(bank):
                            return lambda: self.nc.scalar.activation(
                                out=hg[:, oc * N:(oc + 1) * N], in_=bank, func=AF.Gelu,
                                bias=s["c_b1"][:, oc:oc + 1])
                        return {"mms": mms, "evac": ev}
                    its.append(mk(oc))
                return its
            self.mmpipe(g_items(), nbank=2, bank0=3, width=N)
            for kc in range(8):
                self.mm(ps[5][0:24, 0:N], w_m2[:, kc * 24:(kc + 1) * 24],
                        hg[:, kc * N:(kc + 1) * N], start=(kc == 0), stop=(kc == 7))
            self.act(expw[:, :], ps[5][0:24, 0:N], AF.Exp,
                     bias=s["c_b2i"][:, 0:1], scale=s["c_invt"][:, 0:1])
            self.mm(ps[6][0:4, 0:N], s["c_hsel"][:, :], expw[:, :])
            self.v("tensor_scalar_add", t1[0:4, :], ps[6][0:4, 0:N], 0.0)
            self.v("reciprocal", t1[0:4, :], t1[0:4, :])
            self.mm(ps[6][0:24, 0:N], s["c_hselT"][:, :], t1[0:4, :])
            self.v("tensor_tensor", wf[:, :], ps[6][0:24, 0:N], expw[:, :], ALU.mult)
            for h4 in range(4):
                for br in range(6):
                    bank = ps[br % 3]
                    self.mm(bank[:, 0:N],
                            c_rsel[:, (h4 * 6 + br) * 128:(h4 * 6 + br + 1) * 128], wf[:, :])
                    for cc in range(2):
                        c = h4 * 2 + cc
                        src = branches[br]
                        sl = (b_v[:, c * N:(c + 1) * N] if br == 5 else src[:, c * N:(c + 1) * N])
                        dst = fus[:, c * N:(c + 1) * N]
                        if br == 0:
                            self.v("tensor_tensor", dst, bank[:, 0:N], sl, ALU.mult)
                        else:
                            self.v("tensor_tensor", tf[:, :], bank[:, 0:N], sl, ALU.mult)
                            self.v("tensor_add", dst, dst, tf[:, :])
            for c in range(8):
                self.act(sq[:, c * N:(c + 1) * N], fus[:, c * N:(c + 1) * N], AF.Square)
            for c in range(8):
                h4 = c // 2
                self.mm(ps[3][0:4, 0:N], s["c_ind49"][:, 24 - h4:28 - h4], sq[:, c * N:(c + 1) * N],
                        start=(c == 0), stop=(c == 7))
            self.v("tensor_scalar", t1[0:4, :], ps[3][0:4, 0:N], 1.0 / 256, 1e-5, ALU.mult, ALU.add)
            self.act(t1[4:8, :] if False else mf[0:4, :], t1[0:4, :], AF.Sqrt)
            self.v("reciprocal", t1[0:4, :], mf[0:4, :])
            for h4 in range(4):
                bank = ps[4]
                self.mm(bank[:, 0:N], s["c_sel4f"][:, h4 * 128:(h4 + 1) * 128], t1[0:4, :])
                for cc in range(2):
                    c = h4 * 2 + cc
                    self.v("scalar_tensor_tensor", fzn[:, c * N:(c + 1) * N],
                           fus[:, c * N:(c + 1) * N], s["c_onorm"][:, c % 2:c % 2 + 1],
                           bank[:, 0:N], ALU.mult, ALU.mult)
            def ow_items():
                its = []
                for oc in range(8):
                    def mk(oc):
                        def mms(bank):
                            return [lambda bank=bank, kc=kc, oc=oc: self.nc.tensor.matmul(
                                bank, w_ow[:, kc * D + oc * 128: kc * D + oc * 128 + 128],
                                fzn[:, kc * N:(kc + 1) * N], start=(kc == 0), stop=(kc == 7))
                                for kc in range(8)]
                        def ev(bank):
                            return lambda: self.nc.scalar.activation(
                                out=s_t[:, oc * N:(oc + 1) * N], in_=bank, func=AF.Copy)
                        return {"mms": mms, "evac": ev}
                    its.append(mk(oc))
                return its
            self.mmpipe(ow_items(), nbank=2, bank0=5, width=N)
            for kc in range(8):
                self.mm(ps[0][0:8, 0:N], w_gw[:, kc * 8:(kc + 1) * 8],
                        xnw[:, kc * N:(kc + 1) * N], start=(kc == 0), stop=(kc == 7))
            self.act(egl[:, :], ps[0][0:8, 0:N], AF.Exp)
            self.mm(ps[1][0:1, 0:N], s["c_onescol"][0:8, 0:1], egl[:, :])
            self.v("tensor_scalar_add", erc[:, :], ps[1][0:1, 0:N], 0.0)
            self.v("reciprocal", erc[:, :], erc[:, :])
            self.mm(ps[1][:, 0:N], s["c_onesf"][0:1, 0:128], erc[:, :])
            self.mm(ps[2][:, 0:N], s["c_e16"][:, :], egl[:, :])
            for kc in range(8):
                self.mm(ps[0][:, 0:N], w_U[:, kc * 128:(kc + 1) * 128],
                        xnw[:, kc * N:(kc + 1) * N], start=(kc == 0), stop=(kc == 7))
            self.v("tensor_scalar_add", tf[:, :], ps[0][:, 0:N], 0.0)
            self.v("tensor_tensor", tf[:, :], ps[2][:, 0:N], tf[:, :], ALU.mult)
            self.v("scalar_tensor_tensor", tb[:, :], tf[:, :], 0.0625, ps[1][:, 0:N], ALU.mult, ALU.mult)
            def l0_items():
                its = []
                for oc in range(8):
                    def mk(oc):
                        def mms(bank):
                            fns = [lambda bank=bank, kc=kc, oc=oc: self.nc.tensor.matmul(
                                bank, w_l0[:, kc * D + oc * 128: kc * D + oc * 128 + 128],
                                xnw[:, kc * N:(kc + 1) * N], start=(kc == 0), stop=False)
                                for kc in range(8)]
                            fns.append(lambda bank=bank, oc=oc: self.nc.tensor.matmul(
                                bank, w_V[:, oc * 128:(oc + 1) * 128], tb[:, :],
                                start=False, stop=True))
                            return fns
                        def ev(bank):
                            return lambda: self.nc.scalar.activation(
                                out=g_t[:, oc * N:(oc + 1) * N], in_=bank, func=AF.Silu)
                        return {"mms": mms, "evac": ev}
                    its.append(mk(oc))
                return its
            self.mmpipe(l0_items(), nbank=2, bank0=3, width=N)
            if st == 3 and "dbg4" in self.dbg:
                dbg = self.d("dbg4")
                self.dma("gpsimd", dbg[:, 0:256], s_t[:, 0:N])
                self.dma("gpsimd", dbg[:, 256:512], g_t[:, 0:N])
                self.dma("gpsimd", dbg[:, 512:768], fzn[:, 0:N])
                self.dma("gpsimd", dbg[0:24, 768:1024], wf[:, :])
                self.dma("gpsimd", dbg[0:24, 1024:1280], stm[:, :])
                self.dma("gpsimd", dbg[0:24, 1280:1536], sts[:, :])
                self.dma("gpsimd", dbg[0:24, 1536:1792], sta[:, :])
                self.dma("gpsimd", dbg[0:24, 1792:2048], stl[:, :])
                self.dma("gpsimd", dbg[:, 2048:2304], hg[:, 0:N])
                self.dma("gpsimd", dbg[:, 2304:2560], fus[:, 0:N])
                self.dma("gpsimd", dbg[:, 2560:2816], b_d[:, 0:N])
            for c in range(8):
                self.windma("x_fm", N, sqf[:, :], self.sb["c_idxS"][:, c:c + 1])
                self.v("tensor_tensor", tf[:, :], s_t[:, c * N:(c + 1) * N],
                       g_t[:, c * N:(c + 1) * N], ALU.mult)
                self.v("tensor_add", x1t[:, c * N:(c + 1) * N], sqf[:, :], tf[:, :])
                self.dma("gpsimd", self.d("x1_fm")[c * 128:(c + 1) * 128, w0:w0 + N],
                         x1t[:, c * N:(c + 1) * N])
            for c in range(8):
                self.mm(ps[6][0:1, 0:N], s["c_onescol"][:, 0:1], x1t[:, c * N:(c + 1) * N],
                        start=(c == 0), stop=(c == 7))
            for c in range(8):
                self.act(sqf[:, :], x1t[:, c * N:(c + 1) * N], AF.Square)
                self.mm(ps[7][0:1, 0:N], s["c_onescol"][:, 0:1], sqf[:, :],
                        start=(c == 0), stop=(c == 7))
            self.v("tensor_scalar", lnm[:, 0:N], ps[6][0:1, 0:N], 1.0 / D, None, ALU.mult)
            self.v("tensor_scalar", lnm[:, N:2 * N], ps[7][0:1, 0:N], 1.0 / D, None, ALU.mult)
            self.v("tensor_tensor", erc[:, :], lnm[:, 0:N], lnm[:, 0:N], ALU.mult)
            self.v("tensor_tensor", erc[:, :], lnm[:, N:2 * N], erc[:, :], ALU.subtract)
            self.v("tensor_scalar_add", erc[:, :], erc[:, :], 1e-5)
            self.act(erc[:, :], erc[:, :], AF.Sqrt)
            self.v("reciprocal", erc[:, :], erc[:, :])
            self.mm(ps[6][:, 0:N], s["c_onesf"][0:1, 0:128], lnm[:, 0:N])
            self.mm(ps[7][:, 0:N], s["c_onesf"][0:1, 0:128], erc[:, :])
            for c in range(8):
                self.v("tensor_tensor", tf[:, :], x1t[:, c * N:(c + 1) * N], ps[6][:, 0:N], ALU.subtract)
                self.v("tensor_tensor", xn1t[:, c * N:(c + 1) * N], tf[:, :], ps[7][:, 0:N], ALU.mult)
                self.dma("gpsimd", self.d("xn1_fm")[c * 128:(c + 1) * 128, w0:w0 + N],
                         xn1t[:, c * N:(c + 1) * N])
            for tbk in range(N // 128):
                for c in range(8):
                    self.trans(ps[c % 2][:, 0:256].bitcast(BF16)[:, 0:128],
                               xn1t[:, c * N + tbk * 128: c * N + tbk * 128 + 128], s["c_ident"][:, :])
                    self.v("tensor_copy", xn1tt[:, c * 128:(c + 1) * 128],
                           ps[c % 2][:, 0:256].bitcast(BF16)[:, 0:128])
                self.dma("gpsimd", self.d("xn1_tm")[w0 + tbk * 128:w0 + tbk * 128 + 128, :], xn1tt[:, :])

    def s5_l1attn(self):
        s = self.sb
        o = 0
        def nxt(name, shape, dtype):
            nonlocal o
            h = self.ar(name, shape, dtype, o)
            o += (int(np.prod(shape[1:])) * DTB[dtype] + 511) // 512 * 512
            return h
        w_wv = nxt("s5_wv", [128, 8 * 16], BF16)
        w_kw = nxt("s5_kw", [128, 8 * 264], BF16)
        w_out = nxt("s5_out", [128, 8 * D], BF16)
        xnb = nxt("s5_xnb", [128, 8 * 128], BF16)
        f8 = nxt("s5_f8", [8, 128], F32)
        p8 = nxt("s5_p8", [8, 128], F32)
        fsum = nxt("s5_fsum", [1, 128], F32)
        tp = nxt("s5_tp", [1, 128], F32)
        posl = nxt("s5_posl", [33, 128], F32)
        pos_tm = nxt("s5_postm", [128, 33], F32)
        lo_i = nxt("s5_loi", [128, 33], I32)
        lo_f = nxt("s5_lof", [128, 33], F32)
        frac = nxt("s5_frac", [128, 33], F32)
        idxg = nxt("s5_idxg", [128, 33], I32)
        wk = [nxt(f"s5_wk{i}", [128, 128], F32) for i in range(3)]
        w_tm = nxt("s5_wtm", [128, 264], F32)
        wh = nxt("s5_wh", [128, 264], F32)
        wl = nxt("s5_wl", [128, 264], F32)
        NB = 16
        wl_all = nxt("s5_wlall", [128, NB * 264], BF16)
        wh_all = nxt("s5_whall", [128, NB * 264], BF16)
        idx_all = nxt("s5_idxall", [128, NB * 33], I32)
        NGT = 4
        gt = nxt("s5_gt", [128, NGT * 2048], BF16)
        dg = nxt("s5_dg", [128, 4 * 2048], BF16)
        ohfm_all = nxt("s5_ohfmall", [128, NB * 1024], BF16)
        s1t = nxt("s5_s1t", [128, 4 * 128], F32)
        ps = self.ps

        for kc in range(8):
            self.dma("gpsimd", w_wv[:, kc * 16:(kc + 1) * 16], self.d("l1a_wavew")[kc * 128:(kc + 1) * 128, :])
            self.dma("gpsimd", w_kw[:, kc * 264:(kc + 1) * 264], self.d("l1a_kw")[kc * 128:(kc + 1) * 128, :])
            self.dma("gpsimd", w_out[:, kc * D:(kc + 1) * D], self.d("l1a_outwT")[kc * 128:(kc + 1) * 128, :])
        for bi in range(16):
            w0 = OUT0 + bi * 128
            for kc in range(8):
                self.dma("sync", xnb[:, kc * 128:(kc + 1) * 128],
                         self.d("xn1_fm")[kc * 128:(kc + 1) * 128, w0:w0 + 128])
            for kc in range(8):
                self.mm(ps[0][0:8, 0:128], w_wv[:, kc * 16:kc * 16 + 8],
                        xnb[:, kc * 128:(kc + 1) * 128], start=(kc == 0), stop=(kc == 7))
            for kc in range(8):
                self.mm(ps[1][0:8, 0:128], w_wv[:, kc * 16 + 8:kc * 16 + 16],
                        xnb[:, kc * 128:(kc + 1) * 128], start=(kc == 0), stop=(kc == 7))
            self.act(f8[:, :], ps[0][0:8, 0:128], AF.Silu, bias=s["c_wavebf"][:, 0:1])
            self.act(p8[:, :], ps[1][0:8, 0:128], AF.Silu, bias=s["c_wavebp"][:, 0:1])
            self.act(f8[:, :], f8[:, :], AF.Sigmoid)
            self.act(p8[:, :], p8[:, :], AF.Tanh)
            self.v("tensor_scalar", f8[:, :], f8[:, :], 15.0, 1.0, ALU.mult, ALU.add)
            self.v("tensor_scalar", p8[:, :], p8[:, :], 16.0, None, ALU.mult)
            self.mm(ps[2][0:1, 0:128], s["c_onescol"][0:8, 0:1], f8[:, :])
            self.mm(ps[3][0:1, 0:128], s["c_onescol"][0:8, 0:1], p8[:, :])
            self.v("tensor_scalar_add", fsum[:, :], ps[2][0:1, 0:128], 0.0)
            self.v("scalar_tensor_tensor", tp[:, :], ps[3][0:1, 0:128], 0.125,
                   s["c_iotaG"][0:1, w0:w0 + 128], ALU.mult, ALU.add)
            self.mm(ps[4][0:33, 0:128], s["c_sidx8"][:, :], fsum[:, :], start=True, stop=False)
            self.mm(ps[4][0:33, 0:128], s["c_onesf"][0:1, 0:33], tp[:, :], start=False, stop=True)
            self.v("tensor_scalar", posl[:, :], ps[4][0:33, 0:128], 0.0, 4095.0, ALU.max, ALU.min)
            self.v("tensor_scalar", posl[:, :], posl[:, :], s["c_w0col"][:, 0:1], None, ALU.subtract)
            self.trans(ps[5][:, 0:33], posl[:, :], s["c_identf"][0:33, 0:33])
            self.v("tensor_scalar_add", pos_tm[:, :], ps[5][:, 0:33], 0.0)
            self.v("tensor_scalar", lo_f[:, :], pos_tm[:, :], 0.5, None, ALU.subtract)
            self.v("tensor_copy", lo_i[:, :], lo_f[:, :])
            self.v("tensor_copy", lo_f[:, :], lo_i[:, :])
            self.v("tensor_sub", frac[:, :], pos_tm[:, :], lo_f[:, :])
            self.v("tensor_scalar", idx_all[:, bi * 33:(bi + 1) * 33], lo_i[:, :], D, None, ALU.mult)
            for ocr, rows in ((0, 128), (1, 128), (2, 8)):
                bank = ps[6]
                for kc in range(8):
                    self.mm(bank[0:rows, 0:128], w_kw[:, kc * 264 + ocr * 128: kc * 264 + ocr * 128 + rows],
                            xnb[:, kc * 128:(kc + 1) * 128], start=(kc == 0), stop=(kc == 7))
                self.act(wk[ocr][0:rows, :], bank[0:rows, 0:128], AF.Silu, bias=s["c_kb1"][:, ocr:ocr + 1][0:rows, :])
                self.trans(ps[7][:, 0:rows], wk[ocr][0:rows, :], s["c_identf"][0:rows, 0:rows])
                self.v("tensor_scalar_add", w_tm[:, ocr * 128:ocr * 128 + rows], ps[7][:, 0:rows], 0.0)
            self.v("tensor_tensor", wh[:, :].rearrange("p (h s) -> p h s", h=8),
                   w_tm[:, :].rearrange("p (h s) -> p h s", h=8),
                   frac[:, :].unsqueeze(1).broadcast_to([128, 8, 33]), ALU.mult)
            self.v("tensor_sub", wl[:, :], w_tm[:, :], wh[:, :])
            self.act(wl_all[:, bi * 264:(bi + 1) * 264], wl[:, :], AF.Copy)
            self.act(wh_all[:, bi * 264:(bi + 1) * 264], wh[:, :], AF.Copy)

        for nm in ("5g", "5d", "5p", "5e", "5q"):
            self.rsem(nm)
        p0, d0, e0, gg0, q0 = (self.rc[nm] for nm in ("5p", "5d", "5e", "5g", "5q"))
        oc0 = self.ocnt
        NGI = NB * 33
        PARA5 = getattr(KB, "PARA5", False)
        _rop0 = self.rop
        def rop_p5(eng, fn, incs=(), waits=()):
            if PARA5:
                waits = list(waits) + [(nm, self.rc[nm]) for nm in ("5g", "5d", "5p", "5e")]
            _rop0(eng, fn, incs=incs, waits=waits)
        self.rop = rop_p5

        def gath(gidx, w_):
            bi2, si2 = divmod(gidx, 33)
            buf = gidx % NGT
            self.rop("gpsimd",
                     lambda buf=buf, bi2=bi2, si2=si2: self.nc.gpsimd.indirect_dma_start(
                         gt[:, buf * 2048:(buf + 1) * 2048], None,
                         bass.AP(self.dram["xn1_tm"], 0, [[1, 128], [1, 2048]]),
                         IndirectOffsetOnAxis(idx_all[:, bi2 * 33 + si2:bi2 * 33 + si2 + 1], 1)),
                     incs=[("5g", 16)], waits=w_)

        for gidx in range(NGT):
            gath(gidx, [("O", oc0)])
        for bi in range(NB):
            for si in range(33):
                gidx = bi * 33 + si
                if gidx + NGT < NGI:
                    gath(gidx + NGT, [("5p", p0 + gidx + 1)])
                dwait = ([("5p", p0 + gidx - 3)] if gidx >= 4 else [("O", oc0)])
                for lohi in range(2):
                    wall = wl_all if lohi == 0 else wh_all
                    dv = dg[:, (gidx % 4) * 2048 + lohi * 1024:
                            (gidx % 4) * 2048 + (lohi + 1) * 1024]
                    self.rop("vector",
                             lambda dv=dv, wall=wall, bi=bi, si=si:
                             self.nc.vector.tensor_tensor(
                                 dv.rearrange("p (h d) -> p h d", h=8),
                                 s["c_ident"][:, :].unsqueeze(1).broadcast_to([128, 8, 128]),
                                 wall[:, bi * 264:(bi + 1) * 264]
                                 .rearrange("p (h s) -> p h s", h=8)[:, :, si:si + 1]
                                 .broadcast_to([128, 8, 128]), ALU.mult),
                             incs=[("5d", 1)], waits=(dwait if lohi == 0 else []))
                buf = gidx % NGT
                for lohi in range(2):
                    for h8 in range(8):
                        w_ = []
                        if lohi == 0 and h8 == 0:
                            w_ = [("5g", gg0 + 16 * min(gidx + 2, NGI)),
                                  ("5d", d0 + min(2 * gidx + 4, 2 * NGI))]
                            if si == 0:
                                w_.append(("5e", e0 + 8 * bi))
                                if bi == 0:
                                    w_.append(("O", oc0))
                        incs_ = []
                        if lohi == 1 and h8 == 7:
                            incs_.append(("5p", 1))
                        elif lohi == 1 and si == 32:
                            incs_.append(("5q", 1))
                        self.rop("tensor",
                                 lambda buf=buf, lohi=lohi, h8=h8, gidx=gidx:
                                 self.nc.tensor.matmul(
                                     ps[h8][:, 0:128],
                                     gt[:, buf * 2048 + lohi * D + h8 * 128:
                                        buf * 2048 + lohi * D + (h8 + 1) * 128],
                                     dg[:, (gidx % 4) * 2048 + lohi * 1024 + h8 * 128:
                                        (gidx % 4) * 2048 + lohi * 1024 + (h8 + 1) * 128],
                                     start=(gidx % 33 == 0 and lohi == 0),
                                     stop=(lohi == 1)),
                                 incs=incs_, waits=w_)
            for h8 in range(8):
                self.rop("scalar",
                         lambda h8=h8, bi=bi: self.nc.scalar.activation(
                             out=ohfm_all[:, (bi * 8 + h8) * 128:(bi * 8 + h8 + 1) * 128],
                             in_=ps[h8][:, 0:128], func=AF.Copy),
                         incs=[("5e", 1)],
                         waits=([("5q", q0 + 7 * bi + h8 + 1)] if h8 < 7
                                else [("5p", p0 + 33 * (bi + 1))]))
        self.rop = _rop0
        self.rjoin([("5g", self.rc["5g"]), ("5d", self.rc["5d"]),
                    ("5p", self.rc["5p"]), ("5e", self.rc["5e"]),
                    ("5q", self.rc["5q"])])
        if "dbg4" in self.dbg:
            dbg = self.d("dbg4")
            self.dma("gpsimd", dbg[:, 0:1024], ohfm_all[:, 0:1024])
            self.v("tensor_copy", wl[:, :], wl_all[:, 0:264])
            self.dma("gpsimd", dbg[:, 1024:1288], wl[:, :])
            self.v("tensor_copy", wh[:, 0:33], idx_all[:, 0:33])
            self.dma("gpsimd", dbg[:, 1288:1321], wh[:, 0:33])
            self.dma("gpsimd", dbg[:, 1321:2345], dg[:, 0:1024])
            self.dma("gpsimd", dbg[:, 2345:3369], gt[:, 0:1024])
            self.v("tensor_copy", wh[:, :], wh_all[:, 0:264])
            self.dma("gpsimd", dbg[:, 3369:3633], wh[:, :])

        items = []
        for bi in range(NB):
            for oc in range(8):
                def mk(bi=bi, oc=oc):
                    def mms(bank):
                        return [(lambda kc=kc, bank=bank: self.nc.tensor.matmul(
                            bank[:, 0:128],
                            w_out[:, kc * D + oc * 128: kc * D + oc * 128 + 128],
                            ohfm_all[:, (bi * 8 + kc) * 128:(bi * 8 + kc + 1) * 128],
                            start=(kc == 0), stop=(kc == 7))) for kc in range(8)]
                    it = len(items)
                    sl = s1t[:, (it % 4) * 128:(it % 4 + 1) * 128]
                    return {
                        "mms": mms,
                        "evac": lambda bank, sl=sl: (lambda: self.nc.scalar.activation(
                            out=sl, in_=bank[:, 0:128], func=AF.Silu)),
                        "store": lambda sl=sl, bi=bi, oc=oc: self.nc.gpsimd.dma_start(
                            out=self.d("s1_fm")[oc * 128:(oc + 1) * 128, bi * 128:(bi + 1) * 128],
                            in_=sl),
                    }
                items.append(mk())
        self.mmpipe(items, nbank=2, bank0=0, width=128)

    def s6_moe(self):
        s = self.sb
        ps = self.ps
        o = 0
        def nxt(name, shape, dtype):
            nonlocal o
            h = self.ar(name, shape, dtype, o)
            o += (int(np.prod(shape[1:])) * DTB[dtype] + 511) // 512 * 512
            return h
        w_wg = nxt("s6_wg", [128, 8 * 8], BF16)
        xnb = nxt("s6_xnb", [128, 8 * 256], BF16)
        lgf = nxt("s6_lgf", [8, 256], F32)
        lg_tm = nxt("s6_lgtm", [128, 2 * 8], F32)
        mx = nxt("s6_mx", [128, 2], F32)
        mi = nxt("s6_mi", [128, 2 * 8], I32)
        m_tm = nxt("s6_mtm", [128, 2 * 8], F32)
        me = nxt("s6_me", [1, 8 * 256], F32)
        xe = nxt("s6_xe", [128, 8 * 256], BF16)
        for kc in range(8):
            self.dma("gpsimd", w_wg[:, kc * 8:(kc + 1) * 8], self.d("l1f_wg")[kc * 128:(kc + 1) * 128, :])
        NT = 256
        for tt in range(NOUT // NT):
            w0 = OUT0 + tt * NT
            for kc in range(8):
                self.dma("sync", xnb[:, kc * NT:(kc + 1) * NT],
                         self.d("xn1_fm")[kc * 128:(kc + 1) * 128, w0:w0 + NT])
            for kc in range(8):
                self.mm(ps[0][0:8, 0:NT], w_wg[:, kc * 8:(kc + 1) * 8],
                        xnb[:, kc * NT:(kc + 1) * NT], start=(kc == 0), stop=(kc == 7))
            self.v("tensor_scalar", lgf[:, :], ps[0][0:8, 0:NT], s["c_bg"][:, 0:1], None, ALU.add)
            for tb in range(2):
                self.trans(ps[1][:, 0:8], lgf[:, tb * 128:(tb + 1) * 128], s["c_identf"][0:8, 0:8])
                self.v("tensor_scalar_add", lg_tm[:, tb * 8:(tb + 1) * 8], ps[1][:, 0:8], 0.0)
                self.v("reduce_max", mx[:, tb:tb + 1], lg_tm[:, tb * 8:(tb + 1) * 8], AX.X)
                self.v("tensor_tensor", mi[:, tb * 8:(tb + 1) * 8], lg_tm[:, tb * 8:(tb + 1) * 8],
                       mx[:, tb:tb + 1].broadcast_to([128, 8]), ALU.is_ge)
                self.v("tensor_copy", m_tm[:, tb * 8:(tb + 1) * 8], mi[:, tb * 8:(tb + 1) * 8])
                for e in range(8):
                    self.trans(ps[2][0:1, 0:128], m_tm[:, tb * 8 + e: tb * 8 + e + 1], s["c_identf"][:, :])
                    self.v("tensor_scalar_add", me[0:1, e * NT + tb * 128: e * NT + (tb + 1) * 128],
                           ps[2][0:1, 0:128], 0.0)
            for e in range(8):
                self.mm(ps[3][:, 0:NT], s["c_onesf"][0:1, 0:128], me[0:1, e * NT:(e + 1) * NT])
                for kc in range(8):
                    self.v("tensor_tensor", xe[:, kc * NT:(kc + 1) * NT],
                           xnb[:, kc * NT:(kc + 1) * NT], ps[3][:, 0:NT], ALU.mult)
                for kc in range(8):
                    self.dma("gpsimd", self.d("xt_dram")[(e * 8 + kc) * 128:(e * 8 + kc + 1) * 128,
                                                         tt * NT:(tt + 1) * NT],
                             xe[:, kc * NT:(kc + 1) * NT])
        o = 0
        w1b = nxt("s6_w1b", [128, 2 * 8 * 128], BF16)
        xeb = nxt("s6_xeb", [128, 2 * 8 * NOUT], BF16)
        htb = nxt("s6_htb", [128, 2 * 512], BF16)
        for nm in ("bWL", "bXE", "bMM", "bGE", "bST"):
            self.rsem(nm)
        oc_e = self.ocnt
        mm_hist = {}
        ge_hist = {}
        wl_cnt = 0
        xe_cnt = 0
        gidx = 0

        def wload(e, hc):
            nonlocal wl_cnt
            slot = (e * 32 + hc) % 2
            w_ = [("O", oc_e)] if (e * 32 + hc) < 2 else [("bMM", (e * 32 + hc - 1) * 4)]
            self.rop("gpsimd", lambda slot=slot, e=e, hc=hc: self.nc.gpsimd.dma_start(
                out=w1b[:, slot * 1024:(slot + 1) * 1024].rearrange("p (kc h) -> p kc h", kc=8),
                in_=self.d("l1f_w1")[e * D:(e + 1) * D, hc * 128:(hc + 1) * 128]
                .rearrange("(kc p) h -> p kc h", p=128)),
                incs=[("bWL", 16)], waits=w_)
            wl_cnt += 16

        def xload(e):
            nonlocal xe_cnt
            slot = e % 2
            w_ = [("O", oc_e)] if e < 2 else [("bMM", (e - 1) * 32 * 4)]
            for kc in range(8):
                self.rop("sync", lambda slot=slot, e=e, kc=kc: self.nc.sync.dma_start(
                    out=xeb[:, slot * 8 * NOUT + kc * NOUT: slot * 8 * NOUT + (kc + 1) * NOUT],
                    in_=self.d("xt_dram")[(e * 8 + kc) * 128:(e * 8 + kc + 1) * 128, :]),
                    incs=[("bXE", 16)], waits=w_ if kc == 0 else [])
                xe_cnt += 16

        xload(0); wload(0, 0)
        xload(1); wload(0, 1)
        for e in range(8):
            if e < 7:
                pass
            for hc in range(32):
                gi = e * 32 + hc
                slot = gi % 2
                es = e % 2
                for t4 in range(4):
                    bank = ps[t4 % 2]
                    bkey = t4 % 2
                    w_ = []
                    if t4 == 0:
                        w_ = [("bWL", (gi + 1) * 16), ("bXE", (e + 1) * 8 * 16 if e < 8 else 0)]
                        if e >= 2:
                            w_[1] = ("bXE", min((e + 1), 8) * 8 * 16)
                    if bkey in ge_hist:
                        w_.append(("bGE", ge_hist[bkey]))
                    for kc in range(8):
                        self.rop("tensor", lambda bank=bank, slot=slot, es=es, kc=kc, t4=t4, e=e:
                                 self.nc.tensor.matmul(
                                     bank[:, 0:512], w1b[:, slot * 1024 + kc * 128: slot * 1024 + (kc + 1) * 128],
                                     xeb[:, es * 8 * NOUT + kc * NOUT + t4 * 512: es * 8 * NOUT + kc * NOUT + t4 * 512 + 512],
                                     start=(kc == 0), stop=(kc == 7)),
                                 incs=[("bMM", 1)] if kc == 7 else [],
                                 waits=w_ if kc == 0 else [])
                    hslot = (gi * 4 + t4) % 2
                    hw = [("bMM", gi * 4 + t4 + 1)]
                    if gi * 4 + t4 >= 2:
                        hw.append(("bST", (gi * 4 + t4 - 1) * 16))
                    self.rop("scalar", lambda hslot=hslot, bank=bank: self.nc.scalar.activation(
                        out=htb[:, hslot * 512:(hslot + 1) * 512], in_=bank[:, 0:512],
                        func=AF.Gelu_apprx_tanh), incs=[("bGE", 1)], waits=hw)
                    ge_hist[bkey] = self.rc["bGE"]
                    self.rop("gpsimd", lambda e=e, hc=hc, t4=t4, hslot=hslot: self.nc.gpsimd.dma_start(
                        out=self.d("ht_dram")[(e * 32 + hc) * 128:(e * 32 + hc + 1) * 128,
                                              t4 * 512:(t4 + 1) * 512],
                        in_=htb[:, hslot * 512:(hslot + 1) * 512]),
                        incs=[("bST", 16)], waits=[("bGE", self.rc["bGE"])])
                if gi + 2 < 256:
                    wload((gi + 2) // 32, (gi + 2) % 32)
            if e + 2 < 8:
                xload(e + 2)
        self.rjoin([("bST", self.rc["bST"]), ("bMM", self.rc["bMM"])])
        o = 0
        g1acc = nxt("s6_acc", [128, 8 * NOUT], F32)
        w2b = nxt("s6_w2b", [128, 2 * 16 * D], BF16)
        O_X2 = o
        htt = nxt("s6_htt", [128, 2 * 16 * 512], BF16)
        for nm in ("cWL", "cHT", "cMC", "cAD"):
            self.rsem(nm)
        oc_e = self.ocnt

        def w2load(e, half):
            idx = e * 2 + half
            w_ = [("O", oc_e)] if idx < 2 else [("cMC", 32 * e)]
            for i in range(16):
                hc = half * 16 + i
                self.rop("gpsimd", lambda e=e, half=half, hc=hc, i=i: self.nc.gpsimd.dma_start(
                    out=w2b[:, (half * 16 + i) * D:(half * 16 + i + 1) * D],
                    in_=self.d("l1f_w2")[(e * 32 + hc) * 128:(e * 32 + hc + 1) * 128, :]),
                    incs=[("cWL", 16)], waits=w_ if i == 0 else [])

        def htload(e, t4, half):
            idx = (e * 4 + t4) * 2 + half
            w_ = [("O", oc_e)] if idx < 2 else [("cMC", 8 * ((idx - 2) // 2 + 1))]
            self.rop("sync", lambda e=e, t4=t4, half=half: self.nc.sync.dma_start(
                out=htt[:, half * 16 * 512:(half + 1) * 16 * 512].rearrange("p (hc t) -> p hc t", hc=16),
                in_=self.d("ht_dram").rearrange("(g p) t -> g p t", p=128)
                [e * 32 + half * 16:e * 32 + (half + 1) * 16, :, t4 * 512:(t4 + 1) * 512]
                .transpose([1, 0, 2])),
                incs=[("cHT", 16)], waits=w_)

        w2load(0, 0); htload(0, 0, 0)
        w2load(0, 1); htload(0, 0, 1)
        ad_hist = {}
        for e in range(8):
            for t4 in range(4):
                for half in range(2):
                    hidx = (e * 4 + t4) * 2 + half
                    for oc in range(8):
                        bank = ps[oc]
                        w_ = []
                        if oc == 0:
                            w_ = [("cWL", (e * 2 + half + 1) * 256), ("cHT", (hidx + 1) * 16)]
                        if half == 0 and oc in ad_hist:
                            w_.append(("cAD", ad_hist[oc]))
                        for i in range(16):
                            self.rop("tensor", lambda bank=bank, half=half, i=i, oc=oc:
                                     self.nc.tensor.matmul(
                                         bank[:, 0:512],
                                         w2b[:, (half * 16 + i) * D + oc * 128: (half * 16 + i) * D + oc * 128 + 128],
                                         htt[:, half * 16 * 512 + i * 512: half * 16 * 512 + (i + 1) * 512],
                                         start=(half == 0 and i == 0 and True), stop=(half == 1 and i == 15)),
                                     incs=[("cMC", 1)] if (half == 1 and i == 15) else [],
                                     waits=w_ if i == 0 else [])
                    if hidx + 2 < 64:
                        ne, nt4, nh = (hidx + 2) // 8, ((hidx + 2) % 8) // 2, (hidx + 2) % 2
                        htload(ne, nt4, nh)
                    widx = e * 2 + half
                    if t4 == 3 and widx + 2 < 16:
                        w2load((widx + 2) // 2, (widx + 2) % 2)
                for oc in range(8):
                    bank = ps[oc]
                    dst = g1acc[:, oc * NOUT + t4 * 512: oc * NOUT + (t4 + 1) * 512]
                    w_ = [("cMC", (e * 4 + t4) * 8 + oc + 1)]
                    if e == 0:
                        self.rop("vector", lambda dst=dst, bank=bank:
                                 self.nc.vector.tensor_scalar_add(dst, bank[:, 0:512], 0.0),
                                 incs=[("cAD", 1)], waits=w_)
                    else:
                        self.rop("vector", lambda dst=dst, bank=bank:
                                 self.nc.vector.tensor_tensor(dst, dst, bank[:, 0:512], ALU.add),
                                 incs=[("cAD", 1)], waits=w_)
                    ad_hist[oc] = self.rc["cAD"]
        self.rjoin([("cAD", self.rc["cAD"])])
        if "dbg4" in self.dbg:
            for t4 in range(4):
                self.dma("gpsimd", self.d("dbg4")[:, t4 * 512:(t4 + 1) * 512],
                         g1acc[:, 0 * NOUT + t4 * 512: 0 * NOUT + (t4 + 1) * 512])
        o = O_X2
        s1l = nxt("s6_s1l", [128, 512], F32)
        x1l = nxt("s6_x1l", [128, 512], F32)
        g1s = nxt("s6_g1s", [128, 512], F32)
        x2t = nxt("s6_x2t", [128, 512], F32)
        for t4 in range(4):
            for c in range(8):
                self.dma("sync", s1l[:, :], self.d("s1_fm")[c * 128:(c + 1) * 128, t4 * 512:(t4 + 1) * 512])
                self.dma("sync", x1l[:, :], self.d("x1_fm")[c * 128:(c + 1) * 128,
                                                            OUT0 + t4 * 512:OUT0 + (t4 + 1) * 512])
                self.act(g1s[:, :], g1acc[:, c * NOUT + t4 * 512: c * NOUT + (t4 + 1) * 512], AF.Silu)
                self.v("tensor_mul", g1s[:, :], g1s[:, :], s1l[:, :])
                self.v("tensor_add", x2t[:, :], x1l[:, :], g1s[:, :])
                self.dma("gpsimd", self.d("x2_fm")[c * 128:(c + 1) * 128, t4 * 512:(t4 + 1) * 512], x2t[:, :])


def _consts():
    ident = np.eye(128, dtype=np.float32)
    r = np.arange(128)
    maskAl = np.where(r[None, :] < r[:, None], -1.0, 0.0).astype(np.float32)
    maskAu = np.where(r[None, :] > r[:, None], -1.0, 0.0).astype(np.float32)
    maskIncT = np.where(r[:, None] <= r[None, :], 1.0, 0.0).astype(np.float32)
    ind49 = np.zeros((128, 49), np.float32)
    ind49[:, 24] = 1.0
    sel4 = np.zeros((4, 512), np.float32)
    for h in range(4):
        sel4[h, h * 128:(h + 1) * 128] = 1.0
    hsel = np.zeros((24, 4), np.float32)
    for h in range(4):
        hsel[h * 6:(h + 1) * 6, h] = 1.0
    rsel24 = np.zeros((24, 24 * 128), np.float32)
    for rr in range(24):
        rsel24[rr, rr * 128:(rr + 1) * 128] = 1.0
    e16 = np.zeros((8, 128), np.float32)
    for e in range(8):
        e16[e, e * 16:(e + 1) * 16] = 1.0
    return ident, maskAl, maskAu, maskIncT, ind49, sel4, hsel, rsel24, e16


def make_inputs(full, core):
    b, j = core // 2, core % 2
    x = np.asarray(full["L0_attn_hidden_states"][b], np.float32)
    ident, maskAl, maskAu, maskIncT, ind49, sel4, hsel, rsel24, e16 = _consts()
    xf = np.zeros((D, LPAD), np.float32)
    xf[:, ROW0:ROW0 + L] = x.T
    g = lambda n: np.asarray(full[n], np.float32)
    w_qkvb = np.concatenate([g("L0_attn_q_w"), g("L0_attn_k_w"), g("L0_attn_v_w"),
                             g("L0_attn_b_w")], axis=1)
    convw = np.concatenate([g("L0_attn_qc_w")[:, 0, :], g("L0_attn_kc_w")[:, 0, :],
                            g("L0_attn_vc_w")[:, 0, :]], axis=1)
    firw = np.zeros((D, 42), np.float32)
    firw[:, 0:1] = g("L0_attn_fir_w1")[:, 0, :]
    firw[:, 1:4] = g("L0_attn_fir_w3")[:, 0, :]
    firw[:, 4:11] = g("L0_attn_fir_w7")[:, 0, :]
    firw[:, 11:42] = g("L0_attn_fir_w31")[:, 0, :]
    w1 = g("L0_attn_mlp_w1")
    perm = np.arange(1120)
    for kind in range(4):
        for br in range(6):
            for h in range(4):
                perm[1024 + kind * 24 + br * 4 + h] = 1024 + h * 24 + br * 4 + kind
    w1p = np.ascontiguousarray(w1[perm])
    kb264 = g("L1_attn_kernel_b").reshape(8, 64)[:, :33].reshape(264)
    kbp = np.zeros((128, 3), np.float32)
    kbp.T.flat[0:264] = kb264
    W0g = j * 2048 - PAD
    offs = np.zeros((1, 16), np.int32)
    p = np.arange(128)
    idxA = ((p[:, None] + np.arange(8)[None, :] * 128) * LPAD + j * 2048).astype(np.int32)
    idxB = ((j * 2048 + p) * D).astype(np.int32).reshape(128, 1)
    d = {
        "x_tm": x, "x_fm": xf, "w_qkvb": w_qkvb, "convw": convw, "firw": firw,
        "mlp_w1p": w1p, "mlp_b1c": g("L0_attn_mlp_b1").reshape(8, 128).T.copy(),
        "mlp_w2": g("L0_attn_mlp_w2"), "mlp_b2": g("L0_attn_mlp_b2").reshape(24, 1),
        "glt": g("L0_attn_gate_log_temp").reshape(4, 1),
        "onormc": g("L0_attn_onorm_w").reshape(2, 128).T.copy(),
        "o_wT": g("L0_attn_o_w"),
        "l0f_wT": g("L0_ffn_weight").T.copy(),
        "l0f_gw": g("L0_ffn_gate_w").T.copy(),
        "l0f_U": g("L0_ffn_lora_U").reshape(128, D).T.copy(),
        "l0f_V": g("L0_ffn_lora_V").transpose(0, 2, 1).reshape(128, D).copy(),
        "l1a_wavew": g("L1_attn_wave_w").T.copy(),
        "l1a_wavebf": g("L1_attn_wave_b")[0:8].reshape(8, 1),
        "l1a_wavebp": g("L1_attn_wave_b")[8:16].reshape(8, 1),
        "l1a_kw": g("L1_attn_kernel_w").reshape(8, 64, D)[:, :33].reshape(264, D).T.copy(),
        "l1a_kb": kbp,
        "l1a_outwT": g("L1_attn_out_w").T.copy(),
        "l1f_w1": g("L1_ffn_W1").reshape(8 * D, 4096),
        "l1f_w2": g("L1_ffn_W2").reshape(8 * 4096, D),
        "l1f_wg": g("L1_ffn_Wg"), "l1f_bg": g("L1_ffn_bg").reshape(8, 1),
        "ident": ident, "maskAl": maskAl, "maskAu": maskAu, "maskIncT": maskIncT,
        "ind49": ind49, "sel4": sel4, "sel4f": sel4, "hsel": hsel,
        "hselT": np.ascontiguousarray(hsel.T), "rsel24": rsel24, "e16": e16,
        "iotaG": (W0g + np.arange(WIN, dtype=np.float32))[None, :],
        "sidx8": ((np.arange(33, dtype=np.float32) - 16) / 8.0)[None, :],
        "w0col": np.full((33, 1), W0g, np.float32), "offs": offs,
        "idxA": idxA, "idxB": idxB,
    }
    return {k: np.ascontiguousarray(v) for k, v in d.items()}


def build_kb(stage_limit=99, dbg=()):
    kb = KB(stage_limit=stage_limit, dbg=dbg)
    nc = kb.build()
    return kb, nc


def kernel(**inputs):
    kb, nc = build_kb()
    in_maps = [make_inputs(inputs, c) for c in range(8)]
    res = run_bass_kernel_spmd(nc, in_maps, core_ids=list(range(8)))
    out = np.zeros((B, L, D), np.float32)
    for c in range(8):
        b, j = c // 2, c % 2
        out[b, j * 2048:(j + 1) * 2048, :] = res.results[c]["x2_fm"].T
    return out


# version 13 (final)
# speedup vs baseline: 1.2524x; 1.0056x over previous
import numpy as np
import ml_dtypes
from contextlib import ExitStack

import concourse.bass as bass
import concourse.mybir as mybir
from concourse.bass_core_rust import IndirectOffsetOnAxis
from concourse.bass_utils import run_bass_kernel_spmd

F32 = mybir.dt.float32
BF16 = mybir.dt.bfloat16
I32 = mybir.dt.int32
AF = mybir.ActivationFunctionType
ALU = mybir.AluOpType
AX = mybir.AxisListType

B, L, D = 4, 4096, 1024
DK = 256
CH = 128
NCHUNK = L // CH
PAD = 384
ROW0 = PAD
WIN = 2 * PAD + 2048
LPAD = L + 2 * PAD
TT = 512
ST4 = 256
NST4 = WIN // ST4
OUT0 = PAD
NOUT = 2048

DTB = {F32: 4, I32: 4, BF16: 2}


class KB:
    def __init__(self, stage_limit=99, dbg=()):
        self.stage_limit = stage_limit
        self.dbg = set(dbg)
        self.nc = bass.Bass("TRN2", target_bir_lowering=False)
        self.sb = {}
        self.dram = {}
        self.eng = None
        self.sbuf_off = 0
        self.regs = {}
        self.rsems = {}
        self.rc = {}
        self.ocnt = 0
        self.cur_eng = None
        self.pend = 0
        self.last_instr = None
        self._build_tensors()

    def sbuf(self, name, shape, dtype, at=None):
        if name in self.sb:
            return self.sb[name]
        base = (self.nc.sbuf_base + 511) // 512 * 512
        nbytes = int(np.prod(shape[1:])) * DTB[dtype]
        off = self.sbuf_off if at is None else at
        off = (off + 511) // 512 * 512
        if off < base:
            off = base
        h = self.nc.alloc_sbuf_tensor_at(name, list(shape), dtype, offset=off)
        if at is None:
            self.sbuf_off = off + nbytes
        assert off + nbytes <= 229300, f"SBUF overflow {name}: {off + nbytes}"
        self.sb[name] = h
        return h

    def din(self, name, shape, dtype):
        self.dram[name] = self.nc.dram_tensor(name, list(shape), dtype, kind="ExternalInput")

    def dout(self, name, shape, dtype):
        self.dram[name] = self.nc.dram_tensor(name, list(shape), dtype, kind="ExternalOutput")

    def dscratch(self, name, shape, dtype):
        kind = "ExternalOutput" if name in self.dbg else "Internal"
        self.dram[name] = self.nc.dram_tensor(name, list(shape), dtype, kind=kind)

    def d(self, name):
        return self.dram[name].ap()

    def on(self, e):
        return self.eng == e

    def _flush(self):
        if self.pend and self.last_instr is not None:
            self.last_instr.then_inc(self.osem, self.pend)
        if self.pend:
            self.ocnt += self.pend
        self.pend = 0
        self.last_instr = None

    def _pre(self, eng, is_dma=False):
        if eng != self.cur_eng:
            self._flush()
            self.cur_eng = eng
            if self.on(eng):
                getattr(self.nc, eng).wait_ge(self.osem, self.ocnt)
        elif is_dma and self.pend:
            pass

    def op(self, eng, fn):
        if eng != self.cur_eng:
            self.cur_eng = eng
            if self.on(eng):
                getattr(self.nc, eng).wait_ge(self.osem, self.ocnt)
        elif eng == "vector" and self.on(eng):
            getattr(self.nc, eng).wait_ge(self.osem, self.ocnt)
        if self.on(eng):
            fn().then_inc(self.osem, 1)
        self.ocnt += 1

    def dmaop(self, q, fn):
        self.cur_eng = q
        if self.on(q):
            getattr(self.nc, q).wait_ge(self.osem, self.ocnt)
            fn().then_inc(self.osem, 16)
        self.ocnt += 16

    def dma(self, q, out, in_, **kw):
        self.dmaop(q, lambda: getattr(self.nc, q).dma_start(out=out, in_=in_, **kw))

    def dmaT(self, q, out, in_):
        self.dmaop(q, lambda: getattr(self.nc, q).dma_start(out=out, in_=in_, transpose=True))

    def idma(self, out, in_, in_off, **kw):
        self.dmaop("gpsimd", lambda: self.nc.gpsimd.indirect_dma_start(
            out, None, in_, in_off, **kw))

    def mm(self, out, lhsT, rhs, start=True, stop=True, tp=None):
        self.op("tensor", lambda: self.nc.tensor.matmul(
            out, lhsT, rhs, start=start, stop=stop, tile_position=tp))

    def trans(self, out, in_, ident):
        self.op("tensor", lambda: self.nc.tensor.transpose(out, in_, ident))

    def act(self, out, in_, func, bias=0.0, scale=1.0):
        self.op("scalar", lambda: self.nc.scalar.activation(
            out=out, in_=in_, func=func, bias=bias, scale=scale))

    def v(self, meth, *a, eng="vector", **kw):
        self.op(eng, lambda: getattr(getattr(self.nc, eng), meth)(*a, **kw))

    def memset(self, ap, val, eng="vector"):
        self.op(eng, lambda: getattr(self.nc, eng).memset(ap, val))

    def dyndma(self, q, tensor_name, const_off, pattern, out, slot=0, load=False):
        def fn():
            e = getattr(self.nc, q)
            rt = self._dynoff(q, const_off, slot)
            ap = bass.AP(self.dram[tensor_name], rt, pattern)
            try:
                if load:
                    return e.dma_start(out=out, in_=ap)
                return e.dma_start(out=ap, in_=out)
            except ValueError as ex:
                import traceback as _tb
                _tb.print_exc()
                raise RuntimeError(f"dyndma {tensor_name} slot={slot} const={const_off} pat={pattern}: {ex}")
        self.dmaop(q, fn)

    def rsem(self, name):
        if name not in self.rsems:
            self.rsems[name] = self.nc.alloc_semaphore(name)
            self.rc[name] = 0
        return self.rsems[name]

    def rop(self, eng, fn, incs=(), waits=()):
        if self.on(eng):
            e = getattr(self.nc, eng)
            for nm, v in waits:
                e.wait_ge(self.osem if nm == "O" else self.rsems[nm], v)
            i = fn()
            for nm, n in incs:
                i.then_inc(self.rsems[nm], n)
        for nm, n in incs:
            self.rc[nm] += n

    def rwait(self, eng, waits):
        if self.on(eng):
            e = getattr(self.nc, eng)
            for nm, v in waits:
                e.wait_ge(self.osem if nm == "O" else self.rsems[nm], v)

    def rjoin(self, waits):
        for e in ("sync", "vector", "tensor", "scalar", "gpsimd"):
            self.rwait(e, waits)
        self.cur_eng = None

    def mmpipe(self, items, nbank=2, bank0=0, width=512):
        self.rsem("pM"); self.rsem("pE"); self.rsem("pS")
        m0, e0, s0 = self.rc["pM"], self.rc["pE"], self.rc["pS"]
        oc_e = self.ocnt
        evac_of_bank = {}
        has_store = any(it.get("store") for it in items)
        for i, it in enumerate(items):
            b = i % nbank
            bank = self.ps[bank0 + b][:, 0:width]
            w_ = [("O", oc_e)] if i < nbank else [("pE", evac_of_bank[b])]
            fns = it["mms"](bank)
            for k, f in enumerate(fns):
                self.rop("tensor", f,
                         incs=[("pM", 1)] if k == len(fns) - 1 else [],
                         waits=w_ if k == 0 else [])
            ew = [("pM", m0 + i + 1)]
            if i == 0:
                ew.append(("O", oc_e))
            if it.get("store") and i >= 2 * nbank:
                ew.append(("pS", self.rc["pS"] - 2 * 16))
            self.rop("scalar", it["evac"](bank), incs=[("pE", 1)], waits=ew)
            evac_of_bank[b] = self.rc["pE"]
            if it.get("store"):
                sw = [("pE", self.rc["pE"])]
                if i == 0:
                    sw.append(("O", oc_e))
                self.rop("gpsimd", it["store"], incs=[("pS", 16)], waits=sw)
        self.rjoin([("pM", self.rc["pM"]), ("pE", self.rc["pE"])] +
                   ([("pS", self.rc["pS"])] if has_store else []))

    def windma(self, tensor_name, n, out, idxcol):
        def fn():
            ap = bass.AP(self.dram[tensor_name], 0, [[1, 128], [1, n]])
            return self.nc.gpsimd.indirect_dma_start(
                out, None, ap, IndirectOffsetOnAxis(idxcol, 1))
        self.dmaop("gpsimd", fn)

    def setup_dynregs(self, q, slot=0):
        pass

    def _dynoff(self, q, const_off, slot):
        e = getattr(self.nc, q)
        if (q, "base") not in self.regs:
            self.regs[(q, "base")] = e.alloc_register(f"rb_{q}")
            self.regs[(q, "tmp")] = e.alloc_register(f"rt_{q}")
        rb, rt = self.regs[(q, "base")], self.regs[(q, "tmp")]
        e.wait_ge(self.osem, self.ocnt)
        e.reg_load(rb, self.sb["c_offs"][0:1, slot:slot + 1])
        e.reg_add(rt, rb, const_off)
        return rt

    def _build_tensors(self):
        self.din("x_tm", [L, D], F32)
        self.din("x_fm", [D, LPAD], F32)
        self.din("w_qkvb", [D, 3 * D + 4], F32)
        self.din("convw", [D, 12], F32)
        self.din("firw", [D, 42], F32)
        self.din("mlp_w1p", [1120, D], F32)
        self.din("mlp_b1c", [128, 8], F32)
        self.din("mlp_w2", [D, 24], F32)
        self.din("mlp_b2", [24, 1], F32)
        self.din("glt", [4, 1], F32)
        self.din("onormc", [128, 2], F32)
        self.din("o_wT", [D, D], F32)
        self.din("l0f_wT", [D, D], F32)
        self.din("l0f_gw", [D, 8], F32)
        self.din("l0f_U", [D, 128], F32)
        self.din("l0f_V", [128, D], F32)
        self.din("l1a_wavew", [D, 16], F32)
        self.din("l1a_wavebf", [8, 1], F32)
        self.din("l1a_wavebp", [8, 1], F32)
        self.din("l1a_kw", [D, 264], F32)
        self.din("l1a_kb", [128, 3], F32)
        self.din("l1a_outwT", [D, D], F32)
        self.din("l1f_w1", [8 * D, 4096], F32)
        self.din("l1f_w2", [8 * 4096, D], F32)
        self.din("l1f_wg", [D, 8], F32)
        self.din("l1f_bg", [8, 1], F32)
        self.din("ident", [128, 128], F32)
        self.din("maskAl", [128, 128], F32)
        self.din("maskAu", [128, 128], F32)
        self.din("maskIncT", [128, 128], F32)
        self.din("ind49", [128, 49], F32)
        self.din("sel4", [4, 512], F32)
        self.din("sel4f", [4, 512], F32)
        self.din("hsel", [24, 4], F32)
        self.din("hselT", [4, 24], F32)
        self.din("rsel24", [24, 24 * 128], F32)
        self.din("e16", [8, 128], F32)
        self.din("iotaG", [1, WIN], F32)
        self.din("sidx8", [1, 33], F32)
        self.din("w0col", [33, 1], F32)
        self.din("offs", [1, 16], I32)
        self.din("idxA", [128, 8], I32)
        self.din("idxB", [128, 1], I32)

        self.dscratch("xn_fm", [D, LPAD], BF16)
        self.dscratch("xn_tmp", [L, D], BF16)
        self.dscratch("q_fm", [D, L], BF16)
        self.dscratch("khat_fm", [D, L], BF16)
        self.dscratch("kb_fm", [D, L], BF16)
        self.dscratch("vb_fm", [D, L], BF16)
        self.dscratch("v_fm", [D, LPAD], F32)
        self.dscratch("delta_tm", [LPAD, D], BF16)
        self.dscratch("fir_dram", [4 * D, WIN], BF16)
        self.dscratch("x1_fm", [D, WIN], F32)
        self.dscratch("xn1_fm", [D, WIN], BF16)
        self.dscratch("xn1_tm", [WIN + 128, D], BF16)
        self.dscratch("s1_fm", [D, NOUT], F32)
        self.dscratch("xt_dram", [8 * D, NOUT], BF16)
        self.dscratch("ht_dram", [8 * 4096, NOUT], BF16)
        self.dscratch("invt_d", [4, 1], F32)
        self.dscratch("dbg3", [128, 4096], F32)
        self.dscratch("dbg4", [128, 4096], F32)
        self.dscratch("dbg5", [128, 16], I32)
        self.dout("x2_fm", [D, NOUT], F32)

        self.sbuf("c_nop", [1, 2], F32)
        self.sbuf("c_ident", [128, 128], BF16)
        self.sbuf("c_identf", [128, 128], F32)
        self.sbuf("c_maskAl", [128, 128], F32)
        self.sbuf("c_maskAu", [128, 128], F32)
        self.sbuf("c_maskIncT", [128, 128], F32)
        self.sbuf("c_ind49", [128, 49], BF16)
        self.sbuf("c_ind49f", [128, 49], F32)
        self.sbuf("c_sel4", [4, 512], BF16)
        self.sbuf("c_sel4f", [4, 512], F32)
        self.sbuf("c_hsel", [24, 4], F32)
        self.sbuf("c_hselT", [4, 24], F32)
        self.sbuf("c_e16", [8, 128], F32)
        self.sbuf("c_iotaG", [1, WIN], F32)
        self.sbuf("c_sidx8", [1, 33], F32)
        self.sbuf("c_w0col", [33, 1], F32)
        self.sbuf("c_offs", [1, 16], I32)
        self.sbuf("c_idxA", [128, 8], I32)
        self.sbuf("c_idxB", [128, 1], I32)
        self.sbuf("c_idxS", [128, 8], I32)
        self.sbuf("c_idxD", [128, 1], I32)
        self.sbuf("c_ones", [1, 512], BF16)
        self.sbuf("c_onesf", [1, 512], F32)
        self.sbuf("c_onescol", [128, 1], F32)
        self.sbuf("c_zero", [128, 512], F32)
        self.sbuf("rstdq_tm", [128, NCHUNK * 4], F32)
        self.sbuf("c_b1", [128, 8], F32)
        self.sbuf("c_b2", [24, 1], F32)
        self.sbuf("c_invt", [24, 1], F32)
        self.sbuf("c_b2i", [24, 1], F32)
        self.sbuf("c_onorm", [128, 2], F32)
        self.sbuf("c_wavebf", [8, 1], F32)
        self.sbuf("c_wavebp", [8, 1], F32)
        self.sbuf("c_kb1", [128, 3], F32)
        self.sbuf("c_bg", [8, 1], F32)
        self.sbuf("c_glt", [4, 1], F32)
        self.sbuf("c_sp", [4, 4], F32)
        self.static_top = ((self.sbuf_off + 511) // 512 * 512)

    def ar(self, name, shape, dtype, off):
        return self.sbuf(name, shape, dtype, at=self.static_top + off)

    def build(self):
        nc = self.nc
        with ExitStack() as ctx:
            E = ctx.enter_context
            self.ps = [E(nc.psum_tensor(f"ps{i}", [128, 512], F32)) for i in range(8)]
            self.osem = E(nc.semaphore("O"))
            blk = E(nc.Block())

            def walk(name):
                def f(handle):
                    self.eng = name
                    self.ocnt = 0
                    self.cur_eng = None
                    self.pend = 0
                    self.last_instr = None
                    for k in self.rc:
                        self.rc[k] = 0
                    self.emit_all()
                return f
            blk.sync(walk("sync"))
            blk.vector(walk("vector"))
            blk.tensor(walk("tensor"))
            blk.scalar(walk("scalar"))
            blk.gpsimd(walk("gpsimd"))
        return nc

    def emit_all(self):
        self.s0_consts()
        if self.stage_limit >= 1:
            self.s1_ln()
        if self.stage_limit >= 2:
            self.s2_proj()
        if self.stage_limit >= 3:
            self.s3_scan()
        if self.stage_limit >= 4:
            self.s4_fir()
        if self.stage_limit >= 5:
            self.s4_window()
        if self.stage_limit >= 6:
            self.s5_l1attn()
        if self.stage_limit >= 7:
            self.s6_moe()
        self._flush()
        for e in ("sync", "vector", "tensor", "scalar", "gpsimd"):
            if self.on(e):
                getattr(self.nc, e).wait_ge(self.osem, self.ocnt)

    def s0_consts(self):
        s = self.sb
        for dst, src, castb in [
            ("c_ident", "ident", True), ("c_identf", "ident", False),
            ("c_maskAl", "maskAl", False), ("c_maskAu", "maskAu", False),
            ("c_maskIncT", "maskIncT", False),
            ("c_ind49", "ind49", True), ("c_ind49f", "ind49", False),
            ("c_sel4", "sel4", True), ("c_sel4f", "sel4f", False),
            ("c_hsel", "hsel", False), ("c_hselT", "hselT", False),
            ("c_e16", "e16", False),
            ("c_iotaG", "iotaG", False), ("c_sidx8", "sidx8", False),
            ("c_w0col", "w0col", False), ("c_offs", "offs", False),
            ("c_idxA", "idxA", False), ("c_idxB", "idxB", False),
            ("c_b1", "mlp_b1c", False), ("c_b2", "mlp_b2", False),
            ("c_onorm", "onormc", False),
            ("c_wavebf", "l1a_wavebf", False), ("c_wavebp", "l1a_wavebp", False),
            ("c_kb1", "l1a_kb", False), ("c_bg", "l1f_bg", False),
            ("c_glt", "glt", False),
        ]:
            self.dma("gpsimd" if castb else "sync", s[dst][:, :], self.d(src)[:, :])
        self.memset(s["c_ones"][0:1, :], 1.0)
        self.memset(s["c_onesf"][0:1, :], 1.0)
        self.memset(s["c_onescol"][:, :], 1.0)
        self.memset(s["c_zero"][:, :], 0.0)
        zf = s["c_zero"][:, :]
        zb = s["c_zero"][:, :].bitcast(BF16)
        for c in range(8):
            r0, r1 = c * 128, (c + 1) * 128
            self.dma("gpsimd", self.d("v_fm")[r0:r1, 0:PAD], zf[:, 0:PAD])
            self.dma("gpsimd", self.d("v_fm")[r0:r1, LPAD - PAD:LPAD], zf[:, 0:PAD])
            self.dma("gpsimd", self.d("xn_fm")[r0:r1, 0:PAD], zb[:, 0:PAD])
            self.dma("gpsimd", self.d("xn_fm")[r0:r1, LPAD - PAD:LPAD], zb[:, 0:PAD])
        for r in range(0, PAD, 128):
            self.dma("gpsimd", self.d("delta_tm")[r:r + 128, :], zb[:, 0:D])
            self.dma("gpsimd", self.d("delta_tm")[LPAD - PAD + r:LPAD - PAD + r + 128, :], zb[:, 0:D])
        self.dma("gpsimd", self.d("xn1_tm")[WIN:WIN + 128, :], zb[:, 0:D])
        tmp, glt = s["c_sp"], s["c_glt"]
        self.v("tensor_scalar", tmp[:, 0:1], glt[:, :], -1.0, None, ALU.mult)
        self.v("tensor_max", tmp[:, 1:2], tmp[:, 0:1], glt[:, :])
        self.v("tensor_scalar_max", tmp[:, 2:3], glt[:, :], 0.0)
        self.act(tmp[:, 3:4], tmp[:, 1:2], AF.Exp, scale=-1.0)
        self.act(tmp[:, 0:1], tmp[:, 3:4], AF.Ln, bias=1.0)
        self.v("tensor_add", tmp[:, 1:2], tmp[:, 0:1], tmp[:, 2:3])
        self.v("tensor_scalar_add", tmp[:, 2:3], tmp[:, 1:2], 0.5)
        self.v("reciprocal", tmp[:, 3:4], tmp[:, 2:3])
        self.dma("gpsimd", self.d("invt_d")[:, :], tmp[:, 3:4])
        for s6 in range(6):
            self.dma("gpsimd", s["c_invt"][s6::6, :], self.d("invt_d")[:, :])
        self.v("tensor_mul", s["c_b2i"][:, :], s["c_invt"][:, :], s["c_b2"][:, :])

    def s1_ln(self):
        s = self.sb
        xt = self.ar("s1_x", [128, D], F32, 0)
        xo = self.ar("s1_o", [128, D], BF16, 4096)
        sq = self.ar("s1_sq", [128, D], F32, 6144)
        mv = self.ar("s1_mv", [128, 8], F32, 10752)
        xnT = self.ar("s1_xnT", [128, L], BF16, 11264)
        for t in range(L // 128):
            self.dma("sync", xt[:, :], self.d("x_tm")[t * 128:(t + 1) * 128, :])
            self.act(sq[:, :], xt[:, :], AF.Square)
            self.v("reduce_sum", mv[:, 0:1], xt[:, :], AX.X)
            self.v("reduce_sum", mv[:, 1:2], sq[:, :], AX.X)
            self.v("tensor_scalar_mul", mv[:, 2:3], mv[:, 0:1], 1.0 / D)
            self.v("tensor_mul", mv[:, 3:4], mv[:, 2:3], mv[:, 2:3])
            self.v("tensor_scalar", mv[:, 4:5], mv[:, 1:2], 1.0 / D, None, ALU.mult)
            self.v("tensor_sub", mv[:, 5:6], mv[:, 4:5], mv[:, 3:4])
            self.v("tensor_scalar_add", mv[:, 5:6], mv[:, 5:6], 1e-5)
            self.act(mv[:, 6:7], mv[:, 5:6], AF.Sqrt)
            self.v("reciprocal", mv[:, 7:8], mv[:, 6:7])
            self.v("scalar_tensor_tensor", xo[:, :], xt[:, :], mv[:, 2:3],
                   mv[:, 7:8].broadcast_to([128, D]), ALU.subtract, ALU.mult)
            self.dma("gpsimd", self.d("xn_tmp")[t * 128:(t + 1) * 128, :], xo[:, :])
        for c in range(8):
            self.dmaT("sync", xnT[:, :], self.d("xn_tmp")[:, c * 128:(c + 1) * 128])
            self.dma("gpsimd", self.d("xn_fm")[c * 128:(c + 1) * 128, ROW0:ROW0 + L],
                     xnT[:, :])

    def s2_proj(self):
        s = self.sb
        o = 0
        def nxt(name, shape, dtype):
            nonlocal o
            h = self.ar(name, shape, dtype, o)
            o += (int(np.prod(shape[1:])) * DTB[dtype] + 511) // 512 * 512
            return h
        xn_sb = nxt("s2_xn", [128, 8 * TT], BF16)
        proj = nxt("s2_proj", [128, 24 * (TT + 4)], BF16)
        qraw = nxt("s2_qraw", [128, 16 * TT], F32)
        vraw = nxt("s2_vraw", [128, 8 * TT], F32)
        xsq = nxt("s2_xsq", [128, TT], BF16)
        qt = [nxt(f"s2_qt{i}", [4, TT], F32) for i in range(6)]
        qb = nxt("s2_qb", [4, 3 * TT], BF16)
        obf = nxt("s2_obf", [128, 2 * TT], BF16)
        wq = nxt("s2_w", [128, 8 * 3076], BF16)
        dg = nxt("s2_diag", [128, 96 * 128], BF16)
        cw = nxt("s2_cw", [128, 96], F32)

        for kc in range(8):
            self.dma("gpsimd", wq[:, kc * 3076:(kc + 1) * 3076],
                     self.d("w_qkvb")[kc * 128:(kc + 1) * 128, :])
        self.dma("sync", cw[:, :].rearrange("p (c t) -> p c t", c=8),
                 self.d("convw").rearrange("(c p) t -> p c t", p=128))
        for m in range(24):
            p3, c = divmod(m, 8)
            for j in range(4):
                self.v("tensor_scalar", dg[:, (m * 4 + j) * 128:(m * 4 + j + 1) * 128],
                       s["c_ident"][:, :], cw[:, c * 12 + p3 * 4 + j: c * 12 + p3 * 4 + j + 1],
                       None, ALU.mult)
        self.rsem("2L")
        for tt in range(8):
            t0 = tt * TT
            for kc in range(8):
                oc_now = self.ocnt
                self.rop("sync", lambda kc=kc, t0=t0: self.nc.sync.dma_start(
                    out=xn_sb[:, kc * TT:(kc + 1) * TT],
                    in_=self.d("xn_fm")[kc * 128:(kc + 1) * 128, ROW0 + t0:ROW0 + t0 + TT]),
                    incs=[("2L", 16)], waits=[("O", oc_now)])
            self.rwait("tensor", [("2L", self.rc["2L"])])
            for m in range(25):
                bank = self.ps[m % 2]
                rows = 128 if m < 24 else 4
                for kc in range(8):
                    self.mm(bank[0:rows, 0:TT],
                            wq[:, kc * 3076 + m * 128: kc * 3076 + m * 128 + rows],
                            xn_sb[:, kc * TT:(kc + 1) * TT],
                            start=(kc == 0), stop=(kc == 7))
                if m < 24:
                    pr = proj[:, m * 516:(m + 1) * 516]
                    if tt == 0:
                        self.memset(pr[:, 0:4], 0.0)
                    self.act(pr[:, 4:4 + TT], bank[0:128, 0:TT], AF.Copy)
                else:
                    self.act(qt[3][:, 0:TT], bank[0:4, 0:TT], AF.Sigmoid)
            for m in range(24):
                pr = proj[:, m * 516:(m + 1) * 516]
                bank = self.ps[2 + m % 2]
                for j in range(4):
                    self.mm(bank[:, 0:TT], dg[:, (m * 4 + j) * 128:(m * 4 + j + 1) * 128],
                            pr[:, 1 + j: 1 + j + TT], start=(j == 0), stop=(j == 3))
                dst = qraw[:, m * TT:(m + 1) * TT] if m < 16 else vraw[:, (m - 16) * TT:(m - 15) * TT]
                self.act(dst, bank[:, 0:TT], AF.Silu)
                self.v("tensor_copy", pr[:, 0:4], pr[:, TT:TT + 4])
            for qk in range(2):
                bank = self.ps[4]
                for c in range(8):
                    m = qk * 8 + c
                    self.act(xsq[:, 0:TT], qraw[:, m * TT:(m + 1) * TT], AF.Square)
                    h4 = c // 2
                    self.mm(bank[0:4, 0:TT], s["c_ind49"][:, 24 - h4:28 - h4], xsq[:, 0:TT],
                            start=(c == 0), stop=(c == 7))
                self.v("tensor_scalar_add", qt[4][:, 0:TT], bank[0:4, 0:TT], 1e-6)
                self.act(qt[5][:, 0:TT], qt[4][:, 0:TT], AF.Sqrt)
                self.v("reciprocal", qt[qk][:, 0:TT], qt[5][:, 0:TT])
            self.v("tensor_mul", qt[2][:, 0:TT], qt[1][:, 0:TT], qt[3][:, 0:TT])
            self.v("tensor_copy", qb[:, 0:TT], qt[1][:, 0:TT])
            self.v("tensor_copy", qb[:, TT:2 * TT], qt[2][:, 0:TT])
            self.v("tensor_copy", qb[:, 2 * TT:3 * TT], qt[3][:, 0:TT])
            for cb in range(TT // 128):
                bank = self.ps[5]
                self.trans(bank[:, 0:4], qt[0][:, cb * 128:(cb + 1) * 128], s["c_identf"][0:4, 0:4])
                chn = (t0 + cb * 128) // 128
                self.v("tensor_copy", s["rstdq_tm"][:, chn * 4:(chn + 1) * 4], bank[:, 0:4])
            for h4 in range(4):
                for iq in range(3):
                    bank = self.ps[6 + iq % 2]
                    self.mm(bank[:, 0:TT], s["c_sel4"][:, h4 * 128:(h4 + 1) * 128],
                            qb[:, iq * TT:(iq + 1) * TT])
                    for cc in range(2):
                        m = h4 * 2 + cc
                        src = qraw[:, (8 + m) * TT:(9 + m) * TT] if iq < 2 else vraw[:, m * TT:(m + 1) * TT]
                        ob = obf[:, cc * TT:(cc + 1) * TT]
                        self.v("tensor_tensor", ob, src, bank[:, 0:TT], ALU.mult)
                        nm = ("khat_fm", "kb_fm", "vb_fm")[iq]
                        self.dma("gpsimd", self.d(nm)[m * 128:(m + 1) * 128, t0:t0 + TT], ob)
            for m in range(8):
                ob = obf[:, (m % 2) * TT:(m % 2 + 1) * TT]
                self.act(ob, qraw[:, m * TT:(m + 1) * TT], AF.Copy)
                self.dma("gpsimd", self.d("q_fm")[m * 128:(m + 1) * 128, t0:t0 + TT], ob)
                self.dma("sync", self.d("v_fm")[m * 128:(m + 1) * 128, ROW0 + t0:ROW0 + t0 + TT],
                         vraw[:, m * TT:(m + 1) * TT])

    def s3_scan(self):
        s = self.sb
        G = 512
        o = 0
        def nxt(name, shape, dtype):
            nonlocal o
            h = self.ar(name, shape, dtype, o)
            o += (int(np.prod(shape[1:])) * DTB[dtype] + 511) // 512 * 512
            return h
        g_q = nxt("s3_q", [128, 2 * G], BF16)
        g_kh = nxt("s3_kh", [128, 2 * G], BF16)
        g_kb = nxt("s3_kb", [128, 2 * G], BF16)
        g_kht = nxt("s3_kht", [128, 4 * DK], BF16)
        g_kbt = nxt("s3_kbt", [128, 4 * DK], BF16)
        g_vbt = nxt("s3_vbt", [128, 4 * DK], BF16)
        S_f = nxt("s3_Sf", [128, 2 * DK], F32)
        S_b = nxt("s3_Sb", [128, 2 * DK], BF16)
        A_sb = nxt("s3_A", [128, 128], BF16)
        AT_sb = nxt("s3_AT", [128, 128], BF16)
        Tt = nxt("s3_T", [128, 128], BF16)
        TtT = nxt("s3_TT", [128, 128], BF16)
        Xp = nxt("s3_X", [128, 128], BF16)
        XpT = nxt("s3_XT", [128, 128], BF16)
        uh = nxt("s3_uh", [128, DK], BF16)
        atT = nxt("s3_atT", [128, 128], BF16)
        wng = nxt("s3_wng", [128, 2 * 128], BF16)
        o_sb = nxt("s3_o", [128, DK], BF16)
        dbgf = nxt("s3_dbgf", [128, 128], F32)

        self.rsem("3L")

        def ldma(q, out, in_, transpose=False):
            kw = {"transpose": True} if transpose else {}
            oc_now = self.ocnt
            self.rop(q, lambda: getattr(self.nc, q).dma_start(out=out, in_=in_, **kw),
                     incs=[("3L", 16)], waits=[("O", oc_now)])

        for h in range(4):
            self.memset(S_f[:, :], 0.0)
            self.memset(S_b[:, :], 0.0)
            r0 = h * DK
            for gi in range(L // G):
                t0 = gi * G
                for dc in range(2):
                    rr = r0 + dc * 128
                    ldma("sync", g_q[:, dc * G:(dc + 1) * G], self.d("q_fm")[rr:rr + 128, t0:t0 + G])
                    ldma("sync", g_kh[:, dc * G:(dc + 1) * G], self.d("khat_fm")[rr:rr + 128, t0:t0 + G])
                    ldma("sync", g_kb[:, dc * G:(dc + 1) * G], self.d("kb_fm")[rr:rr + 128, t0:t0 + G])
                for ci in range(4):
                    tc = t0 + ci * CH
                    ldma("scalar", g_kht[:, ci * DK:(ci + 1) * DK],
                         self.d("khat_fm")[r0:r0 + DK, tc:tc + CH], transpose=True)
                    ldma("scalar", g_kbt[:, ci * DK:(ci + 1) * DK],
                         self.d("kb_fm")[r0:r0 + DK, tc:tc + CH], transpose=True)
                    ldma("scalar", g_vbt[:, ci * DK:(ci + 1) * DK],
                         self.d("vb_fm")[r0:r0 + DK, tc:tc + CH], transpose=True)
                self.rwait("tensor", [("3L", self.rc["3L"])])
                for ci in range(4):
                    cg = gi * 4 + ci
                    qs = [g_q[:, dc * G + ci * CH: dc * G + (ci + 1) * CH] for dc in range(2)]
                    khs = [g_kh[:, dc * G + ci * CH: dc * G + (ci + 1) * CH] for dc in range(2)]
                    kbs = [g_kb[:, dc * G + ci * CH: dc * G + (ci + 1) * CH] for dc in range(2)]
                    kht = g_kht[:, ci * DK:(ci + 1) * DK]
                    kbt = g_kbt[:, ci * DK:(ci + 1) * DK]
                    vbt = g_vbt[:, ci * DK:(ci + 1) * DK]
                    ps = self.ps
                    for dc in range(2):
                        self.mm(ps[0][:, 0:128], kbs[dc], khs[dc], start=(dc == 0), stop=(dc == 1))
                    for dc in range(2):
                        self.mm(ps[1][:, 0:128], khs[dc], kbs[dc], start=(dc == 0), stop=(dc == 1))
                    self.v("tensor_tensor", A_sb[:, :], ps[0][:, 0:128], s["c_maskAl"][:, :], ALU.mult)
                    self.v("tensor_tensor", AT_sb[:, :], ps[1][:, 0:128], s["c_maskAu"][:, :], ALU.mult)
                    if h == 0 and cg == 0 and "dbg3" in self.dbg:
                        self.dma("gpsimd", self.d("dbg3")[:, 3328:3456], s["c_maskAu"][:, :])
                    self.v("tensor_add", Tt[:, :], A_sb[:, :], s["c_ident"][:, :])
                    self.v("tensor_add", TtT[:, :], AT_sb[:, :], s["c_ident"][:, :])
                    self.v("tensor_copy", Xp[:, :], A_sb[:, :])
                    self.v("tensor_copy", XpT[:, :], AT_sb[:, :])
                    for j in range(6):
                        self.mm(ps[0][:, 0:128], XpT[:, :], Xp[:, :])
                        self.mm(ps[1][:, 0:128], Xp[:, :], XpT[:, :])
                        self.act(Xp[:, :], ps[0][:, 0:128], AF.Copy)
                        self.act(XpT[:, :], ps[1][:, 0:128], AF.Copy)
                        if h == 0 and cg == 0 and j == 0 and "dbg3" in self.dbg:
                            self.dma("gpsimd", self.d("dbg3")[:, 3456:3584], s["c_maskAu"][:, :])
                        self.mm(ps[2][:, 0:128], s["c_ident"][:, :], Tt[:, :], start=True, stop=False)
                        self.mm(ps[2][:, 0:128], XpT[:, :], Tt[:, :], start=False, stop=True)
                        self.mm(ps[3][:, 0:128], s["c_ident"][:, :], TtT[:, :], start=True, stop=False)
                        self.mm(ps[3][:, 0:128], Tt[:, :], XpT[:, :], start=False, stop=True)
                        self.act(Tt[:, :], ps[2][:, 0:128], AF.Copy)
                        self.act(TtT[:, :], ps[3][:, 0:128], AF.Copy)
                        if h == 0 and cg == 0 and j == 0 and "dbg3" in self.dbg:
                            self.dma("gpsimd", self.d("dbg3")[:, 3584:3712], s["c_maskAu"][:, :])
                        if h == 0 and cg == 0 and "dbg3" in self.dbg:
                            self.dma("gpsimd", self.d("dbg3")[:, 1536 + j * 384:1536 + j * 384 + 128], Tt[:, :])
                            self.dma("gpsimd", self.d("dbg3")[:, 1536 + j * 384 + 128:1536 + j * 384 + 256], Xp[:, :])
                            self.dma("gpsimd", self.d("dbg3")[:, 1536 + j * 384 + 256:1536 + j * 384 + 384], XpT[:, :])
                    for dc in range(2):
                        self.mm(ps[4][:, dc * 128:(dc + 1) * 128],
                                kbt[:, dc * 128:(dc + 1) * 128], TtT[:, :])
                    for dc in range(2):
                        self.v("tensor_scalar", wng[:, dc * 128:(dc + 1) * 128],
                               ps[4][:, dc * 128:(dc + 1) * 128], -1.0, None, ALU.mult)
                    self.mm(ps[5][:, 0:DK], TtT[:, :], vbt[:, :], start=True, stop=False)
                    for dc in range(2):
                        self.mm(ps[5][:, 0:DK], wng[:, dc * 128:(dc + 1) * 128],
                                S_b[:, dc * DK:(dc + 1) * DK], start=False, stop=(dc == 1))
                    self.act(uh[:, :], ps[5][:, 0:DK], AF.Copy)
                    for dc in range(2):
                        self.mm(ps[6][:, 0:128], khs[dc], qs[dc], start=(dc == 0), stop=(dc == 1))
                    self.v("tensor_tensor", atT[:, :], ps[6][:, 0:128], s["c_maskIncT"][:, :], ALU.mult)
                    for dc in range(2):
                        self.mm(ps[7][:, 0:DK], qs[dc], S_b[:, dc * DK:(dc + 1) * DK],
                                start=(dc == 0), stop=False)
                    self.mm(ps[7][:, 0:DK], atT[:, :], uh[:, :], start=False, stop=True)
                    self.v("tensor_scalar", o_sb[:, :], ps[7][:, 0:DK],
                           s["rstdq_tm"][:, cg * 4 + h: cg * 4 + h + 1], None, ALU.mult)
                    self.dma("gpsimd", self.d("delta_tm")[ROW0 + cg * CH:ROW0 + (cg + 1) * CH,
                                                          h * DK:(h + 1) * DK], o_sb[:, :])
                    if h == 0 and cg == 0 and "dbg3" in self.dbg:
                        dbg = self.d("dbg3")
                        self.dma("gpsimd", dbg[:, 0:128], A_sb[:, :])
                        self.dma("gpsimd", dbg[:, 128:256], Tt[:, :])
                        self.dma("gpsimd", dbg[:, 256:512], uh[:, :])
                        self.dma("gpsimd", dbg[:, 512:768], wng[:, :])
                        self.dma("gpsimd", dbg[:, 768:1024], g_kht[:, 0:256])
                        self.dma("gpsimd", dbg[:, 1024:1280], g_vbt[:, 0:256])
                        self.dma("gpsimd", dbg[:, 1280:1408], atT[:, :])
                    for dc in range(2):
                        self.mm(ps[dc][:, 0:DK], kht[:, dc * 128:(dc + 1) * 128], uh[:, :])
                    for dc in range(2):
                        self.v("tensor_tensor", S_f[:, dc * DK:(dc + 1) * DK],
                               S_f[:, dc * DK:(dc + 1) * DK], ps[dc][:, 0:DK], ALU.add)
                        self.act(S_b[:, dc * DK:(dc + 1) * DK], S_f[:, dc * DK:(dc + 1) * DK], AF.Copy)

    def s4_fir(self):
        s = self.sb
        o = 0
        def nxt(name, shape, dtype):
            nonlocal o
            h = self.ar(name, shape, dtype, o)
            o += (int(np.prod(shape[1:])) * DTB[dtype] + 511) // 512 * 512
            return h
        fw = nxt("s4a_fw", [128, 8 * 42], F32)
        dgf = nxt("s4a_dg", [128, 42 * 128], BF16)
        vwf = nxt("s4a_vwf", [128, 2 * 320], F32)
        vw = nxt("s4a_vw", [128, 2 * 320], BF16)
        ob = nxt("s4a_ob", [128, 4 * 256], BF16)
        idx11 = nxt("s4a_idx11", [128, NST4], I32)
        self.dma("sync", fw[:, :].rearrange("p (c t) -> p c t", c=8),
                 self.d("firw").rearrange("(c p) t -> p c t", p=128))
        TAPS = [(0, 0, 1), (1, 1, 3), (2, 4, 7), (3, 11, 31)]
        for nm in ("4L", "4P", "4E", "4S"):
            self.rsem(nm)
        for c in range(8):
            for t in range(42):
                self.v("tensor_scalar", dgf[:, t * 128:(t + 1) * 128], s["c_ident"][:, :],
                       fw[:, c * 42 + t: c * 42 + t + 1], None, ALU.mult)
            for st in range(NST4):
                self.v("tensor_scalar_add", idx11[:, st:st + 1], self.sb["c_idxA"][:, c:c + 1],
                       (0 if st == 0 else st * ST4 - 32))
            self.memset(vwf[:, 0:32], 0.0)
            oc0 = self.ocnt
            l0_, p0_, e0_, s0_ = (self.rc[nm] for nm in ("4L", "4P", "4E", "4S"))
            l_after, cp_after, ev_after = {}, {}, {}

            def gath4(st):
                b = st % 2
                w_ = [("O", oc0)] if st < 2 else [("4E", cp_after[st - 2])]
                if st == 0:
                    self.rop("gpsimd", lambda b=b, st=st: self.nc.gpsimd.indirect_dma_start(
                        vwf[:, b * 320 + 32:b * 320 + 320], None,
                        bass.AP(self.dram["v_fm"], 0, [[1, 128], [1, 288]]),
                        IndirectOffsetOnAxis(idx11[:, st:st + 1], 1)),
                        incs=[("4L", 16)], waits=w_)
                else:
                    self.rop("gpsimd", lambda b=b, st=st: self.nc.gpsimd.indirect_dma_start(
                        vwf[:, b * 320:b * 320 + 320], None,
                        bass.AP(self.dram["v_fm"], 0, [[1, 128], [1, 320]]),
                        IndirectOffsetOnAxis(idx11[:, st:st + 1], 1)),
                        incs=[("4L", 16)], waits=w_)
                l_after[st] = self.rc["4L"]

            gath4(0)
            for st in range(NST4):
                if st + 1 < NST4:
                    gath4(st + 1)
                b = st % 2
                cw = [("4L", l_after.get(st + 1, l_after[st]))]
                if st >= 2:
                    cw.append(("4P", p0_ + (st - 2) * 4 + 4))
                if st == 0:
                    cw.append(("O", oc0))
                self.rop("scalar", lambda b=b: self.nc.scalar.activation(
                    out=vw[:, b * 320:(b + 1) * 320], in_=vwf[:, b * 320:(b + 1) * 320],
                    func=AF.Copy), incs=[("4E", 1)], waits=cw)
                cp_after[st] = self.rc["4E"]
                for br, t0_, ntap in TAPS:
                    bank = self.ps[br]
                    w_ = [("4E", cp_after[st])]
                    if st >= 1:
                        w_.append(("4E", ev_after[(st - 1, br)]))
                    if c == 0 and st == 0 and br == 0:
                        w_.append(("O", oc0))
                    for i in range(ntap):
                        self.rop("tensor", lambda bank=bank, t0_=t0_, i=i, b=b, ntap=ntap:
                                 self.nc.tensor.matmul(
                                     bank[:, 0:ST4], dgf[:, (t0_ + i) * 128:(t0_ + i + 1) * 128],
                                     vw[:, b * 320 + 32 - (ntap - 1) + i:
                                        b * 320 + 32 - (ntap - 1) + i + ST4],
                                     start=(i == 0), stop=(i == ntap - 1)),
                                 incs=[("4P", 1)] if i == ntap - 1 else [],
                                 waits=w_ if i == 0 else [])
                    ew = [("4P", p0_ + st * 4 + br + 1)]
                    if st >= 1:
                        ew.append(("4S", s0_ + ((st - 1) * 4 + br + 1) * 16))
                    self.rop("scalar", lambda br=br, bank=bank: self.nc.scalar.activation(
                        out=ob[:, br * 256:br * 256 + ST4], in_=bank[:, 0:ST4], func=AF.Copy),
                        incs=[("4E", 1)], waits=ew)
                    ev_after[(st, br)] = self.rc["4E"]
                    self.rop("gpsimd", lambda br=br, c=c, st=st: self.nc.gpsimd.dma_start(
                        out=self.d("fir_dram")[(br * 8 + c) * 128:(br * 8 + c + 1) * 128,
                                               st * ST4:(st + 1) * ST4],
                        in_=ob[:, br * 256:br * 256 + ST4]),
                        incs=[("4S", 16)], waits=[("4E", ev_after[(st, br)])])
            self.rjoin([(nm, self.rc[nm]) for nm in ("4L", "4P", "4E", "4S")])

    def s4_window(self):
        s = self.sb
        o = 0
        def nxt(name, shape, dtype):
            nonlocal o
            h = self.ar(name, shape, dtype, o)
            o += (int(np.prod(shape[1:])) * DTB[dtype] + 511) // 512 * 512
            return h
        N = ST4
        w_ow = nxt("s4_ow", [128, 8 * D], BF16)
        w_m1 = nxt("s4_m1", [128, 8 * D], BF16)
        w_m1s = [nxt(f"s4_m1s{k}", [24, D], BF16) for k in range(4)]
        w_m2 = nxt("s4_m2", [128, 8 * 24], BF16)
        w_l0 = nxt("s4_l0", [128, 8 * D], BF16)
        w_gw = nxt("s4_gw", [128, 8 * 8], BF16)
        w_U = nxt("s4_U", [128, 8 * 128], BF16)
        w_V = nxt("s4_V", [128, D], BF16)
        c_rsel = nxt("s4_rsel", [24, 24 * 128], F32)
        b_v = nxt("s4_bv", [128, 8 * N], F32)
        b_vb = nxt("s4_bvb", [128, 8 * N], BF16)
        b_d = nxt("s4_bd", [128, 8 * N], BF16)
        b_f = [nxt(f"s4_bf{i}", [128, 8 * N], BF16) for i in range(4)]
        xnw = nxt("s4_xnw", [128, 8 * N], BF16)
        sq = nxt("s4_sq", [128, 8 * N], BF16)
        stm = nxt("s4_stm", [24, N], BF16)
        sts = nxt("s4_sts", [24, N], BF16)
        sta = nxt("s4_sta", [24, N], BF16)
        stl = nxt("s4_stl", [24, N], BF16)
        mf = nxt("s4_mf", [24, N], F32)
        t1 = nxt("s4_t1", [24, N], F32)
        hg = nxt("s4_hg", [128, 8 * N], BF16)
        expw = nxt("s4_expw", [24, N], F32)
        wf = nxt("s4_wf", [24, N], F32)
        fus = nxt("s4_fus", [128, 8 * N], F32)
        fzn = nxt("s4_fzn", [128, 8 * N], BF16)
        s_t = nxt("s4_s", [128, 8 * N], F32)
        g_t = nxt("s4_g", [128, 8 * N], F32)
        tf = nxt("s4_tf", [128, N], F32)
        tb = nxt("s4_tb", [128, N], BF16)
        x1t = nxt("s4_x1", [128, 8 * N], F32)
        sqf = nxt("s4_sqf", [128, N], F32)
        o_xn1t = o
        xn1t = nxt("s4_xn1", [128, 8 * N], BF16)
        ab = self.ar("s4_ab", [128, 8 * N], BF16, o_xn1t)
        xn1tt = nxt("s4_xn1t", [128, D], BF16)
        egl = nxt("s4_egl", [8, N], F32)
        erc = nxt("s4_erc", [1, N], F32)
        lnm = nxt("s4_lnm", [1, 2 * N], F32)

        for kc in range(8):
            self.dma("gpsimd", w_ow[:, kc * D:(kc + 1) * D], self.d("o_wT")[kc * 128:(kc + 1) * 128, :])
            self.dma("gpsimd", w_m1[:, kc * D:(kc + 1) * D], self.d("mlp_w1p")[kc * 128:(kc + 1) * 128, :])
            self.dma("gpsimd", w_m2[:, kc * 24:(kc + 1) * 24], self.d("mlp_w2")[kc * 128:(kc + 1) * 128, :])
            self.dma("gpsimd", w_l0[:, kc * D:(kc + 1) * D], self.d("l0f_wT")[kc * 128:(kc + 1) * 128, :])
            self.dma("gpsimd", w_gw[:, kc * 8:(kc + 1) * 8], self.d("l0f_gw")[kc * 128:(kc + 1) * 128, :])
            self.dma("gpsimd", w_U[:, kc * 128:(kc + 1) * 128], self.d("l0f_U")[kc * 128:(kc + 1) * 128, :])
        for k in range(4):
            self.dma("gpsimd", w_m1s[k][:, :], self.d("mlp_w1p")[1024 + k * 24:1024 + (k + 1) * 24, :])
        self.dma("gpsimd", w_V[:, :], self.d("l0f_V")[:, :])
        self.dma("sync", c_rsel[:, :], self.d("rsel24")[:, :])
        ps = self.ps

        self.rsem("4W")

        def wrop(fn):
            oc_now = self.ocnt
            self.rop("gpsimd", fn, incs=[("4W", 16)], waits=[("O", oc_now)])

        for st in range(NST4):
            w0 = st * N
            self.v("tensor_scalar_add", self.sb["c_idxS"][:, :], self.sb["c_idxA"][:, :], w0)
            for c in range(8):
                ic = self.sb["c_idxS"][:, c:c + 1]
                wrop(lambda c=c, ic=ic: self.nc.gpsimd.indirect_dma_start(
                    b_v[:, c * N:(c + 1) * N], None,
                    bass.AP(self.dram["v_fm"], 0, [[1, 128], [1, N]]),
                    IndirectOffsetOnAxis(ic, 1)))
                wrop(lambda c=c, ic=ic: self.nc.gpsimd.indirect_dma_start(
                    xnw[:, c * N:(c + 1) * N], None,
                    bass.AP(self.dram["xn_fm"], 0, [[1, 128], [1, N]]),
                    IndirectOffsetOnAxis(ic, 1)))
                for br in range(4):
                    wrop(lambda br=br, c=c, w0=w0: self.nc.gpsimd.dma_start(
                        out=b_f[br][:, c * N:(c + 1) * N],
                        in_=self.d("fir_dram")[(br * 8 + c) * 128:(br * 8 + c + 1) * 128, w0:w0 + N]))
            for e_ in ("scalar", "vector", "tensor"):
                self.rwait(e_, [("4W", self.rc["4W"])])
            for c in range(8):
                self.act(b_vb[:, c * N:(c + 1) * N], b_v[:, c * N:(c + 1) * N], AF.Copy)
            for tbk in range(N // 128):
                self.v("tensor_scalar_add", self.sb["c_idxD"][:, :], self.sb["c_idxB"][:, :], (w0 + tbk * 128) * D)
                self.windma("delta_tm", D, xn1tt[:, :], self.sb["c_idxD"][:, 0:1])
                for c in range(8):
                    self.trans(ps[3][:, 0:256].bitcast(BF16)[:, 0:128],
                               xn1tt[:, c * 128:(c + 1) * 128], s["c_ident"][:, :])
                    self.v("tensor_copy", b_d[:, c * N + tbk * 128: c * N + tbk * 128 + 128],
                           ps[3][:, 0:256].bitcast(BF16)[:, 0:128])
            branches = [b_f[0], b_f[1], b_f[2], b_f[3], b_d, b_vb]
            for bi, bt in enumerate(branches):
                for c in range(8):
                    self.act(sq[:, c * N:(c + 1) * N], bt[:, c * N:(c + 1) * N], AF.Square)
                for c in range(8):
                    sl = bt[:, c * N:(c + 1) * N]
                    self.v("scalar_tensor_tensor", ab[:, c * N:(c + 1) * N], sl, -1.0, sl,
                           ALU.mult, ALU.max)
                for c in range(8):
                    h4 = c // 2
                    col = 24 - (bi * 4 + h4)
                    self.mm(ps[0][0:24, 0:N], s["c_ind49"][:, col:col + 24], bt[:, c * N:(c + 1) * N],
                            start=(bi == 0 and c == 0), stop=(bi == 5 and c == 7))
                    self.mm(ps[1][0:24, 0:N], s["c_ind49"][:, col:col + 24], ab[:, c * N:(c + 1) * N],
                            start=(bi == 0 and c == 0), stop=(bi == 5 and c == 7))
                    self.mm(ps[2][0:24, 0:N], s["c_ind49"][:, col:col + 24], sq[:, c * N:(c + 1) * N],
                            start=(bi == 0 and c == 0), stop=(bi == 5 and c == 7))
            self.v("tensor_scalar", mf[:, :], ps[0][0:24, 0:N], 1.0 / 256, None, ALU.mult)
            self.v("tensor_copy", stm[:, :], mf[:, :])
            self.v("tensor_scalar", sta[:, :], ps[1][0:24, 0:N], 1.0 / 256, None, ALU.mult)
            self.act(stl[:, :], ps[2][0:24, 0:N], AF.Sqrt)
            self.v("tensor_tensor", t1[:, :], ps[0][0:24, 0:N], mf[:, :], ALU.mult)
            self.v("tensor_tensor", t1[:, :], ps[2][0:24, 0:N], t1[:, :], ALU.subtract)
            self.v("tensor_scalar", t1[:, :], t1[:, :], 1.0 / 255, None, ALU.mult)
            self.act(sts[:, :], t1[:, :], AF.Sqrt)
            def g_items():
                its = []
                for oc in range(8):
                    def mk(oc):
                        def mms(bank):
                            fns = []
                            for kc in range(8):
                                fns.append(lambda bank=bank, kc=kc, oc=oc: self.nc.tensor.matmul(
                                    bank, w_m1[:, kc * D + oc * 128: kc * D + oc * 128 + 128],
                                    xnw[:, kc * N:(kc + 1) * N], start=(kc == 0), stop=False))
                            for k in range(4):
                                stt = (stm, sts, sta, stl)[k]
                                fns.append(lambda bank=bank, k=k, oc=oc, stt=stt: self.nc.tensor.matmul(
                                    bank, w_m1s[k][:, oc * 128:(oc + 1) * 128], stt[:, :],
                                    start=False, stop=(k == 3)))
                            return fns
                        def ev(bank):
                            return lambda: self.nc.scalar.activation(
                                out=hg[:, oc * N:(oc + 1) * N], in_=bank, func=AF.Gelu,
                                bias=s["c_b1"][:, oc:oc + 1])
                        return {"mms": mms, "evac": ev}
                    its.append(mk(oc))
                return its
            self.mmpipe(g_items(), nbank=2, bank0=3, width=N)
            for kc in range(8):
                self.mm(ps[5][0:24, 0:N], w_m2[:, kc * 24:(kc + 1) * 24],
                        hg[:, kc * N:(kc + 1) * N], start=(kc == 0), stop=(kc == 7))
            self.act(expw[:, :], ps[5][0:24, 0:N], AF.Exp,
                     bias=s["c_b2i"][:, 0:1], scale=s["c_invt"][:, 0:1])
            self.mm(ps[6][0:4, 0:N], s["c_hsel"][:, :], expw[:, :])
            self.v("tensor_scalar_add", t1[0:4, :], ps[6][0:4, 0:N], 0.0)
            self.v("reciprocal", t1[0:4, :], t1[0:4, :])
            self.mm(ps[6][0:24, 0:N], s["c_hselT"][:, :], t1[0:4, :])
            self.v("tensor_tensor", wf[:, :], ps[6][0:24, 0:N], expw[:, :], ALU.mult)
            for h4 in range(4):
                for br in range(6):
                    bank = ps[br % 3]
                    self.mm(bank[:, 0:N],
                            c_rsel[:, (h4 * 6 + br) * 128:(h4 * 6 + br + 1) * 128], wf[:, :])
                    for cc in range(2):
                        c = h4 * 2 + cc
                        src = branches[br]
                        sl = (b_v[:, c * N:(c + 1) * N] if br == 5 else src[:, c * N:(c + 1) * N])
                        dst = fus[:, c * N:(c + 1) * N]
                        if br == 0:
                            self.v("tensor_tensor", dst, bank[:, 0:N], sl, ALU.mult)
                        else:
                            self.v("tensor_tensor", tf[:, :], bank[:, 0:N], sl, ALU.mult)
                            self.v("tensor_add", dst, dst, tf[:, :])
            for c in range(8):
                self.act(sq[:, c * N:(c + 1) * N], fus[:, c * N:(c + 1) * N], AF.Square)
            for c in range(8):
                h4 = c // 2
                self.mm(ps[3][0:4, 0:N], s["c_ind49"][:, 24 - h4:28 - h4], sq[:, c * N:(c + 1) * N],
                        start=(c == 0), stop=(c == 7))
            self.v("tensor_scalar", t1[0:4, :], ps[3][0:4, 0:N], 1.0 / 256, 1e-5, ALU.mult, ALU.add)
            self.act(t1[4:8, :] if False else mf[0:4, :], t1[0:4, :], AF.Sqrt)
            self.v("reciprocal", t1[0:4, :], mf[0:4, :])
            for h4 in range(4):
                bank = ps[4]
                self.mm(bank[:, 0:N], s["c_sel4f"][:, h4 * 128:(h4 + 1) * 128], t1[0:4, :])
                for cc in range(2):
                    c = h4 * 2 + cc
                    self.v("scalar_tensor_tensor", fzn[:, c * N:(c + 1) * N],
                           fus[:, c * N:(c + 1) * N], s["c_onorm"][:, c % 2:c % 2 + 1],
                           bank[:, 0:N], ALU.mult, ALU.mult)
            def ow_items():
                its = []
                for oc in range(8):
                    def mk(oc):
                        def mms(bank):
                            return [lambda bank=bank, kc=kc, oc=oc: self.nc.tensor.matmul(
                                bank, w_ow[:, kc * D + oc * 128: kc * D + oc * 128 + 128],
                                fzn[:, kc * N:(kc + 1) * N], start=(kc == 0), stop=(kc == 7))
                                for kc in range(8)]
                        def ev(bank):
                            return lambda: self.nc.scalar.activation(
                                out=s_t[:, oc * N:(oc + 1) * N], in_=bank, func=AF.Copy)
                        return {"mms": mms, "evac": ev}
                    its.append(mk(oc))
                return its
            self.mmpipe(ow_items(), nbank=2, bank0=5, width=N)
            for kc in range(8):
                self.mm(ps[0][0:8, 0:N], w_gw[:, kc * 8:(kc + 1) * 8],
                        xnw[:, kc * N:(kc + 1) * N], start=(kc == 0), stop=(kc == 7))
            self.act(egl[:, :], ps[0][0:8, 0:N], AF.Exp)
            self.mm(ps[1][0:1, 0:N], s["c_onescol"][0:8, 0:1], egl[:, :])
            self.v("tensor_scalar_add", erc[:, :], ps[1][0:1, 0:N], 0.0)
            self.v("reciprocal", erc[:, :], erc[:, :])
            self.mm(ps[1][:, 0:N], s["c_onesf"][0:1, 0:128], erc[:, :])
            self.mm(ps[2][:, 0:N], s["c_e16"][:, :], egl[:, :])
            for kc in range(8):
                self.mm(ps[0][:, 0:N], w_U[:, kc * 128:(kc + 1) * 128],
                        xnw[:, kc * N:(kc + 1) * N], start=(kc == 0), stop=(kc == 7))
            self.v("tensor_scalar_add", tf[:, :], ps[0][:, 0:N], 0.0)
            self.v("tensor_tensor", tf[:, :], ps[2][:, 0:N], tf[:, :], ALU.mult)
            self.v("scalar_tensor_tensor", tb[:, :], tf[:, :], 0.0625, ps[1][:, 0:N], ALU.mult, ALU.mult)
            def l0_items():
                its = []
                for oc in range(8):
                    def mk(oc):
                        def mms(bank):
                            fns = [lambda bank=bank, kc=kc, oc=oc: self.nc.tensor.matmul(
                                bank, w_l0[:, kc * D + oc * 128: kc * D + oc * 128 + 128],
                                xnw[:, kc * N:(kc + 1) * N], start=(kc == 0), stop=False)
                                for kc in range(8)]
                            fns.append(lambda bank=bank, oc=oc: self.nc.tensor.matmul(
                                bank, w_V[:, oc * 128:(oc + 1) * 128], tb[:, :],
                                start=False, stop=True))
                            return fns
                        def ev(bank):
                            return lambda: self.nc.scalar.activation(
                                out=g_t[:, oc * N:(oc + 1) * N], in_=bank, func=AF.Silu)
                        return {"mms": mms, "evac": ev}
                    its.append(mk(oc))
                return its
            self.mmpipe(l0_items(), nbank=2, bank0=3, width=N)
            if st == 3 and "dbg4" in self.dbg:
                dbg = self.d("dbg4")
                self.dma("gpsimd", dbg[:, 0:256], s_t[:, 0:N])
                self.dma("gpsimd", dbg[:, 256:512], g_t[:, 0:N])
                self.dma("gpsimd", dbg[:, 512:768], fzn[:, 0:N])
                self.dma("gpsimd", dbg[0:24, 768:1024], wf[:, :])
                self.dma("gpsimd", dbg[0:24, 1024:1280], stm[:, :])
                self.dma("gpsimd", dbg[0:24, 1280:1536], sts[:, :])
                self.dma("gpsimd", dbg[0:24, 1536:1792], sta[:, :])
                self.dma("gpsimd", dbg[0:24, 1792:2048], stl[:, :])
                self.dma("gpsimd", dbg[:, 2048:2304], hg[:, 0:N])
                self.dma("gpsimd", dbg[:, 2304:2560], fus[:, 0:N])
                self.dma("gpsimd", dbg[:, 2560:2816], b_d[:, 0:N])
            for c in range(8):
                self.windma("x_fm", N, sqf[:, :], self.sb["c_idxS"][:, c:c + 1])
                self.v("tensor_tensor", tf[:, :], s_t[:, c * N:(c + 1) * N],
                       g_t[:, c * N:(c + 1) * N], ALU.mult)
                self.v("tensor_add", x1t[:, c * N:(c + 1) * N], sqf[:, :], tf[:, :])
                self.dma("gpsimd", self.d("x1_fm")[c * 128:(c + 1) * 128, w0:w0 + N],
                         x1t[:, c * N:(c + 1) * N])
            for c in range(8):
                self.mm(ps[6][0:1, 0:N], s["c_onescol"][:, 0:1], x1t[:, c * N:(c + 1) * N],
                        start=(c == 0), stop=(c == 7))
            for c in range(8):
                self.act(sqf[:, :], x1t[:, c * N:(c + 1) * N], AF.Square)
                self.mm(ps[7][0:1, 0:N], s["c_onescol"][:, 0:1], sqf[:, :],
                        start=(c == 0), stop=(c == 7))
            self.v("tensor_scalar", lnm[:, 0:N], ps[6][0:1, 0:N], 1.0 / D, None, ALU.mult)
            self.v("tensor_scalar", lnm[:, N:2 * N], ps[7][0:1, 0:N], 1.0 / D, None, ALU.mult)
            self.v("tensor_tensor", erc[:, :], lnm[:, 0:N], lnm[:, 0:N], ALU.mult)
            self.v("tensor_tensor", erc[:, :], lnm[:, N:2 * N], erc[:, :], ALU.subtract)
            self.v("tensor_scalar_add", erc[:, :], erc[:, :], 1e-5)
            self.act(erc[:, :], erc[:, :], AF.Sqrt)
            self.v("reciprocal", erc[:, :], erc[:, :])
            self.mm(ps[6][:, 0:N], s["c_onesf"][0:1, 0:128], lnm[:, 0:N])
            self.mm(ps[7][:, 0:N], s["c_onesf"][0:1, 0:128], erc[:, :])
            for c in range(8):
                self.v("tensor_tensor", tf[:, :], x1t[:, c * N:(c + 1) * N], ps[6][:, 0:N], ALU.subtract)
                self.v("tensor_tensor", xn1t[:, c * N:(c + 1) * N], tf[:, :], ps[7][:, 0:N], ALU.mult)
                self.dma("gpsimd", self.d("xn1_fm")[c * 128:(c + 1) * 128, w0:w0 + N],
                         xn1t[:, c * N:(c + 1) * N])
            for tbk in range(N // 128):
                for c in range(8):
                    self.trans(ps[c % 2][:, 0:256].bitcast(BF16)[:, 0:128],
                               xn1t[:, c * N + tbk * 128: c * N + tbk * 128 + 128], s["c_ident"][:, :])
                    self.v("tensor_copy", xn1tt[:, c * 128:(c + 1) * 128],
                           ps[c % 2][:, 0:256].bitcast(BF16)[:, 0:128])
                self.dma("gpsimd", self.d("xn1_tm")[w0 + tbk * 128:w0 + tbk * 128 + 128, :], xn1tt[:, :])

    def s5_l1attn(self):
        s = self.sb
        o = 0
        def nxt(name, shape, dtype):
            nonlocal o
            h = self.ar(name, shape, dtype, o)
            o += (int(np.prod(shape[1:])) * DTB[dtype] + 511) // 512 * 512
            return h
        w_wv = nxt("s5_wv", [128, 8 * 16], BF16)
        w_kw = nxt("s5_kw", [128, 8 * 264], BF16)
        w_out = nxt("s5_out", [128, 8 * D], BF16)
        xnb = nxt("s5_xnb", [128, 8 * 128], BF16)
        f8 = nxt("s5_f8", [8, 128], F32)
        p8 = nxt("s5_p8", [8, 128], F32)
        fsum = nxt("s5_fsum", [1, 128], F32)
        tp = nxt("s5_tp", [1, 128], F32)
        posl = nxt("s5_posl", [33, 128], F32)
        pos_tm = nxt("s5_postm", [128, 33], F32)
        lo_i = nxt("s5_loi", [128, 33], I32)
        lo_f = nxt("s5_lof", [128, 33], F32)
        frac = nxt("s5_frac", [128, 33], F32)
        idxg = nxt("s5_idxg", [128, 33], I32)
        wk = [nxt(f"s5_wk{i}", [128, 128], F32) for i in range(3)]
        w_tm = nxt("s5_wtm", [128, 264], F32)
        wh = nxt("s5_wh", [128, 264], F32)
        wl = nxt("s5_wl", [128, 264], F32)
        NB = 16
        wl_all = nxt("s5_wlall", [128, NB * 264], BF16)
        wh_all = nxt("s5_whall", [128, NB * 264], BF16)
        idx_all = nxt("s5_idxall", [128, NB * 33], I32)
        NGT = 4
        gt = nxt("s5_gt", [128, NGT * 2048], BF16)
        dg = nxt("s5_dg", [128, 4 * 2048], BF16)
        ohfm_all = nxt("s5_ohfmall", [128, NB * 1024], BF16)
        s1t = nxt("s5_s1t", [128, 4 * 128], F32)
        ps = self.ps

        for kc in range(8):
            self.dma("gpsimd", w_wv[:, kc * 16:(kc + 1) * 16], self.d("l1a_wavew")[kc * 128:(kc + 1) * 128, :])
            self.dma("gpsimd", w_kw[:, kc * 264:(kc + 1) * 264], self.d("l1a_kw")[kc * 128:(kc + 1) * 128, :])
            self.dma("gpsimd", w_out[:, kc * D:(kc + 1) * D], self.d("l1a_outwT")[kc * 128:(kc + 1) * 128, :])
        self.rsem("5x")
        for bi in range(16):
            w0 = OUT0 + bi * 128
            for kc in range(8):
                oc_now = self.ocnt
                self.rop("sync", lambda kc=kc, w0=w0: self.nc.sync.dma_start(
                    out=xnb[:, kc * 128:(kc + 1) * 128],
                    in_=self.d("xn1_fm")[kc * 128:(kc + 1) * 128, w0:w0 + 128]),
                    incs=[("5x", 16)], waits=[("O", oc_now)])
            self.rwait("tensor", [("5x", self.rc["5x"])])
            for kc in range(8):
                self.mm(ps[0][0:8, 0:128], w_wv[:, kc * 16:kc * 16 + 8],
                        xnb[:, kc * 128:(kc + 1) * 128], start=(kc == 0), stop=(kc == 7))
            for kc in range(8):
                self.mm(ps[1][0:8, 0:128], w_wv[:, kc * 16 + 8:kc * 16 + 16],
                        xnb[:, kc * 128:(kc + 1) * 128], start=(kc == 0), stop=(kc == 7))
            self.act(f8[:, :], ps[0][0:8, 0:128], AF.Silu, bias=s["c_wavebf"][:, 0:1])
            self.act(p8[:, :], ps[1][0:8, 0:128], AF.Silu, bias=s["c_wavebp"][:, 0:1])
            self.act(f8[:, :], f8[:, :], AF.Sigmoid)
            self.act(p8[:, :], p8[:, :], AF.Tanh)
            self.v("tensor_scalar", f8[:, :], f8[:, :], 15.0, 1.0, ALU.mult, ALU.add)
            self.v("tensor_scalar", p8[:, :], p8[:, :], 16.0, None, ALU.mult)
            self.mm(ps[2][0:1, 0:128], s["c_onescol"][0:8, 0:1], f8[:, :])
            self.mm(ps[3][0:1, 0:128], s["c_onescol"][0:8, 0:1], p8[:, :])
            self.v("tensor_scalar_add", fsum[:, :], ps[2][0:1, 0:128], 0.0)
            self.v("scalar_tensor_tensor", tp[:, :], ps[3][0:1, 0:128], 0.125,
                   s["c_iotaG"][0:1, w0:w0 + 128], ALU.mult, ALU.add)
            self.mm(ps[4][0:33, 0:128], s["c_sidx8"][:, :], fsum[:, :], start=True, stop=False)
            self.mm(ps[4][0:33, 0:128], s["c_onesf"][0:1, 0:33], tp[:, :], start=False, stop=True)
            self.v("tensor_scalar", posl[:, :], ps[4][0:33, 0:128], 0.0, 4095.0, ALU.max, ALU.min)
            self.v("tensor_scalar", posl[:, :], posl[:, :], s["c_w0col"][:, 0:1], None, ALU.subtract)
            self.trans(ps[5][:, 0:33], posl[:, :], s["c_identf"][0:33, 0:33])
            self.v("tensor_scalar_add", pos_tm[:, :], ps[5][:, 0:33], 0.0)
            self.v("tensor_scalar", lo_f[:, :], pos_tm[:, :], 0.5, None, ALU.subtract)
            self.v("tensor_copy", lo_i[:, :], lo_f[:, :])
            self.v("tensor_copy", lo_f[:, :], lo_i[:, :])
            self.v("tensor_sub", frac[:, :], pos_tm[:, :], lo_f[:, :])
            self.v("tensor_scalar", idx_all[:, bi * 33:(bi + 1) * 33], lo_i[:, :], D, None, ALU.mult)
            for ocr, rows in ((0, 128), (1, 128), (2, 8)):
                bank = ps[6]
                for kc in range(8):
                    self.mm(bank[0:rows, 0:128], w_kw[:, kc * 264 + ocr * 128: kc * 264 + ocr * 128 + rows],
                            xnb[:, kc * 128:(kc + 1) * 128], start=(kc == 0), stop=(kc == 7))
                self.act(wk[ocr][0:rows, :], bank[0:rows, 0:128], AF.Silu, bias=s["c_kb1"][:, ocr:ocr + 1][0:rows, :])
                self.trans(ps[7][:, 0:rows], wk[ocr][0:rows, :], s["c_identf"][0:rows, 0:rows])
                self.v("tensor_scalar_add", w_tm[:, ocr * 128:ocr * 128 + rows], ps[7][:, 0:rows], 0.0)
            self.v("tensor_tensor", wh[:, :].rearrange("p (h s) -> p h s", h=8),
                   w_tm[:, :].rearrange("p (h s) -> p h s", h=8),
                   frac[:, :].unsqueeze(1).broadcast_to([128, 8, 33]), ALU.mult)
            self.v("tensor_sub", wl[:, :], w_tm[:, :], wh[:, :])
            self.act(wl_all[:, bi * 264:(bi + 1) * 264], wl[:, :], AF.Copy)
            self.act(wh_all[:, bi * 264:(bi + 1) * 264], wh[:, :], AF.Copy)

        for nm in ("5g", "5d", "5p", "5e", "5q"):
            self.rsem(nm)
        p0, d0, e0, gg0, q0 = (self.rc[nm] for nm in ("5p", "5d", "5e", "5g", "5q"))
        oc0 = self.ocnt
        NGI = NB * 33
        PARA5 = getattr(KB, "PARA5", False)
        _rop0 = self.rop
        def rop_p5(eng, fn, incs=(), waits=()):
            if PARA5:
                waits = list(waits) + [(nm, self.rc[nm]) for nm in ("5g", "5d", "5p", "5e")]
            _rop0(eng, fn, incs=incs, waits=waits)
        self.rop = rop_p5

        def gath(gidx, w_):
            bi2, si2 = divmod(gidx, 33)
            buf = gidx % NGT
            self.rop("gpsimd",
                     lambda buf=buf, bi2=bi2, si2=si2: self.nc.gpsimd.indirect_dma_start(
                         gt[:, buf * 2048:(buf + 1) * 2048], None,
                         bass.AP(self.dram["xn1_tm"], 0, [[1, 128], [1, 2048]]),
                         IndirectOffsetOnAxis(idx_all[:, bi2 * 33 + si2:bi2 * 33 + si2 + 1], 1)),
                     incs=[("5g", 16)], waits=w_)

        for gidx in range(NGT):
            gath(gidx, [("O", oc0)])
        for bi in range(NB):
            for si in range(33):
                gidx = bi * 33 + si
                if gidx + NGT < NGI:
                    gath(gidx + NGT, [("5p", p0 + gidx + 1)])
                dwait = ([("5p", p0 + gidx - 3)] if gidx >= 4 else [("O", oc0)])
                for lohi in range(2):
                    wall = wl_all if lohi == 0 else wh_all
                    dv = dg[:, (gidx % 4) * 2048 + lohi * 1024:
                            (gidx % 4) * 2048 + (lohi + 1) * 1024]
                    self.rop("vector",
                             lambda dv=dv, wall=wall, bi=bi, si=si:
                             self.nc.vector.tensor_tensor(
                                 dv.rearrange("p (h d) -> p h d", h=8),
                                 s["c_ident"][:, :].unsqueeze(1).broadcast_to([128, 8, 128]),
                                 wall[:, bi * 264:(bi + 1) * 264]
                                 .rearrange("p (h s) -> p h s", h=8)[:, :, si:si + 1]
                                 .broadcast_to([128, 8, 128]), ALU.mult),
                             incs=[("5d", 1)], waits=(dwait if lohi == 0 else []))
                buf = gidx % NGT
                for lohi in range(2):
                    for h8 in range(8):
                        w_ = []
                        if lohi == 0 and h8 == 0:
                            w_ = [("5g", gg0 + 16 * min(gidx + 2, NGI)),
                                  ("5d", d0 + min(2 * gidx + 4, 2 * NGI))]
                            if si == 0:
                                w_.append(("5e", e0 + 8 * bi))
                                if bi == 0:
                                    w_.append(("O", oc0))
                        incs_ = []
                        if lohi == 1 and h8 == 7:
                            incs_.append(("5p", 1))
                        elif lohi == 1 and si == 32:
                            incs_.append(("5q", 1))
                        self.rop("tensor",
                                 lambda buf=buf, lohi=lohi, h8=h8, gidx=gidx:
                                 self.nc.tensor.matmul(
                                     ps[h8][:, 0:128],
                                     gt[:, buf * 2048 + lohi * D + h8 * 128:
                                        buf * 2048 + lohi * D + (h8 + 1) * 128],
                                     dg[:, (gidx % 4) * 2048 + lohi * 1024 + h8 * 128:
                                        (gidx % 4) * 2048 + lohi * 1024 + (h8 + 1) * 128],
                                     start=(gidx % 33 == 0 and lohi == 0),
                                     stop=(lohi == 1)),
                                 incs=incs_, waits=w_)
            for h8 in range(8):
                self.rop("scalar",
                         lambda h8=h8, bi=bi: self.nc.scalar.activation(
                             out=ohfm_all[:, (bi * 8 + h8) * 128:(bi * 8 + h8 + 1) * 128],
                             in_=ps[h8][:, 0:128], func=AF.Copy),
                         incs=[("5e", 1)],
                         waits=([("5q", q0 + 7 * bi + h8 + 1)] if h8 < 7
                                else [("5p", p0 + 33 * (bi + 1))]))
        self.rop = _rop0
        self.rjoin([("5g", self.rc["5g"]), ("5d", self.rc["5d"]),
                    ("5p", self.rc["5p"]), ("5e", self.rc["5e"]),
                    ("5q", self.rc["5q"])])
        if "dbg4" in self.dbg:
            dbg = self.d("dbg4")
            self.dma("gpsimd", dbg[:, 0:1024], ohfm_all[:, 0:1024])
            self.v("tensor_copy", wl[:, :], wl_all[:, 0:264])
            self.dma("gpsimd", dbg[:, 1024:1288], wl[:, :])
            self.v("tensor_copy", wh[:, 0:33], idx_all[:, 0:33])
            self.dma("gpsimd", dbg[:, 1288:1321], wh[:, 0:33])
            self.dma("gpsimd", dbg[:, 1321:2345], dg[:, 0:1024])
            self.dma("gpsimd", dbg[:, 2345:3369], gt[:, 0:1024])
            self.v("tensor_copy", wh[:, :], wh_all[:, 0:264])
            self.dma("gpsimd", dbg[:, 3369:3633], wh[:, :])

        items = []
        for bi in range(NB):
            for oc in range(8):
                def mk(bi=bi, oc=oc):
                    def mms(bank):
                        return [(lambda kc=kc, bank=bank: self.nc.tensor.matmul(
                            bank[:, 0:128],
                            w_out[:, kc * D + oc * 128: kc * D + oc * 128 + 128],
                            ohfm_all[:, (bi * 8 + kc) * 128:(bi * 8 + kc + 1) * 128],
                            start=(kc == 0), stop=(kc == 7))) for kc in range(8)]
                    it = len(items)
                    sl = s1t[:, (it % 4) * 128:(it % 4 + 1) * 128]
                    return {
                        "mms": mms,
                        "evac": lambda bank, sl=sl: (lambda: self.nc.scalar.activation(
                            out=sl, in_=bank[:, 0:128], func=AF.Silu)),
                        "store": lambda sl=sl, bi=bi, oc=oc: self.nc.gpsimd.dma_start(
                            out=self.d("s1_fm")[oc * 128:(oc + 1) * 128, bi * 128:(bi + 1) * 128],
                            in_=sl),
                    }
                items.append(mk())
        self.mmpipe(items, nbank=2, bank0=0, width=128)

    def s6_moe(self):
        s = self.sb
        ps = self.ps
        o = 0
        def nxt(name, shape, dtype):
            nonlocal o
            h = self.ar(name, shape, dtype, o)
            o += (int(np.prod(shape[1:])) * DTB[dtype] + 511) // 512 * 512
            return h
        w_wg = nxt("s6_wg", [128, 8 * 8], BF16)
        xnb = nxt("s6_xnb", [128, 8 * 256], BF16)
        lgf = nxt("s6_lgf", [8, 256], F32)
        lg_tm = nxt("s6_lgtm", [128, 2 * 8], F32)
        mx = nxt("s6_mx", [128, 2], F32)
        mi = nxt("s6_mi", [128, 2 * 8], I32)
        m_tm = nxt("s6_mtm", [128, 2 * 8], F32)
        me = nxt("s6_me", [1, 8 * 256], F32)
        xe = nxt("s6_xe", [128, 8 * 256], BF16)
        for kc in range(8):
            self.dma("gpsimd", w_wg[:, kc * 8:(kc + 1) * 8], self.d("l1f_wg")[kc * 128:(kc + 1) * 128, :])
        NT = 256
        for tt in range(NOUT // NT):
            w0 = OUT0 + tt * NT
            for kc in range(8):
                self.dma("sync", xnb[:, kc * NT:(kc + 1) * NT],
                         self.d("xn1_fm")[kc * 128:(kc + 1) * 128, w0:w0 + NT])
            for kc in range(8):
                self.mm(ps[0][0:8, 0:NT], w_wg[:, kc * 8:(kc + 1) * 8],
                        xnb[:, kc * NT:(kc + 1) * NT], start=(kc == 0), stop=(kc == 7))
            self.v("tensor_scalar", lgf[:, :], ps[0][0:8, 0:NT], s["c_bg"][:, 0:1], None, ALU.add)
            for tb in range(2):
                self.trans(ps[1][:, 0:8], lgf[:, tb * 128:(tb + 1) * 128], s["c_identf"][0:8, 0:8])
                self.v("tensor_scalar_add", lg_tm[:, tb * 8:(tb + 1) * 8], ps[1][:, 0:8], 0.0)
                self.v("reduce_max", mx[:, tb:tb + 1], lg_tm[:, tb * 8:(tb + 1) * 8], AX.X)
                self.v("tensor_tensor", mi[:, tb * 8:(tb + 1) * 8], lg_tm[:, tb * 8:(tb + 1) * 8],
                       mx[:, tb:tb + 1].broadcast_to([128, 8]), ALU.is_ge)
                self.v("tensor_copy", m_tm[:, tb * 8:(tb + 1) * 8], mi[:, tb * 8:(tb + 1) * 8])
                for e in range(8):
                    self.trans(ps[2][0:1, 0:128], m_tm[:, tb * 8 + e: tb * 8 + e + 1], s["c_identf"][:, :])
                    self.v("tensor_scalar_add", me[0:1, e * NT + tb * 128: e * NT + (tb + 1) * 128],
                           ps[2][0:1, 0:128], 0.0)
            for e in range(8):
                self.mm(ps[3][:, 0:NT], s["c_onesf"][0:1, 0:128], me[0:1, e * NT:(e + 1) * NT])
                for kc in range(8):
                    self.v("tensor_tensor", xe[:, kc * NT:(kc + 1) * NT],
                           xnb[:, kc * NT:(kc + 1) * NT], ps[3][:, 0:NT], ALU.mult)
                for kc in range(8):
                    self.dma("gpsimd", self.d("xt_dram")[(e * 8 + kc) * 128:(e * 8 + kc + 1) * 128,
                                                         tt * NT:(tt + 1) * NT],
                             xe[:, kc * NT:(kc + 1) * NT])
        o = 0
        w1b = nxt("s6_w1b", [128, 2 * 8 * 128], BF16)
        xeb = nxt("s6_xeb", [128, 2 * 8 * NOUT], BF16)
        htb = nxt("s6_htb", [128, 2 * 512], BF16)
        for nm in ("bWL", "bXE", "bMM", "bGE", "bST"):
            self.rsem(nm)
        oc_e = self.ocnt
        mm_hist = {}
        ge_hist = {}
        wl_cnt = 0
        xe_cnt = 0
        gidx = 0

        def wload(e, hc):
            nonlocal wl_cnt
            slot = (e * 32 + hc) % 2
            w_ = [("O", oc_e)] if (e * 32 + hc) < 2 else [("bMM", (e * 32 + hc - 1) * 4)]
            self.rop("gpsimd", lambda slot=slot, e=e, hc=hc: self.nc.gpsimd.dma_start(
                out=w1b[:, slot * 1024:(slot + 1) * 1024].rearrange("p (kc h) -> p kc h", kc=8),
                in_=self.d("l1f_w1")[e * D:(e + 1) * D, hc * 128:(hc + 1) * 128]
                .rearrange("(kc p) h -> p kc h", p=128)),
                incs=[("bWL", 16)], waits=w_)
            wl_cnt += 16

        def xload(e):
            nonlocal xe_cnt
            slot = e % 2
            w_ = [("O", oc_e)] if e < 2 else [("bMM", (e - 1) * 32 * 4)]
            for kc in range(8):
                self.rop("sync", lambda slot=slot, e=e, kc=kc: self.nc.sync.dma_start(
                    out=xeb[:, slot * 8 * NOUT + kc * NOUT: slot * 8 * NOUT + (kc + 1) * NOUT],
                    in_=self.d("xt_dram")[(e * 8 + kc) * 128:(e * 8 + kc + 1) * 128, :]),
                    incs=[("bXE", 16)], waits=w_ if kc == 0 else [])
                xe_cnt += 16

        xload(0); wload(0, 0)
        xload(1); wload(0, 1)
        for e in range(8):
            if e < 7:
                pass
            for hc in range(32):
                gi = e * 32 + hc
                slot = gi % 2
                es = e % 2
                for t4 in range(4):
                    bank = ps[t4 % 2]
                    bkey = t4 % 2
                    w_ = []
                    if t4 == 0:
                        w_ = [("bWL", (gi + 1) * 16), ("bXE", (e + 1) * 8 * 16 if e < 8 else 0)]
                        if e >= 2:
                            w_[1] = ("bXE", min((e + 1), 8) * 8 * 16)
                    if bkey in ge_hist:
                        w_.append(("bGE", ge_hist[bkey]))
                    for kc in range(8):
                        self.rop("tensor", lambda bank=bank, slot=slot, es=es, kc=kc, t4=t4, e=e:
                                 self.nc.tensor.matmul(
                                     bank[:, 0:512], w1b[:, slot * 1024 + kc * 128: slot * 1024 + (kc + 1) * 128],
                                     xeb[:, es * 8 * NOUT + kc * NOUT + t4 * 512: es * 8 * NOUT + kc * NOUT + t4 * 512 + 512],
                                     start=(kc == 0), stop=(kc == 7)),
                                 incs=[("bMM", 1)] if kc == 7 else [],
                                 waits=w_ if kc == 0 else [])
                    hslot = (gi * 4 + t4) % 2
                    hw = [("bMM", gi * 4 + t4 + 1)]
                    if gi * 4 + t4 >= 2:
                        hw.append(("bST", (gi * 4 + t4 - 1) * 16))
                    self.rop("scalar", lambda hslot=hslot, bank=bank: self.nc.scalar.activation(
                        out=htb[:, hslot * 512:(hslot + 1) * 512], in_=bank[:, 0:512],
                        func=AF.Gelu_apprx_tanh), incs=[("bGE", 1)], waits=hw)
                    ge_hist[bkey] = self.rc["bGE"]
                    self.rop("gpsimd", lambda e=e, hc=hc, t4=t4, hslot=hslot: self.nc.gpsimd.dma_start(
                        out=self.d("ht_dram")[(e * 32 + hc) * 128:(e * 32 + hc + 1) * 128,
                                              t4 * 512:(t4 + 1) * 512],
                        in_=htb[:, hslot * 512:(hslot + 1) * 512]),
                        incs=[("bST", 16)], waits=[("bGE", self.rc["bGE"])])
                if gi + 2 < 256:
                    wload((gi + 2) // 32, (gi + 2) % 32)
            if e + 2 < 8:
                xload(e + 2)
        self.rjoin([("bST", self.rc["bST"]), ("bMM", self.rc["bMM"])])
        o = 0
        g1acc = nxt("s6_acc", [128, 8 * NOUT], F32)
        w2b = nxt("s6_w2b", [128, 2 * 16 * D], BF16)
        O_X2 = o
        htt = nxt("s6_htt", [128, 2 * 16 * 512], BF16)
        for nm in ("cWL", "cHT", "cMC", "cAD"):
            self.rsem(nm)
        oc_e = self.ocnt

        def w2load(e, half):
            idx = e * 2 + half
            w_ = [("O", oc_e)] if idx < 2 else [("cMC", 32 * e)]
            for i in range(16):
                hc = half * 16 + i
                self.rop("gpsimd", lambda e=e, half=half, hc=hc, i=i: self.nc.gpsimd.dma_start(
                    out=w2b[:, (half * 16 + i) * D:(half * 16 + i + 1) * D],
                    in_=self.d("l1f_w2")[(e * 32 + hc) * 128:(e * 32 + hc + 1) * 128, :]),
                    incs=[("cWL", 16)], waits=w_ if i == 0 else [])

        def htload(e, t4, half):
            idx = (e * 4 + t4) * 2 + half
            w_ = [("O", oc_e)] if idx < 2 else [("cMC", 8 * ((idx - 2) // 2 + 1))]
            self.rop("sync", lambda e=e, t4=t4, half=half: self.nc.sync.dma_start(
                out=htt[:, half * 16 * 512:(half + 1) * 16 * 512].rearrange("p (hc t) -> p hc t", hc=16),
                in_=self.d("ht_dram").rearrange("(g p) t -> g p t", p=128)
                [e * 32 + half * 16:e * 32 + (half + 1) * 16, :, t4 * 512:(t4 + 1) * 512]
                .transpose([1, 0, 2])),
                incs=[("cHT", 16)], waits=w_)

        w2load(0, 0); htload(0, 0, 0)
        w2load(0, 1); htload(0, 0, 1)
        ad_hist = {}
        for e in range(8):
            for t4 in range(4):
                for half in range(2):
                    hidx = (e * 4 + t4) * 2 + half
                    for oc in range(8):
                        bank = ps[oc]
                        w_ = []
                        if oc == 0:
                            w_ = [("cWL", (e * 2 + half + 1) * 256), ("cHT", (hidx + 1) * 16)]
                        if half == 0 and oc in ad_hist:
                            w_.append(("cAD", ad_hist[oc]))
                        for i in range(16):
                            self.rop("tensor", lambda bank=bank, half=half, i=i, oc=oc:
                                     self.nc.tensor.matmul(
                                         bank[:, 0:512],
                                         w2b[:, (half * 16 + i) * D + oc * 128: (half * 16 + i) * D + oc * 128 + 128],
                                         htt[:, half * 16 * 512 + i * 512: half * 16 * 512 + (i + 1) * 512],
                                         start=(half == 0 and i == 0 and True), stop=(half == 1 and i == 15)),
                                     incs=[("cMC", 1)] if (half == 1 and i == 15) else [],
                                     waits=w_ if i == 0 else [])
                    if hidx + 2 < 64:
                        ne, nt4, nh = (hidx + 2) // 8, ((hidx + 2) % 8) // 2, (hidx + 2) % 2
                        htload(ne, nt4, nh)
                    widx = e * 2 + half
                    if t4 == 3 and widx + 2 < 16:
                        w2load((widx + 2) // 2, (widx + 2) % 2)
                for oc in range(8):
                    bank = ps[oc]
                    dst = g1acc[:, oc * NOUT + t4 * 512: oc * NOUT + (t4 + 1) * 512]
                    w_ = [("cMC", (e * 4 + t4) * 8 + oc + 1)]
                    if e == 0:
                        self.rop("vector", lambda dst=dst, bank=bank:
                                 self.nc.vector.tensor_scalar_add(dst, bank[:, 0:512], 0.0),
                                 incs=[("cAD", 1)], waits=w_)
                    else:
                        self.rop("vector", lambda dst=dst, bank=bank:
                                 self.nc.vector.tensor_tensor(dst, dst, bank[:, 0:512], ALU.add),
                                 incs=[("cAD", 1)], waits=w_)
                    ad_hist[oc] = self.rc["cAD"]
        self.rjoin([("cAD", self.rc["cAD"])])
        if "dbg4" in self.dbg:
            for t4 in range(4):
                self.dma("gpsimd", self.d("dbg4")[:, t4 * 512:(t4 + 1) * 512],
                         g1acc[:, 0 * NOUT + t4 * 512: 0 * NOUT + (t4 + 1) * 512])
        o = O_X2
        s1l = nxt("s6_s1l", [128, 512], F32)
        x1l = nxt("s6_x1l", [128, 512], F32)
        g1s = nxt("s6_g1s", [128, 512], F32)
        x2t = nxt("s6_x2t", [128, 512], F32)
        for t4 in range(4):
            for c in range(8):
                self.dma("sync", s1l[:, :], self.d("s1_fm")[c * 128:(c + 1) * 128, t4 * 512:(t4 + 1) * 512])
                self.dma("sync", x1l[:, :], self.d("x1_fm")[c * 128:(c + 1) * 128,
                                                            OUT0 + t4 * 512:OUT0 + (t4 + 1) * 512])
                self.act(g1s[:, :], g1acc[:, c * NOUT + t4 * 512: c * NOUT + (t4 + 1) * 512], AF.Silu)
                self.v("tensor_mul", g1s[:, :], g1s[:, :], s1l[:, :])
                self.v("tensor_add", x2t[:, :], x1l[:, :], g1s[:, :])
                self.dma("gpsimd", self.d("x2_fm")[c * 128:(c + 1) * 128, t4 * 512:(t4 + 1) * 512], x2t[:, :])


def _consts():
    ident = np.eye(128, dtype=np.float32)
    r = np.arange(128)
    maskAl = np.where(r[None, :] < r[:, None], -1.0, 0.0).astype(np.float32)
    maskAu = np.where(r[None, :] > r[:, None], -1.0, 0.0).astype(np.float32)
    maskIncT = np.where(r[:, None] <= r[None, :], 1.0, 0.0).astype(np.float32)
    ind49 = np.zeros((128, 49), np.float32)
    ind49[:, 24] = 1.0
    sel4 = np.zeros((4, 512), np.float32)
    for h in range(4):
        sel4[h, h * 128:(h + 1) * 128] = 1.0
    hsel = np.zeros((24, 4), np.float32)
    for h in range(4):
        hsel[h * 6:(h + 1) * 6, h] = 1.0
    rsel24 = np.zeros((24, 24 * 128), np.float32)
    for rr in range(24):
        rsel24[rr, rr * 128:(rr + 1) * 128] = 1.0
    e16 = np.zeros((8, 128), np.float32)
    for e in range(8):
        e16[e, e * 16:(e + 1) * 16] = 1.0
    return ident, maskAl, maskAu, maskIncT, ind49, sel4, hsel, rsel24, e16


def make_inputs(full, core):
    b, j = core // 2, core % 2
    x = np.asarray(full["L0_attn_hidden_states"][b], np.float32)
    ident, maskAl, maskAu, maskIncT, ind49, sel4, hsel, rsel24, e16 = _consts()
    xf = np.zeros((D, LPAD), np.float32)
    xf[:, ROW0:ROW0 + L] = x.T
    g = lambda n: np.asarray(full[n], np.float32)
    w_qkvb = np.concatenate([g("L0_attn_q_w"), g("L0_attn_k_w"), g("L0_attn_v_w"),
                             g("L0_attn_b_w")], axis=1)
    convw = np.concatenate([g("L0_attn_qc_w")[:, 0, :], g("L0_attn_kc_w")[:, 0, :],
                            g("L0_attn_vc_w")[:, 0, :]], axis=1)
    firw = np.zeros((D, 42), np.float32)
    firw[:, 0:1] = g("L0_attn_fir_w1")[:, 0, :]
    firw[:, 1:4] = g("L0_attn_fir_w3")[:, 0, :]
    firw[:, 4:11] = g("L0_attn_fir_w7")[:, 0, :]
    firw[:, 11:42] = g("L0_attn_fir_w31")[:, 0, :]
    w1 = g("L0_attn_mlp_w1")
    perm = np.arange(1120)
    for kind in range(4):
        for br in range(6):
            for h in range(4):
                perm[1024 + kind * 24 + br * 4 + h] = 1024 + h * 24 + br * 4 + kind
    w1p = np.ascontiguousarray(w1[perm])
    kb264 = g("L1_attn_kernel_b").reshape(8, 64)[:, :33].reshape(264)
    kbp = np.zeros((128, 3), np.float32)
    kbp.T.flat[0:264] = kb264
    W0g = j * 2048 - PAD
    offs = np.zeros((1, 16), np.int32)
    p = np.arange(128)
    idxA = ((p[:, None] + np.arange(8)[None, :] * 128) * LPAD + j * 2048).astype(np.int32)
    idxB = ((j * 2048 + p) * D).astype(np.int32).reshape(128, 1)
    d = {
        "x_tm": x, "x_fm": xf, "w_qkvb": w_qkvb, "convw": convw, "firw": firw,
        "mlp_w1p": w1p, "mlp_b1c": g("L0_attn_mlp_b1").reshape(8, 128).T.copy(),
        "mlp_w2": g("L0_attn_mlp_w2"), "mlp_b2": g("L0_attn_mlp_b2").reshape(24, 1),
        "glt": g("L0_attn_gate_log_temp").reshape(4, 1),
        "onormc": g("L0_attn_onorm_w").reshape(2, 128).T.copy(),
        "o_wT": g("L0_attn_o_w"),
        "l0f_wT": g("L0_ffn_weight").T.copy(),
        "l0f_gw": g("L0_ffn_gate_w").T.copy(),
        "l0f_U": g("L0_ffn_lora_U").reshape(128, D).T.copy(),
        "l0f_V": g("L0_ffn_lora_V").transpose(0, 2, 1).reshape(128, D).copy(),
        "l1a_wavew": g("L1_attn_wave_w").T.copy(),
        "l1a_wavebf": g("L1_attn_wave_b")[0:8].reshape(8, 1),
        "l1a_wavebp": g("L1_attn_wave_b")[8:16].reshape(8, 1),
        "l1a_kw": g("L1_attn_kernel_w").reshape(8, 64, D)[:, :33].reshape(264, D).T.copy(),
        "l1a_kb": kbp,
        "l1a_outwT": g("L1_attn_out_w").T.copy(),
        "l1f_w1": g("L1_ffn_W1").reshape(8 * D, 4096),
        "l1f_w2": g("L1_ffn_W2").reshape(8 * 4096, D),
        "l1f_wg": g("L1_ffn_Wg"), "l1f_bg": g("L1_ffn_bg").reshape(8, 1),
        "ident": ident, "maskAl": maskAl, "maskAu": maskAu, "maskIncT": maskIncT,
        "ind49": ind49, "sel4": sel4, "sel4f": sel4, "hsel": hsel,
        "hselT": np.ascontiguousarray(hsel.T), "rsel24": rsel24, "e16": e16,
        "iotaG": (W0g + np.arange(WIN, dtype=np.float32))[None, :],
        "sidx8": ((np.arange(33, dtype=np.float32) - 16) / 8.0)[None, :],
        "w0col": np.full((33, 1), W0g, np.float32), "offs": offs,
        "idxA": idxA, "idxB": idxB,
    }
    return {k: np.ascontiguousarray(v) for k, v in d.items()}


def build_kb(stage_limit=99, dbg=()):
    kb = KB(stage_limit=stage_limit, dbg=dbg)
    nc = kb.build()
    return kb, nc


def kernel(**inputs):
    kb, nc = build_kb()
    in_maps = [make_inputs(inputs, c) for c in range(8)]
    res = run_bass_kernel_spmd(nc, in_maps, core_ids=list(range(8)))
    out = np.zeros((B, L, D), np.float32)
    for c in range(8):
        b, j = c // 2, c % 2
        out[b, j * 2048:(j + 1) * 2048, :] = res.results[c]["x2_fm"].T
    return out
